# version 06
# speedup vs baseline: 1.1120x; 1.0849x over previous
import contextlib
import numpy as np
import ml_dtypes
import concourse.bass as bass
import concourse.mybir as mybir
from concourse.bass_utils import run_bass_kernel_spmd

F32 = mybir.dt.float32
BF16 = mybir.dt.bfloat16
AF = mybir.ActivationFunctionType
ALU = mybir.AluOpType
AX = mybir.AxisListType

BATCH = 8
T = 2048
D = 1024
DC = D // 128
NT = T // 512
BF = ml_dtypes.bfloat16


def host_constants():
    c = {}
    c["c_ident"] = np.eye(128, dtype=np.float32)
    c["c_ones_col"] = np.ones((128, 1), dtype=np.float32)
    c["c_ones_row"] = np.ones((1, 128), dtype=np.float32)
    c["c_ones_row_f"] = np.ones((1, 128), dtype=np.float32)
    c["c_ones_col_f"] = np.ones((128, 1), dtype=np.float32)
    c["c_nb10"] = np.full((128, 1), -10.0, dtype=np.float32)
    c["c_nb20"] = np.full((128, 1), -20.0, dtype=np.float32)
    tri = np.triu(np.ones((64, 64), dtype=np.float32))
    c["c_triu2"] = np.concatenate([tri, tri], axis=0)
    c["c_trium"] = tri.copy()
    c["c_lnscale"] = np.full((128, 1), np.log(128 ** -0.5), dtype=np.float32)
    c["c_p1"] = np.full((128, 1), 1.0, dtype=np.float32)
    c["c_eps6"] = np.full((128, 1), 1e-06, dtype=np.float32)
    c["c_identf"] = np.eye(128, dtype=np.float32)
    DH = 64
    half = np.arange(0, DH, 2, dtype=np.float64)
    inv_freq = 1.0 / 10000.0 ** (half / DH)
    tpos = np.arange(2048, dtype=np.float64)
    fr = tpos[:, None] * inv_freq[None, :]
    fr = np.concatenate([fr, fr], -1)
    base_scale = (half + 0.4 * DH) / (1.4 * DH)
    power = (tpos - 1024) / 128.0
    xsc = np.concatenate([base_scale[None, :] ** power[:, None]] * 2, -1)
    cosf = np.cos(fr)
    sinf = np.sin(fr)
    sig = np.concatenate([np.arange(32, 64), np.arange(0, 32)])
    def t2tab(a):
        return np.ascontiguousarray(np.tile(a.T.astype(np.float32), (2, 1)))
    c["c_l1f_csq"] = t2tab(cosf * xsc)
    c["c_l1f_ssq"] = t2tab(sinf[:, sig] * xsc[:, sig])
    c["c_l1f_csk"] = t2tab(cosf / xsc)
    c["c_l1f_ssk"] = t2tab(sinf[:, sig] / xsc[:, sig])
    rot = np.zeros((64, 64), dtype=np.float32)
    for mcol in range(32):
        rot[mcol + 32, mcol] = -1.0
    for mcol in range(32, 64):
        rot[mcol - 32, mcol] = 1.0
    c["c_rot2"] = np.ascontiguousarray(np.block(
        [[rot, np.zeros((64, 64), np.float32)],
         [np.zeros((64, 64), np.float32), rot]]))
    bd = np.zeros((128, 128), dtype=np.float32)
    bd[0:64, 0:64] = 1.0
    bd[64:128, 64:128] = 1.0
    c["c_bd2"] = bd
    wm = np.zeros((128, 384), dtype=np.float32)
    for kp in range(128):
        diff = np.arange(384) - kp
        wm[kp, :] = (diff >= 0) & (diff <= 256)
    c["c_l1f_wmask"] = wm
    ext = np.zeros((128, 896), dtype=np.float32)
    for kp in range(128):
        ext[kp, :] = (np.arange(896) >= kp + 384)
    c["c_l2a_cmask"] = ext
    return c


CONST_NEED = {
    "L1_attn": {"c_triu2", "c_trium", "c_lnscale", "c_p1", "c_identf", "c_eps6"},
    "L1_ffn": {"c_p1", "c_eps6"},
}
CONST_BASE = {"c_ones_col", "c_ones_row", "c_ident", "c_ones_row_f", "c_nb10",
              "c_nb20", "c_ones_col_f"}
CONST_SPECS = [("c_ones_col", [128, 1], BF16), ("c_ones_row", [1, 128], BF16),
               ("c_ident", [128, 128], BF16), ("c_ones_row_f", [1, 128], F32),
               ("c_nb10", [128, 1], F32), ("c_nb20", [128, 1], F32),
               ("c_ones_col_f", [128, 1], F32),

               ("c_triu2", [128, 64], BF16), ("c_trium", [64, 64], BF16),
               ("c_lnscale", [128, 1], F32), ("c_p1", [128, 1], F32),
               ("c_eps6", [128, 1], F32),

               ("c_identf", [128, 128], F32)]


class Sem:
    def __init__(self, st, nc, name):
        self.h = st.enter_context(nc.semaphore(name))
        self.n = 0

    def bump(self, instr, k=1):
        instr.then_inc(self.h, k)
        self.n += k
        return self.n

    def reserve(self, k=1):
        self.n += k
        return self.n


class Prog:
    def __init__(self, nc, st):
        self.nc = nc
        self.st = st
        self.dw = Sem(st, nc, "dw")
        self.dx = Sem(st, nc, "dx")
        self.ts = Sem(st, nc, "ts")
        self.vs = Sem(st, nc, "vs")
        self.ss = Sem(st, nc, "ss")
        self.gs = Sem(st, nc, "gs")
        self.consts = {}
        self.input_names = []
        self.taps = []

    def sb(self, name, shape, dtype=F32, stack=None):
        return (stack or self.st).enter_context(self.nc.sbuf_tensor(name, shape, dtype))

    def ps(self, name, shape, dtype=F32, stack=None):
        return (stack or self.st).enter_context(self.nc.psum_tensor(name, shape, dtype))

    def din(self, name, shape):
        self.input_names.append(name)
        return self.nc.dram_tensor(name, list(shape), F32, kind="ExternalInput").ap()

    def tap(self, name, tile_ap):
        self.taps.append((name, tile_ap))



_LN_CTR = [0]


def emit_ln_fm(P, block, x_fm, xn_bf, eps=1e-5, gamma_col=None, beta_col=None,
               n_feat=D, tokens=T, x_gate=None, scr=None, stack=None):
    _LN_CTR[0] += 1
    width = min(512, tokens)
    own_stack = contextlib.ExitStack()
    if scr is None:
        with own_stack:
            scr = ln_scratch(P, own_stack, width=width, tag=f"_{_LN_CTR[0]}")
            r = None
            for off in range(0, tokens, width):
                r = _emit_ln_fm_inner(P, block, x_fm, xn_bf, scr, eps, gamma_col,
                                      beta_col, n_feat, tokens, x_gate, off, width)
            return r
    r = None
    for off in range(0, tokens, width):
        r = _emit_ln_fm_inner(P, block, x_fm, xn_bf, scr, eps, gamma_col, beta_col,
                              n_feat, tokens, x_gate, off, width)
    return r


def _emit_ln_fm_inner(P, block, x_fm, xn_bf, scr, eps, gamma_col, beta_col,
                      n_feat, tokens, x_gate, off=0, width=None):
    if width is None:
        width = tokens
    FC = n_feat // 128
    NTl = width // 512
    ones_col = P.consts["c_ones_col"]
    ones_row = P.consts["c_ones_row"]
    pp = scr["pp"]
    row = scr["row"]
    rowb = scr["rowb"]
    ab = scr["ab"]
    psA = scr["psA"]
    psB = scr["psB"]
    bcA = scr["bcA"]
    bcB = scr["bcB"]

    if x_fm is not None and off == 0:
        @block.vector
        def _(v):
            if x_gate is not None:
                v.wait_ge(x_gate[0], x_gate[1])
            for kc in range(FC):
                ins = v.tensor_copy(xn_bf[:, kc * tokens:(kc + 1) * tokens],
                                    x_fm[:, kc * tokens:(kc + 1) * tokens])
            P.vs.bump(ins, 1)
        gate0 = (P.vs.h, P.vs.n)
        P.lngate = gate0
    elif x_fm is not None:
        gate0 = P.lngate
    else:
        gate0 = x_gate
    v_cast = gate0[1]

    ts0 = P.ts.n
    ss0 = P.ss.n

    @block.scalar
    def _(s):
        s.wait_ge(gate0[0], gate0[1])
        for nt in range(NTl):
            for kc in range(FC):
                gi = nt * FC + kc
                if gi >= 2:
                    s.wait_ge(P.ts.h, ts0 + gi - 1)
                ins = s.activation(pp[:, (gi % 2) * 512:(gi % 2) * 512 + 512],
                                   xn_bf[:, kc * tokens + off + nt * 512: kc * tokens + off + nt * 512 + 512],
                                   AF.Square)
                P.ss.bump(ins, 1)
            s.wait_ge(P.ts.h, ts0 + (nt + 1) * FC)
            s.copy(row[0:1, nt * 512:(nt + 1) * 512], psA[:])
            ins = s.copy(row[0:1, width + nt * 512: width + nt * 512 + 512], psB[:])
            P.ss.bump(ins, 1)

    @block.tensor
    def _(t):
        t.wait_ge(gate0[0], gate0[1])
        for nt in range(NTl):
            for kc in range(FC):
                gi = nt * FC + kc
                t.wait_ge(P.ss.h, ss0 + nt * (FC + 1) + kc + 1)
                t.matmul(psA[:], ones_col[:],
                         xn_bf[:, kc * tokens + off + nt * 512: kc * tokens + off + nt * 512 + 512],
                         start=(kc == 0), stop=(kc == FC - 1))
                ins = t.matmul(psB[:], ones_col[:],
                               pp[:, (gi % 2) * 512:(gi % 2) * 512 + 512],
                               start=(kc == 0), stop=(kc == FC - 1))
                P.ts.bump(ins, 1)
    assert P.ts.n == ts0 + NTl * FC

    @block.scalar
    def _(s):
        s.mul(row[0:1, 0:width], row[0:1, 0:width], 1.0 / n_feat)
        ins = s.mul(row[0:1, width:2 * width], row[0:1, width:2 * width], 1.0 / n_feat)
        P.ss.bump(ins, 1)
    s_rows = P.ss.n

    @block.vector
    def _(v):
        v.wait_ge(P.ss.h, s_rows)
        v.tensor_tensor(rowb[0:1, 0:width], row[0:1, 0:width],
                        row[0:1, 0:width], ALU.mult)
        ins = v.scalar_tensor_tensor(row[0:1, width:2 * width], row[0:1, width:2 * width],
                                     eps, rowb[0:1, 0:width],
                                     ALU.add, ALU.subtract)
        P.vs.bump(ins, 1)
    v_var = P.vs.n

    @block.scalar
    def _(s):
        s.wait_ge(P.vs.h, v_var)
        s.activation(row[0:1, width:2 * width], row[0:1, width:2 * width], AF.Ln)
        ins = s.activation(rowb[0:1, 0:width], row[0:1, width:2 * width], AF.Exp,
                           scale=-0.5)
        P.ss.bump(ins, 1)
    s_rstd = P.ss.n

    @block.vector
    def _(v):
        v.wait_ge(P.ss.h, s_rstd)
        ins = v.scalar_tensor_tensor(rowb[0:1, width:2 * width], row[0:1, 0:width], -1.0,
                                     rowb[0:1, 0:width], ALU.mult, ALU.mult)
        P.vs.bump(ins, 1)
    v_brow = P.vs.n

    tb0 = P.ts.n
    sb0 = P.ss.n

    @block.tensor
    def _(t):
        t.wait_ge(P.vs.h, v_brow)
        for nt in range(NTl):
            if nt >= 1:
                t.wait_ge(P.ss.h, sb0 + nt)
            t.matmul(bcA[:], ones_row[:], rowb[0:1, nt * 512:(nt + 1) * 512],
                     start=True, stop=True)
            ins = t.matmul(bcB[:], ones_row[:],
                           rowb[0:1, width + nt * 512: width + nt * 512 + 512],
                           start=True, stop=True)
            P.ts.bump(ins, 1)

    @block.scalar
    def _(s):
        for nt in range(NTl):
            s.wait_ge(P.ts.h, tb0 + nt + 1)
            s.copy(ab[:, nt * 512:(nt + 1) * 512], bcA[:])
            ins = s.copy(ab[:, width + nt * 512: width + nt * 512 + 512], bcB[:])
            P.ss.bump(ins, 1)
    s_ab = P.ss.n

    @block.vector
    def _(v):
        v.wait_ge(P.ss.h, s_ab)
        for kc in range(FC):
            sl = slice(kc * tokens + off, kc * tokens + off + width)
            v.tensor_tensor(xn_bf[:, sl], xn_bf[:, sl], ab[:, 0:width], ALU.mult)
            ins = v.tensor_tensor(xn_bf[:, sl], xn_bf[:, sl],
                                  ab[:, width:2 * width], ALU.add)
            if gamma_col is not None:
                ins = v.tensor_scalar(xn_bf[:, sl], xn_bf[:, sl],
                                      gamma_col[:, kc:kc + 1], beta_col[:, kc:kc + 1],
                                      ALU.mult, ALU.add)
        P.vs.bump(ins, 1)
    return P.vs.n


def ln_scratch(P, stack, width=1024, tag=""):
    return dict(
        pp=P.sb("lnpp" + tag, [128, 1024], BF16, stack),
        row=P.sb("lnrow" + tag, [1, 2 * width], F32, stack),
        rowb=P.sb("lnrowb" + tag, [1, 2 * width], BF16, stack),
        ab=P.sb("lnab" + tag, [128, 2 * width], BF16, stack),
        psA=P.ps("lnpsA" + tag, [1, 512], F32, stack),
        psB=P.ps("lnpsB" + tag, [1, 512], F32, stack),
        bcA=P.ps("lnbcA" + tag, [128, 512], F32, stack),
        bcB=P.ps("lnbcB" + tag, [128, 512], F32, stack),
    )



class Evac:

    def __init__(self, P, kind, dst=None, bias=None, act=AF.Copy, scale=1.0,
                 dtokens=None, engine="scalar", dst_col0=0):
        self.P = P
        self.kind = kind
        self.dst = dst
        self.bias = bias
        self.act = act
        self.scale = scale
        self.dtok = dtokens
        self.plan_list = []
        self.engine = "vector" if kind == "resid" else engine
        self.sem = P.vs if self.engine == "vector" else P.ss
        self.dst_col0 = dst_col0

    def plan(self, m, nt, ps_ap, mmv):
        v = self.sem.reserve(1)
        self.plan_list.append((m, nt, ps_ap, mmv, v))
        return (self.sem.h, v)

    def emit(self, block, tile_n=512):
        P = self.P
        if self.engine == "vector":
            @block.vector
            def _(v):
                for (m, nt, ps_ap, mmv, val) in self.plan_list:
                    v.wait_ge(P.ts.h, mmv)
                    c0 = self.dst_col0 + m * self.dtok + nt * tile_n
                    if self.kind == "resid":
                        ins = v.scalar_tensor_tensor(
                            self.dst[:, c0:c0 + tile_n], ps_ap,
                            self.bias[:, m:m + 1] if self.bias is not None else 0.0,
                            self.dst[:, c0:c0 + tile_n], ALU.add, ALU.add)
                    else:
                        ins = v.tensor_copy(self.dst[:, c0:c0 + tile_n], ps_ap)
                    ins.then_inc(self.sem.h, 1)
        else:
            @block.scalar
            def _(s):
                for (m, nt, ps_ap, mmv, val) in self.plan_list:
                    s.wait_ge(P.ts.h, mmv)
                    c0 = self.dst_col0 + m * self.dtok + nt * tile_n
                    if self.bias is not None:
                        ins = s.activation(self.dst[:, c0:c0 + tile_n], ps_ap, self.act,
                                           bias=self.bias[:, m:m + 1], scale=self.scale)
                    else:
                        ins = s.activation(self.dst[:, c0:c0 + tile_n], ps_ap, self.act,
                                           scale=self.scale)
                    ins.then_inc(self.sem.h, 1)


def emit_proj(P, block, w_ap, din, dout, rhs_bf, tokens, evac, extra_mm=None,
              rhs_ready=None, wtag="w", n_banks=4, stack=None, tile_n=512,
              wait_extra=None, wb=None, w_gate=None, rhs_stride=None, rhs_col0=0,
              banks=None):
    KC = din // 128
    MC = (dout + 127) // 128
    NTl = tokens // tile_n
    if rhs_stride is None:
        rhs_stride = tokens
    st = stack or P.st
    if wb is None:
        wb = [P.sb(f"{wtag}{i}", [128, KC * 128], BF16, st) for i in range(2)]
    if banks is None:
        banks = [P.ps(f"{wtag}ps{i}", [128, tile_n], F32, st) for i in range(n_banks)]
    n_banks = len(banks)

    ts0 = P.ts.n

    @block.gpsimd
    def _(g):
        if w_gate is not None:
            g.wait_ge(w_gate[0], w_gate[1])
        for m in range(MC):
            par = m % 2
            mw = min(128, dout - m * 128)
            if m >= 2:
                g.wait_ge(P.ts.h, ts0 + (m - 1) * NTl)
            src = w_ap[:, m * 128:m * 128 + mw]
            if KC > 1:
                src = src.rearrange("(kc p) m -> p kc m", p=128)
                dst = wb[par][:, 0:KC * mw].rearrange("p (kc m) -> p kc m", kc=KC)
            else:
                dst = wb[par][0:din, 0:mw]
            P.dw.bump(g.dma_start(dst, src), 16)
    w_base = P.dw.n - 16 * MC

    bank_gate = [None] * n_banks

    @block.tensor
    def _(t):
        if rhs_ready is not None:
            t.wait_ge(rhs_ready[0], rhs_ready[1])
        if wait_extra is not None:
            t.wait_ge(wait_extra[0], wait_extra[1])
        for m in range(MC):
            par = m % 2
            mw = min(128, dout - m * 128)
            t.wait_ge(P.dw.h, w_base + 16 * (m + 1))
            for nt in range(NTl):
                gi = m * NTl + nt
                bank = gi % n_banks
                if bank_gate[bank] is not None:
                    t.wait_ge(*bank_gate[bank])
                for k in range(KC):
                    last = (k == KC - 1) and extra_mm is None
                    c0 = k * rhs_stride + rhs_col0 + nt * tile_n
                    ins = t.matmul(
                        banks[bank][0:mw, :],
                        wb[par][:, k * mw:(k + 1) * mw],
                        rhs_bf[:, c0: c0 + tile_n],
                        start=(k == 0), stop=last)
                if extra_mm is not None:
                    ins = extra_mm(t, m, nt, banks[bank][0:mw, :])
                P.ts.bump(ins, 1)
                bank_gate[bank] = evac.plan(m, nt, banks[bank][0:mw, :], P.ts.n)
    evac.emit(block, tile_n=tile_n)



class Sched:
    ENGS = ("tensor", "vector", "scalar", "gpsimd", "sync")

    def __init__(self, P):
        self.P = P
        self.ops = {e: [] for e in self.ENGS}

    def add(self, eng, fn, waits=(), bump=None, k=1):
        val = bump.reserve(k) if bump is not None else None
        self.ops[eng].append((fn, [w for w in waits if w is not None],
                              bump.h if bump is not None else None, k))
        return val

    def emit(self, block):
        for eng in self.ENGS:
            if not self.ops[eng]:
                continue

            def section(e, eng=eng):
                for fn, waits, bh, k in self.ops[eng]:
                    for (h, v) in waits:
                        e.wait_ge(h, v)
                    ins = fn(e)
                    if bh is not None:
                        ins.then_inc(bh, k)
            getattr(block, eng)(section)


def emit_join(P, block, eng, waits, sem):
    def section(e):
        for (h, v) in waits:
            e.wait_ge(h, v)
        e.nop().then_inc(sem.h, 1)
    getattr(block, eng)(section)
    sem.n += 1
    return (sem.h, sem.n)



def emit_L3_attn(P, x_fm, xn_bf, x_gate):
    Wqkv = P.din("L3a_Wqkv", [D, 3 * D])
    Wo = P.din("L3a_Wo", [D, D])
    bqk = P.din("L3a_bqk", [128, 16])
    bv = P.din("L3a_bv", [1, D])
    bo = P.din("L3a_bo", [128, DC])
    gg = P.din("L3a_g", [128, DC])
    bb = P.din("L3a_b", [128, DC])

    with contextlib.ExitStack() as st:
        with P.nc.Block() as block:
            bqk_sb = P.sb("l3a_bqk", [128, 16], F32, st)
            bo_sb = P.sb("l3a_bo", [128, DC], F32, st)
            g_sb = P.sb("l3a_g", [128, DC], F32, st)
            b_sb = P.sb("l3a_b", [128, DC], F32, st)
            bvrow = P.sb("l3a_bvrow", [1, D], BF16, st)
            bvbc = P.sb("l3a_bvbc", [128, D], BF16, st)
            wqb = [P.sb(f"l3a_wq{i}", [128, DC * 128], BF16, st) for i in range(2)]
            wkb = [P.sb(f"l3a_wk{i}", [128, DC * 128], BF16, st) for i in range(2)]
            wvb = [P.sb(f"l3a_wv{i}", [128, DC * 128], BF16, st) for i in range(2)]
            q_sb = P.sb("l3a_q", [128, T], BF16, st)
            k_sb = P.sb("l3a_k", [128, T], BF16, st)
            v_tm = P.sb("l3a_v", [128, 16 * 130], BF16, st)
            ptile = P.sb("l3a_p", [128, 2 * 512], BF16, st)
            zrow = P.sb("l3a_z", [1, 512], F32, st)
            rz = P.sb("l3a_rz", [1, 512], F32, st)
            rzb = P.sb("l3a_rzb", [64, 512], F32, st)
            attn = P.sb("l3a_attn", [128, DC * T], BF16, st)
            wob = [P.sb(f"l3a_wo{i}", [128, DC * 128], BF16, st) for i in range(2)]

            @block.gpsimd
            def _(g):
                P.dw.bump(g.dma_start(bqk_sb[:], bqk), 16)
                P.dw.bump(g.dma_start(bo_sb[:], bo), 16)
                P.dw.bump(g.dma_start(g_sb[:], gg), 16)
                P.dw.bump(g.dma_start(b_sb[:], bb), 16)
                P.dw.bump(g.dma_start(bvrow[:], bv), 16)
            small_v = P.dw.n

            xng = emit_ln_fm(P, block, x_fm, xn_bf, x_gate=x_gate)
            gate_inner = emit_join(P, block, "vector",
                                   [(P.vs.h, xng), (P.dw.h, small_v)], P.vs)
            xng2 = emit_ln_fm(P, block, None, xn_bf, eps=1e-5, gamma_col=g_sb,
                              beta_col=b_sb, x_gate=gate_inner)
            xn_rdy = (P.vs.h, xng2)

            pst = contextlib.ExitStack()
            scps = [P.ps(f"l3a_sc{i}", [128, 512], F32, pst) for i in range(2)]
            outps = [P.ps(f"l3a_out{i}", [65, 512], F32, pst) for i in range(2)]
            rzps = P.ps("l3a_rzps", [64, 512], F32, pst)
            vps = [P.ps(f"l3a_vps{i}", [128, 128], F32, pst) for i in range(2)]

            S = Sched(P)
            ones_row = P.consts["c_ones_row"]

            gate_scps = [None, None]
            gate_outps = [None, None]
            gate_rzps = None
            gate_vps = [None, None]
            for half in range(2):
                tv = S.add("tensor", lambda t, half=half: t.matmul(
                    scps[0][:], ones_row[:], bvrow[0:1, half * 512:(half + 1) * 512],
                    start=True, stop=True),
                    waits=[(P.dw.h, small_v), xn_rdy, gate_scps[0]], bump=P.ts)
                sv = S.add("scalar", lambda s, half=half: s.copy(
                    bvbc[:, half * 512:(half + 1) * 512], scps[0][:]),
                    waits=[(P.ts.h, tv)], bump=P.ss)
                gate_scps[0] = (P.ss.h, sv)
            vset = S.add("vector", lambda v: v.memset(v_tm[:], 1.0), bump=P.vs)

            w_loaded = []
            qkv_done = []
            oevac_last = None
            for hg in range(8):
                par = hg % 2
                def wload(g, hg=hg, par=par):
                    srcs = ((wqb[par], 0), (wkb[par], D), (wvb[par], 2 * D))
                    for i, (wbuf, c0) in enumerate(srcs):
                        ins = g.dma_start(
                            wbuf[:].rearrange("p (kc m) -> p kc m", kc=DC),
                            Wqkv[:, c0 + hg * 128: c0 + hg * 128 + 128]
                            .rearrange("(kc p) m -> p kc m", p=128))
                        if i < 2:
                            ins.then_inc(P.dw.h, 16)
                    return ins
                P.dw.reserve(32)
                dv = S.add("gpsimd", wload,
                           waits=[(P.ts.h, qkv_done[hg - 2])] if hg >= 2 else [],
                           bump=P.dw, k=16)
                w_loaded.append(dv)

                for which, wbuf, dst, bcol in (("q", wqb[par], q_sb, hg),
                                               ("k", wkb[par], k_sb, 8 + hg)):
                    for nt in range(NT):
                        bank = nt % 2
                        def mmf(t, wbuf=wbuf, nt=nt, bank=bank):
                            for kc in range(DC):
                                ins = t.matmul(
                                    scps[bank][:],
                                    wbuf[:, kc * 128:(kc + 1) * 128],
                                    xn_bf[:, kc * T + nt * 512: kc * T + nt * 512 + 512],
                                    start=(kc == 0), stop=(kc == DC - 1))
                            return ins
                        tv = S.add("tensor", mmf,
                                   waits=[xn_rdy, (P.dw.h, dv), gate_scps[bank]],
                                   bump=P.ts)
                        sv = S.add("vector", lambda v, dst=dst, nt=nt, bank=bank, bcol=bcol:
                                   v.tensor_scalar(dst[:, nt * 512:(nt + 1) * 512],
                                                   scps[bank][:],
                                                   bqk_sb[:, bcol:bcol + 1], None, ALU.add),
                                   waits=[(P.ts.h, tv)], bump=P.vs)
                        gate_scps[bank] = (P.vs.h, sv)

                for tt in range(16):
                    bank = tt % 2
                    def vmmf(t, tt=tt, bank=bank, par=par):
                        for kc in range(DC):
                            ins = t.matmul(
                                vps[bank][:],
                                xn_bf[:, kc * T + tt * 128: kc * T + tt * 128 + 128],
                                wvb[par][:, kc * 128:(kc + 1) * 128],
                                start=(kc == 0), stop=(kc == DC - 1))
                        return ins
                    tv = S.add("tensor", vmmf,
                               waits=[xn_rdy, (P.dw.h, dv), gate_vps[bank]],
                               bump=P.ts)
                    def vev(v, tt=tt, bank=bank, hg=hg):
                        v.tensor_tensor(v_tm[:, tt * 130: tt * 130 + 64],
                                        vps[bank][:, 0:64],
                                        bvbc[:, hg * 128: hg * 128 + 64], ALU.add)
                        return v.tensor_tensor(v_tm[:, tt * 130 + 65: tt * 130 + 129],
                                               vps[bank][:, 64:128],
                                               bvbc[:, hg * 128 + 64: hg * 128 + 128],
                                               ALU.add)
                    vv = S.add("vector", vev,
                               waits=[(P.ts.h, tv), (P.vs.h, vset)], bump=P.vs)
                    gate_vps[bank] = (P.vs.h, vv)
                qkv_done.append(P.ts.n)

                for head in range(2):
                    hb = head * 64
                    for qg in range(NT):
                        grp = head * NT + qg
                        ob = grp % 2
                        pv_last = None
                        exps = {}
                        def pv_emit(kt, head=head, ob=ob):
                            sb_ = kt % 2
                            pv_waits = [(P.ss.h, exps[kt])]
                            if kt == 0:
                                pv_waits.append(gate_outps[ob])
                            return S.add("tensor",
                                         lambda t, kt=kt, head=head, ob=ob, sb_=sb_:
                                         t.matmul(outps[ob][:],
                                                  v_tm[:, kt * 130 + head * 65:
                                                       kt * 130 + head * 65 + 65],
                                                  ptile[:, sb_ * 512:(sb_ + 1) * 512],
                                                  start=(kt == 0), stop=(kt == 15)),
                                         waits=pv_waits, bump=P.ts)
                        for kt in range(16):
                            sb_ = kt % 2
                            tv = S.add("tensor", lambda t, hb=hb, kt=kt, qg=qg, sb_=sb_:
                                       t.matmul(scps[sb_][:],
                                                k_sb[hb:hb + 64, kt * 128:(kt + 1) * 128],
                                                q_sb[hb:hb + 64, qg * 512:(qg + 1) * 512],
                                                start=True, stop=True),
                                       waits=[gate_scps[sb_]], bump=P.ts)
                            ev = S.add("scalar", lambda s, sb_=sb_:
                                       s.activation(ptile[:, sb_ * 512:(sb_ + 1) * 512],
                                                    scps[sb_][:], AF.Exp,
                                                    scale=0.125, bias=P.consts["c_nb10"][:]),
                                       waits=[(P.ts.h, tv)], bump=P.ss)
                            exps[kt] = ev
                            gate_scps[sb_] = (P.ss.h, ev)
                            if kt >= 1:
                                pv_last = pv_emit(kt - 1)
                                gate_scps[(kt - 1) % 2] = (P.ts.h, P.ts.n)
                        pv_last = pv_emit(15)
                        gate_scps[1] = (P.ts.h, P.ts.n)
                        zv = S.add("scalar", lambda s, ob=ob: s.copy(zrow[:], outps[ob][64:65, :]),
                                   waits=[(P.ts.h, pv_last)], bump=P.ss)
                        rv = S.add("vector", lambda v: v.reciprocal(rz[:], zrow[:]),
                                   waits=[(P.ss.h, zv)], bump=P.vs)
                        bcv = S.add("tensor", lambda t: t.matmul(
                            rzps[:], P.consts["c_ones_row_f"][0:1, 0:64], rz[:],
                            start=True, stop=True),
                            waits=[(P.vs.h, rv), gate_rzps], bump=P.ts)
                        rbv = S.add("scalar", lambda s: s.copy(rzb[:], rzps[:]),
                                    waits=[(P.ts.h, bcv)], bump=P.ss)
                        gate_rzps = (P.ss.h, rbv)
                        oev = S.add("vector", lambda v, hb=hb, hg=hg, qg=qg, ob=ob:
                                    v.tensor_tensor(
                                        attn[hb:hb + 64, hg * T + qg * 512: hg * T + qg * 512 + 512],
                                        outps[ob][0:64, :], rzb[:], ALU.mult),
                                    waits=[(P.ss.h, rbv)], bump=P.vs)
                        gate_outps[ob] = (P.vs.h, oev)
                        oevac_last = oev
            S.emit(block)
            pst.close()

            ev = Evac(P, "resid", dst=x_fm, bias=bo_sb, dtokens=T)
            emit_proj(P, block, Wo, D, D, attn, T, ev,
                      rhs_ready=(P.vs.h, oevac_last), wtag="l3aw", stack=st, wb=wob)
    return



def emit_L0_attn(P, x_fm, xn_bf, x_gate):
    Wq = P.din("L0a_Wq", [D, D])
    Wk = P.din("L0a_Wk", [D, D])
    Wv = P.din("L0a_Wv", [D, D])
    WoT = P.din("L0a_WoT", [D, D])
    qb = P.din("L0a_qb", [128, DC])
    kb = P.din("L0a_kb", [128, DC])
    bv = P.din("L0a_bv", [1, D])
    bo = P.din("L0a_bo", [128, DC])
    porder = [2 * i for i in range(8)] + [2 * i + 1 for i in range(8)]

    with contextlib.ExitStack() as st:
        with P.nc.Block() as block:
            wq_sb = P.sb("l0a_wq", [128, DC * 1024], BF16, st)
            wk_sb = P.sb("l0a_wk", [128, DC * 1024], BF16, st)
            wv_sb = P.sb("l0a_wv", [128, DC * 1024], BF16, st)
            wo_sb = P.sb("l0a_wo", [128, DC * 1024], BF16, st)
            qb_sb = P.sb("l0a_qbs", [128, DC], F32, st)
            kb_sb = P.sb("l0a_kbs", [128, DC], F32, st)
            bo_sb = P.sb("l0a_bos", [128, DC], F32, st)
            bvrow = P.sb("l0a_bvrow", [1, D], BF16, st)
            bvbc = P.sb("l0a_bvbc", [128, D], BF16, st)
            qt = P.sb("l0a_qt", [64, 2048], BF16, st)
            kt = P.sb("l0a_kt", [64, 2048], BF16, st)
            v_tm = P.sb("l0a_v", [128, 16 * 65], BF16, st)
            ptile = P.sb("l0a_p", [128, 2 * 512], BF16, st)
            zrow = P.sb("l0a_z", [1, 512], F32, st)
            rz = P.sb("l0a_rz", [1, 512], F32, st)
            rzb = P.sb("l0a_rzb", [64, 512], F32, st)
            attn_h = P.sb("l0a_attn", [128, DC * 128], BF16, st)

            @block.gpsimd
            def _(g):
                for dst, src in ((wq_sb, Wq), (wk_sb, Wk), (wv_sb, Wv), (wo_sb, WoT)):
                    P.dw.bump(g.dma_start(
                        dst[:].rearrange("p (kc m) -> p kc m", kc=DC),
                        src.rearrange("(kc p) m -> p kc m", p=128)), 16)
                P.dw.bump(g.dma_start(qb_sb[:], qb), 16)
                P.dw.bump(g.dma_start(kb_sb[:], kb), 16)
                P.dw.bump(g.dma_start(bo_sb[:], bo), 16)
                P.dw.bump(g.dma_start(bvrow[:], bv), 16)
            w_v = P.dw.n

            xng = emit_ln_fm(P, block, x_fm, xn_bf, x_gate=x_gate)
            xn_rdy = (P.vs.h, xng)

            pst = contextlib.ExitStack()
            scps = [P.ps(f"l0a_sc{i}", [128, 512], F32, pst) for i in range(2)]
            outps = [P.ps(f"l0a_out{i}", [65, 512], F32, pst) for i in range(2)]
            rzps = P.ps("l0a_rzps", [64, 512], F32, pst)
            vps = [P.ps(f"l0a_vps{i}", [128, 512], F32, pst) for i in range(2)]
            wops = P.ps("l0a_wops", [128, 128], F32, pst)

            S = Sched(P)
            ones_row = P.consts["c_ones_row"]
            onesf = P.consts["c_ones_row_f"]

            gate_scps = [None, None]
            gate_outps = [None, None]
            gate_rzps = None
            gate_vps = [None, None]
            gate_wops = None
            for half in range(2):
                tv = S.add("tensor", lambda t, half=half: t.matmul(
                    scps[0][:], ones_row[:], bvrow[0:1, half * 512:(half + 1) * 512],
                    start=True, stop=True),
                    waits=[(P.dw.h, w_v), xn_rdy, gate_scps[0]], bump=P.ts)
                sv = S.add("scalar", lambda s, half=half: s.copy(
                    bvbc[:, half * 512:(half + 1) * 512], scps[0][:]),
                    waits=[(P.ts.h, tv)], bump=P.ss)
                gate_scps[0] = (P.ss.h, sv)
            vset = S.add("vector", lambda v: v.memset(v_tm[:], 1.0), bump=P.vs)

            for h in range(16):
                for wsb, dst, bcol_sb, order in ((wq_sb, qt, qb_sb, "porder"),
                                                 (wk_sb, kt, kb_sb, "nat")):
                    for j in range(8):
                        bank = j % 2
                        def mmf(t, wsb=wsb, j=j, bank=bank, h=h):
                            for kc in range(DC):
                                ins = t.matmul(
                                    scps[bank][:, 0:128],
                                    wsb[:, kc * 1024 + j * 128: kc * 1024 + j * 128 + 128],
                                    xn_bf[:, kc * T + h * 128: kc * T + h * 128 + 128],
                                    start=(kc == 0), stop=(kc == DC - 1))
                            return ins
                        tv = S.add("tensor", mmf,
                                   waits=[xn_rdy, (P.dw.h, w_v), gate_scps[bank]],
                                   bump=P.ts)
                        if order == "porder":
                            p0, p1 = j, 8 + j
                        else:
                            p0, p1 = 2 * j, 2 * j + 1
                        def qev(v, dst=dst, j=j, bank=bank, bcol_sb=bcol_sb, p0=p0, p1=p1):
                            v.tensor_scalar(dst[:, p0 * 128:(p0 + 1) * 128],
                                            scps[bank][0:64, 0:128],
                                            bcol_sb[0:64, j:j + 1], None, ALU.add)
                            return v.tensor_scalar(dst[:, p1 * 128:(p1 + 1) * 128],
                                                   scps[bank][64:128, 0:128],
                                                   bcol_sb[64:128, j:j + 1], None, ALU.add)
                        vv = S.add("vector", qev, waits=[(P.ts.h, tv)], bump=P.vs)
                        gate_scps[bank] = (P.vs.h, vv)
                for half in range(2):
                    bank = half % 2
                    def vmmf(t, half=half, bank=bank, h=h):
                        for kc in range(DC):
                            ins = t.matmul(
                                vps[bank][:],
                                xn_bf[:, kc * T + h * 128: kc * T + h * 128 + 128],
                                wv_sb[:, kc * 1024 + half * 512: kc * 1024 + half * 512 + 512],
                                start=(kc == 0), stop=(kc == DC - 1))
                        return ins
                    tv = S.add("tensor", vmmf,
                               waits=[xn_rdy, (P.dw.h, w_v), gate_vps[bank]],
                               bump=P.ts)
                    def vev(v, half=half, bank=bank):
                        for i in range(8):
                            c16 = half * 8 + i
                            ins = v.tensor_tensor(
                                v_tm[:, c16 * 65: c16 * 65 + 64],
                                vps[bank][:, i * 64: i * 64 + 64],
                                bvbc[:, c16 * 64: c16 * 64 + 64], ALU.add)
                        return ins
                    vv = S.add("vector", vev,
                               waits=[(P.ts.h, tv), (P.vs.h, vset)], bump=P.vs)
                    gate_vps[bank] = (P.vs.h, vv)
                oevac_h = None
                for qg in range(4):
                    ob = qg % 2
                    pv_last = None
                    exps = {}
                    def pv_emit(c16k, ob=ob):
                        sb_ = c16k % 2
                        pv_waits = [(P.ss.h, exps[c16k])]
                        if c16k == 0:
                            pv_waits.append(gate_outps[ob])
                        return S.add("tensor", lambda t, c16k=c16k, ob=ob, sb_=sb_:
                                     t.matmul(outps[ob][:],
                                              v_tm[:, c16k * 65: c16k * 65 + 65],
                                              ptile[:, sb_ * 512:(sb_ + 1) * 512],
                                              start=(c16k == 0), stop=(c16k == 15)),
                                     waits=pv_waits, bump=P.ts)
                    for c16k in range(16):
                        sb_ = c16k % 2
                        tv = S.add("tensor", lambda t, c16k=c16k, qg=qg, sb_=sb_:
                                   t.matmul(scps[sb_][:],
                                            kt[:, c16k * 128:(c16k + 1) * 128],
                                            qt[:, qg * 512:(qg + 1) * 512],
                                            start=True, stop=True),
                                   waits=[gate_scps[sb_]], bump=P.ts)
                        ev = S.add("scalar", lambda s, sb_=sb_:
                                   s.activation(ptile[:, sb_ * 512:(sb_ + 1) * 512],
                                                scps[sb_][:], AF.Exp,
                                                scale=0.125, bias=P.consts["c_nb10"][:]),
                                   waits=[(P.ts.h, tv)], bump=P.ss)
                        exps[c16k] = ev
                        gate_scps[sb_] = (P.ss.h, ev)
                        if c16k >= 1:
                            pv_last = pv_emit(c16k - 1)
                            gate_scps[(c16k - 1) % 2] = (P.ts.h, P.ts.n)
                    pv_last = pv_emit(15)
                    gate_scps[1] = (P.ts.h, P.ts.n)
                    zv = S.add("scalar", lambda s, ob=ob: s.copy(zrow[:], outps[ob][64:65, :]),
                               waits=[(P.ts.h, pv_last)], bump=P.ss)
                    rv = S.add("vector", lambda v: v.reciprocal(rz[:], zrow[:]),
                               waits=[(P.ss.h, zv)], bump=P.vs)
                    bcv = S.add("tensor", lambda t: t.matmul(
                        rzps[:], onesf[0:1, 0:64], rz[:], start=True, stop=True),
                        waits=[(P.vs.h, rv), gate_rzps], bump=P.ts)
                    rbv = S.add("scalar", lambda s: s.copy(rzb[:], rzps[:]),
                                waits=[(P.ts.h, bcv)], bump=P.ss)
                    gate_rzps = (P.ss.h, rbv)
                    def oev_f(v, qg=qg, ob=ob):
                        for i in range(4):
                            c16q = porder[qg * 4 + i]
                            ins = v.tensor_tensor(
                                attn_h[(c16q % 2) * 64:(c16q % 2) * 64 + 64,
                                       (c16q // 2) * 128:(c16q // 2) * 128 + 128],
                                outps[ob][0:64, i * 128:(i + 1) * 128],
                                rzb[:, i * 128:(i + 1) * 128], ALU.mult)
                        return ins
                    oev = S.add("vector", oev_f, waits=[(P.ss.h, rbv)], bump=P.vs)
                    gate_outps[ob] = (P.vs.h, oev)
                    oevac_h = oev
                for m in range(8):
                    def womm(t, m=m, h=h):
                        for kc in range(DC):
                            ins = t.matmul(
                                wops[:],
                                wo_sb[:, kc * 1024 + m * 128: kc * 1024 + m * 128 + 128],
                                attn_h[:, kc * 128:(kc + 1) * 128],
                                start=(kc == 0), stop=(kc == DC - 1))
                        return ins
                    tv = S.add("tensor", womm,
                               waits=[(P.vs.h, oevac_h), gate_wops], bump=P.ts)
                    vv = S.add("vector", lambda v, m=m, h=h:
                               v.scalar_tensor_tensor(
                                   x_fm[:, m * T + h * 128: m * T + h * 128 + 128],
                                   wops[:], bo_sb[:, m:m + 1],
                                   x_fm[:, m * T + h * 128: m * T + h * 128 + 128],
                                   ALU.add, ALU.add),
                               waits=[(P.ts.h, tv)], bump=P.vs)
                    gate_wops = (P.vs.h, vv)
            S.emit(block)
            pst.close()
    return




def emit_L2_attn(P, x_fm, xn_bf, x_gate):
    Wqm = P.din("L2a_wq", [D, D])
    Wkd = P.din("L2a_wkd", [D, 64])
    Wvd = P.din("L2a_wvd", [D, 64])
    Wku = P.din("L2a_wku", [32, 128])
    Wvu = P.din("L2a_wvu", [32, 128])
    Wom = P.din("L2a_wo", [D, D])
    CMA = P.din("c_l2a_cmask", [128, 896])

    with contextlib.ExitStack() as st:
        with P.nc.Block() as block:
            wkd_sb = P.sb("l2a_wkd", [128, DC * 64], BF16, st)
            wvd_sb = P.sb("l2a_wvd", [128, DC * 64], BF16, st)
            wku_sb = P.sb("l2a_wku", [32, 128], BF16, st)
            wvu_sb = P.sb("l2a_wvu", [32, 128], BF16, st)
            cm_sb = P.sb("l2a_cmloc", [128, 896], BF16, st)
            wqb = [P.sb(f"l2a_wq{i}", [128, DC * 128], BF16, st) for i in range(2)]
            klat = P.sb("l2a_klat", [32, 2 * T], BF16, st)
            vlat = P.sb("l2a_vlat", [32, 2 * T], BF16, st)
            k_sb = P.sb("l2a_k", [128, 2 * T], BF16, st)
            v_tm = P.sb("l2a_v", [128, 16 * 129], BF16, st)
            q_h = P.sb("l2a_q", [128, T], BF16, st)
            ptile = P.sb("l2a_p", [128, 2 * 512], BF16, st)
            zrow = P.sb("l2a_z", [1, 512], F32, st)
            rz = P.sb("l2a_rz", [1, 512], F32, st)
            rzb = P.sb("l2a_rzb", [64, 512], F32, st)
            attn = P.sb("l2a_attn", [128, DC * T], BF16, st)
            wob = [P.sb(f"l2a_wo{i}", [128, DC * 128], BF16, st) for i in range(2)]

            @block.gpsimd
            def _(g):
                P.dw.bump(g.dma_start(
                    wkd_sb[:].rearrange("p (kc m) -> p kc m", kc=DC),
                    Wkd.rearrange("(kc p) m -> p kc m", p=128)), 16)
                P.dw.bump(g.dma_start(
                    wvd_sb[:].rearrange("p (kc m) -> p kc m", kc=DC),
                    Wvd.rearrange("(kc p) m -> p kc m", p=128)), 16)
                P.dw.bump(g.dma_start(wku_sb[:], Wku), 16)
                P.dw.bump(g.dma_start(wvu_sb[:], Wvu), 16)
                P.dw.bump(g.dma_start(cm_sb[:], CMA), 16)
            w_v = P.dw.n

            xng = emit_ln_fm(P, block, x_fm, xn_bf, x_gate=x_gate)
            xn_rdy = (P.vs.h, xng)

            pst = contextlib.ExitStack()
            scps = [P.ps(f"l2a_sc{i}", [128, 512], F32, pst) for i in range(2)]
            out1 = [P.ps(f"l2a_o1{i}", [65, 512], F32, pst) for i in range(2)]
            out2 = [P.ps(f"l2a_o2{i}", [64, 512], F32, pst) for i in range(2)]
            rzps = P.ps("l2a_rzps", [64, 512], F32, pst)
            latps = P.ps("l2a_lat", [64, 512], F32, pst)

            S = Sched(P)
            onesf = P.consts["c_ones_row_f"]
            gate_scps = [None, None]
            gate_out = [None, None]
            gate_rzps = None
            gate_latps = None

            for which, wsb, dst in (("k", wkd_sb, klat), ("v", wvd_sb, vlat)):
                for nt in range(NT):
                    def lmm(t, wsb=wsb, nt=nt):
                        for kc in range(DC):
                            ins = t.matmul(
                                latps[:],
                                wsb[:, kc * 64:(kc + 1) * 64],
                                xn_bf[:, kc * T + nt * 512: kc * T + nt * 512 + 512],
                                start=(kc == 0), stop=(kc == DC - 1))
                        return ins
                    tv = S.add("tensor", lmm,
                               waits=[xn_rdy, (P.dw.h, w_v), gate_latps], bump=P.ts)
                    def lev(v, dst=dst, nt=nt):
                        v.tensor_copy(dst[:, nt * 512:(nt + 1) * 512], latps[0:32, :])
                        return v.tensor_copy(dst[:, T + nt * 512: T + nt * 512 + 512],
                                             latps[32:64, :])
                    vv = S.add("vector", lev, waits=[(P.ts.h, tv)], bump=P.vs)
                    gate_latps = (P.vs.h, vv)

            kv_done = []
            for hk in range(2):
                for nt in range(NT):
                    def kmm(t, hk=hk, nt=nt):
                        return t.matmul(
                            scps[nt % 2][:],
                            wku_sb[:],
                            klat[:, hk * T + nt * 512: hk * T + nt * 512 + 512],
                            start=True, stop=True)
                    tv = S.add("tensor", kmm,
                               waits=[(P.dw.h, w_v), gate_scps[nt % 2]], bump=P.ts)
                    sv = S.add("scalar", lambda s, hk=hk, nt=nt:
                               s.copy(k_sb[:, hk * T + nt * 512: hk * T + nt * 512 + 512],
                                      scps[nt % 2][:]),
                               waits=[(P.ts.h, tv)], bump=P.ss)
                    gate_scps[nt % 2] = (P.ss.h, sv)
            vset = S.add("vector", lambda v: v.memset(v_tm[:], 1.0), bump=P.vs)

            oev_last = None
            for hk in range(2):
                for tt in range(16):
                    def vmm(t, hk=hk, tt=tt):
                        return t.matmul(
                            scps[tt % 2][:, 0:128],
                            vlat[:, hk * T + tt * 128: hk * T + tt * 128 + 128],
                            wvu_sb[:],
                            start=True, stop=True)
                    tv = S.add("tensor", vmm,
                               waits=[(P.vs.h, vset), gate_scps[tt % 2]], bump=P.ts)
                    def vev(v, tt=tt):
                        v.tensor_copy(v_tm[:, tt * 129: tt * 129 + 64],
                                      scps[tt % 2][:, 0:64])
                        return v.tensor_copy(v_tm[:, tt * 129 + 65: tt * 129 + 129],
                                             scps[tt % 2][:, 64:128])
                    vv = S.add("vector", vev, waits=[(P.ts.h, tv)], bump=P.vs)
                    gate_scps[tt % 2] = (P.vs.h, vv)

                for r in range(4):
                    par = r % 2
                    hh = hk * 4 + r
                    dv = S.add("gpsimd", lambda g, hk=hk, r=r, par=par: g.dma_start(
                        wqb[par][:].rearrange("p (kc m) -> p kc m", kc=DC),
                        Wqm[:, hk * 512 + r * 128: hk * 512 + r * 128 + 128]
                        .rearrange("(kc p) m -> p kc m", p=128)),
                        waits=[(P.ts.h, kv_done[hh - 2])] if hh >= 2 else [],
                        bump=P.dw, k=16)
                    for nt in range(NT):
                        def qmm(t, par=par, nt=nt):
                            for kc in range(DC):
                                ins = t.matmul(
                                    scps[nt % 2][:],
                                    wqb[par][:, kc * 128:(kc + 1) * 128],
                                    xn_bf[:, kc * T + nt * 512: kc * T + nt * 512 + 512],
                                    start=(kc == 0), stop=(kc == DC - 1))
                            return ins
                        tv = S.add("tensor", qmm,
                                   waits=[(P.dw.h, dv), gate_scps[nt % 2]], bump=P.ts)
                        sv = S.add("scalar", lambda s, nt=nt:
                                   s.copy(q_h[:, nt * 512:(nt + 1) * 512], scps[nt % 2][:]),
                                   waits=[(P.ts.h, tv)], bump=P.ss)
                        gate_scps[nt % 2] = (P.ss.h, sv)
                    kv_done.append(P.ts.n)

                    for qg in range(NT):
                        ob = qg % 2
                        nkt = 4 * qg + 4
                        pv_last = None
                        gates_p = {}
                        def pv_emit(kt, ob=ob, nkt=nkt):
                            sb_ = kt % 2
                            pv_waits = [gates_p[kt]]
                            if kt == 0:
                                pv_waits.append(gate_out[ob])
                            S.add("tensor", lambda t, kt=kt, ob=ob, sb_=sb_, nkt=nkt:
                                  t.matmul(out1[ob][:],
                                           v_tm[:, kt * 129: kt * 129 + 65],
                                           ptile[:, sb_ * 512:(sb_ + 1) * 512],
                                           start=(kt == 0), stop=(kt == nkt - 1)),
                                  waits=pv_waits, bump=P.ts)
                            return S.add("tensor",
                                         lambda t, kt=kt, ob=ob, sb_=sb_, nkt=nkt:
                                         t.matmul(out2[ob][:],
                                                  v_tm[:, kt * 129 + 65: kt * 129 + 129],
                                                  ptile[:, sb_ * 512:(sb_ + 1) * 512],
                                                  start=(kt == 0), stop=(kt == nkt - 1)),
                                         waits=[], bump=P.ts)
                        for kt in range(nkt):
                            sb_ = kt % 2
                            tv = S.add("tensor", lambda t, hk=hk, kt=kt, qg=qg, sb_=sb_:
                                       t.matmul(scps[sb_][:],
                                                k_sb[:, hk * T + kt * 128: hk * T + kt * 128 + 128],
                                                q_h[:, qg * 512:(qg + 1) * 512],
                                                start=True, stop=True),
                                       waits=[gate_scps[sb_]], bump=P.ts)
                            ev = S.add("scalar", lambda s, sb_=sb_:
                                       s.activation(ptile[:, sb_ * 512:(sb_ + 1) * 512],
                                                    scps[sb_][:], AF.Exp,
                                                    scale=0.0883883476483184,
                                                    bias=P.consts["c_nb10"][:]),
                                       waits=[(P.ts.h, tv)], bump=P.ss)
                            gate_scps[sb_] = (P.ss.h, ev)
                            gates_p[kt] = (P.ss.h, ev)
                            if kt >= 4 * qg:
                                j = kt - 4 * qg
                                mv = S.add("vector", lambda v, sb_=sb_, j=j:
                                           v.tensor_tensor(
                                               ptile[:, sb_ * 512:(sb_ + 1) * 512],
                                               ptile[:, sb_ * 512:(sb_ + 1) * 512],
                                               cm_sb[:, 384 - j * 128: 896 - j * 128],
                                               ALU.mult),
                                           waits=[(P.ss.h, ev)], bump=P.vs)
                                gate_scps[sb_] = (P.vs.h, mv)
                                gates_p[kt] = (P.vs.h, mv)
                            if kt >= 1:
                                pv_last = pv_emit(kt - 1)
                                gate_scps[(kt - 1) % 2] = (P.ts.h, P.ts.n)
                        pv_last = pv_emit(nkt - 1)
                        gate_scps[(nkt - 1) % 2] = (P.ts.h, P.ts.n)
                        zv = S.add("scalar", lambda s, ob=ob: s.copy(zrow[:], out1[ob][64:65, :]),
                                   waits=[(P.ts.h, pv_last)], bump=P.ss)
                        rv = S.add("vector", lambda v: v.reciprocal(rz[:], zrow[:]),
                                   waits=[(P.ss.h, zv)], bump=P.vs)
                        bcv = S.add("tensor", lambda t: t.matmul(
                            rzps[:], onesf[0:1, 0:64], rz[:], start=True, stop=True),
                            waits=[(P.vs.h, rv), gate_rzps], bump=P.ts)
                        rbv = S.add("scalar", lambda s: s.copy(rzb[:], rzps[:]),
                                    waits=[(P.ts.h, bcv)], bump=P.ss)
                        gate_rzps = (P.ss.h, rbv)
                        ch = hk * 4 + r
                        def oev_f(v, ch=ch, qg=qg, ob=ob):
                            v.tensor_tensor(
                                attn[0:64, ch * T + qg * 512: ch * T + qg * 512 + 512],
                                out1[ob][0:64, :], rzb[:], ALU.mult)
                            return v.tensor_tensor(
                                attn[64:128, ch * T + qg * 512: ch * T + qg * 512 + 512],
                                out2[ob][0:64, :], rzb[:], ALU.mult)
                        oev = S.add("vector", oev_f, waits=[(P.ss.h, rbv)], bump=P.vs)
                        gate_out[ob] = (P.vs.h, oev)
                        oev_last = oev
            S.emit(block)
            pst.close()

            ev = Evac(P, "resid", dst=x_fm, bias=None, dtokens=T)
            emit_proj(P, block, Wom, D, D, attn, T, ev,
                      rhs_ready=(P.vs.h, oev_last), wtag="l2aw", stack=st, wb=wob)
    return




def emit_quant(P, block, S, w_ap, kchunks, dout, dram_out, bufs, gates, tag):
    qbuf, absb, sgnb, qout, partials, thr, thcol, sps, bcps = bufs
    nslice = dout // 512
    nelem = kchunks * 128 * dout
    loaded = []
    for kc in range(kchunks):
        par = kc % 2
        gate = gates["qbuf"][par]
        dv = S.add("gpsimd", lambda g, kc=kc, par=par: g.dma_start(
            qbuf[par][:, 0:dout], w_ap[kc * 128:(kc + 1) * 128, :]),
            waits=[gate], bump=P.dw, k=16)
        av = S.add("scalar", lambda s, par=par: s.activation(
            absb[:, 0:dout], qbuf[par][:, 0:dout], AF.Abs),
            waits=[(P.dw.h, dv), gates["absb"]], bump=P.ss)
        gates["qbuf"][par] = (P.ss.h, av)

        def red(v, kc=kc):
            for sl in range(nslice):
                ins = v.reduce_sum(partials[:, kc * nslice + sl: kc * nslice + sl + 1],
                                   absb[:, sl * 512:(sl + 1) * 512], axis=AX.X)
            return ins
        rv = S.add("vector", red, waits=[(P.ss.h, av)], bump=P.vs)
        gates["absb"] = (P.vs.h, rv)
    tv = S.add("tensor", lambda t: t.matmul(
        sps[:], P.consts["c_ones_col_f"][:], partials[:, 0:kchunks * nslice],
        start=True, stop=True), waits=[gates["absb"]], bump=P.ts)
    sv = S.add("vector", lambda v: v.reduce_sum(thr[0:1, 0:1], sps[0:1, 0:kchunks * nslice],
                                                axis=AX.X),
               waits=[(P.ts.h, tv)], bump=P.vs)
    s2 = S.add("vector", lambda v: v.tensor_single_scalar(
        thr[0:1, 1:2], thr[0:1, 0:1], 0.5 / nelem, ALU.mult),
        waits=[(P.vs.h, sv)], bump=P.vs)
    s3 = S.add("vector", lambda v: v.tensor_single_scalar(
        thr[0:1, 1:2], thr[0:1, 1:2], 5e-7, ALU.add),
        waits=[(P.vs.h, s2)], bump=P.vs)
    bv = S.add("tensor", lambda t: t.matmul(
        bcps[:], P.consts["c_ones_row_f"][:], thr[0:1, 1:2], start=True, stop=True),
        waits=[(P.vs.h, s3)], bump=P.ts)
    cv = S.add("scalar", lambda s: s.copy(thcol[:], bcps[:]),
               waits=[(P.ts.h, bv)], bump=P.ss)
    th_gate = (P.ss.h, cv)
    for kc in range(kchunks):
        par = kc % 2
        dv = S.add("gpsimd", lambda g, kc=kc, par=par: g.dma_start(
            qbuf[par][:, 0:dout], w_ap[kc * 128:(kc + 1) * 128, :]),
            waits=[gates["qbuf"][par]], bump=P.dw, k=16)
        av = S.add("scalar", lambda s, par=par: s.activation(
            absb[:, 0:dout], qbuf[par][:, 0:dout], AF.Abs),
            waits=[(P.dw.h, dv), gates["absb"], th_gate], bump=P.ss)
        gv = S.add("scalar", lambda s, par=par: s.activation(
            sgnb[:, 0:dout], qbuf[par][:, 0:dout], AF.Sign),
            waits=[gates["sgnb"]], bump=P.ss)
        gates["qbuf"][par] = (P.ss.h, gv)
        mv = S.add("vector", lambda v: v.tensor_scalar(
            absb[:, 0:dout], absb[:, 0:dout], thcol[:], None, ALU.is_ge),
            waits=[(P.ss.h, av)], bump=P.vs)
        gates["absb"] = (P.vs.h, mv)
        qv = S.add("vector", lambda v: v.tensor_tensor(
            qout[:, 0:dout], sgnb[:, 0:dout], absb[:, 0:dout], ALU.mult),
            waits=[(P.vs.h, mv), (P.ss.h, gv), gates["qout"]], bump=P.vs)
        gates["sgnb"] = (P.vs.h, qv)
        gates["absb"] = (P.vs.h, qv)
        ov = S.add("sync", lambda sy, kc=kc: sy.dma_start(
            dram_out[kc * 128:(kc + 1) * 128, :], qout[:, 0:dout]),
            waits=[(P.vs.h, qv)], bump=P.dx, k=16)
        gates["qout"] = (P.dx.h, ov)
    return P.dx.n


def emit_L3_ffn(P, x_fm, xn_bf, x_gate):
    H3 = 4096
    HC = H3 // 128
    WfcT = P.din("L3f_WfcT", [D, H3])
    WprT = P.din("L3f_WprT", [H3, D])
    lnw = P.din("L3f_lnw", [128, HC])
    lnb = P.din("L3f_lnb", [128, HC])
    nc = P.nc
    qfc = nc.dram_tensor("l3f_qfc", [D, H3], BF16).ap()
    qpr = nc.dram_tensor("l3f_qpr", [H3, D], BF16).ap()

    with contextlib.ExitStack() as st:
        qsb = contextlib.ExitStack()
        with P.nc.Block() as block:
            qbuf = [P.sb(f"l3f_qb{i}", [128, H3], F32, qsb) for i in range(2)]
            absb = P.sb("l3f_abs", [128, H3], F32, qsb)
            sgnb = P.sb("l3f_sgn", [128, H3], F32, qsb)
            qout = P.sb("l3f_qo", [128, H3], BF16, qsb)
            partials = P.sb("l3f_part", [128, 64], F32, qsb)
            thr = P.sb("l3f_thr", [1, 2], F32, qsb)
            thcol = P.sb("l3f_thc", [128, 1], F32, qsb)
            qst = contextlib.ExitStack()
            sps = P.ps("l3f_sps", [1, 64], F32, qst)
            bcps = P.ps("l3f_bc", [128, 1], F32, qst)
            with nc.allow_low_precision("bitnet quantized weights are exact in bf16"):
                S = Sched(P)
                gates = dict(qbuf=[None, None], absb=None, sgnb=None, qout=None)
                emit_quant(P, block, S, WfcT, DC, H3, qfc,
                           (qbuf, absb, sgnb, qout, partials, thr, thcol, sps, bcps),
                           gates, "fc")
                S.emit(block)
                S2 = Sched(P)
                emit_quant(P, block, S2, WprT, HC, D, qpr,
                           (qbuf, absb, sgnb, qout, partials, thr, thcol, sps, bcps),
                           gates, "pr")
                S2.emit(block)
            qst.close()
        qsb.close()
        quant_done = P.dx.n

        with P.nc.Block() as block:
            lnw_sb = P.sb("l3f_lnw", [128, HC], F32, st)
            lnb_sb = P.sb("l3f_lnb", [128, HC], F32, st)
            h_bf = P.sb("l3f_h", [128, HC * 512], BF16, st)
            wbh = [P.sb(f"l3f_wh{i}", [128, DC * 128], BF16, st) for i in range(2)]
            wbp = [P.sb(f"l3f_wp{i}", [128, HC * 128], BF16, st) for i in range(2)]

            @block.gpsimd
            def _(g):
                P.dw.bump(g.dma_start(lnw_sb[:], lnw), 16)
                P.dw.bump(g.dma_start(lnb_sb[:], lnb), 16)
            small_v = P.dw.n

            xng = emit_ln_fm(P, block, x_fm, xn_bf, x_gate=x_gate)
            xn_rdy = (P.vs.h, xng)

            bank_st = contextlib.ExitStack()
            banksA = [P.ps(f"l3f_psA{i}", [128, 512], F32, bank_st) for i in range(2)]
            banksB = [P.ps(f"l3f_psB{i}", [128, 512], F32, bank_st) for i in range(2)]

            for tt in range(NT):
                evh = Evac(P, "copy", dst=h_bf, act=AF.Gelu, dtokens=512, engine="scalar")
                emit_proj(P, block, qfc, D, H3, xn_bf, 512, evh,
                          rhs_ready=xn_rdy, wtag=f"l3fh{tt}", stack=st,
                          wb=wbh, banks=banksA, rhs_stride=T, rhs_col0=tt * 512,
                          w_gate=(P.dx.h, quant_done))
                h_gate = (evh.sem.h, evh.sem.n)
                jg = emit_join(P, block, "scalar",
                               [h_gate, (P.dw.h, small_v)], P.ss)
                lng = emit_ln_fm(P, block, None, h_bf, eps=1e-5, gamma_col=lnw_sb,
                                 beta_col=lnb_sb, n_feat=H3, tokens=512, x_gate=jg)
                evp = Evac(P, "resid", dst=x_fm, bias=None, dtokens=T,
                           dst_col0=tt * 512)
                emit_proj(P, block, qpr, H3, D, h_bf, 512, evp,
                          rhs_ready=(P.vs.h, lng), wtag=f"l3fp{tt}", stack=st,
                          wb=wbp, banks=banksB, w_gate=(P.dx.h, quant_done))
            bank_st.close()
    return




BATCHER8 = [(0, 1), (2, 3), (4, 5), (6, 7), (0, 2), (1, 3), (4, 6), (5, 7),
            (1, 2), (5, 6), (0, 4), (1, 5), (2, 6), (3, 7), (2, 4), (3, 5),
            (1, 2), (3, 4), (5, 6)]
L2F_EXPERTS = list(range(9))
L2F_DEBUG = False


def emit_L2_ffn(P, x_fm, xn_bf, x_gate):
    HH = 2730
    HALF = [(0, 1408), (1408, 2730)]
    WgT = P.din("L2f_WgT", [D, 8])
    rbr = P.din("L2f_rb", [1, 8])
    W1T = P.din("L2f_W1T", [9 * D, HH])
    W2T = P.din("L2f_W2T", [9 * D, HH])
    W3T = P.din("L2f_W3T", [9 * HH, D])

    with contextlib.ExitStack() as st:
        with P.nc.Block() as block:
            dst_ = P.st if L2F_DEBUG else st
            g8 = P.sb("l2f_g8", [128, 128], F32, dst_)
            s8 = P.sb("l2f_s8", [128, 128], F32, dst_)
            w8 = P.sb("l2f_w8", [128, 144], F32, dst_)
            tmp16 = P.sb("l2f_tmp", [128, 16], F32, dst_)
            rec16 = P.sb("l2f_rec", [128, 16], F32, dst_)
            rbs = P.sb("l2f_rbrow", [1, 8], F32, dst_)
            acc = P.sb("l2f_acc", [128, 8 * 1024], F32, dst_)
            wg_sb = P.sb("l2f_wg", [128, DC * 8], BF16, st)
            rb_bc = P.sb("l2f_rbbc", [128, 8], F32, st)
            act = P.sb("l2f_act", [128, 11 * 1024], BF16, st)
            h1pp = P.sb("l2f_h1", [128, 2 * 512], BF16, st)
            w3h = P.sb("l2f_w3", [128, 11 * 1024], BF16, st)
            w12b = [P.sb(f"l2f_w12{i}", [128, DC * 128], BF16, st) for i in range(4)]
            tr_bf = P.sb("l2f_trb", [128, 128], BF16, st)

            @block.gpsimd
            def _(g):
                P.dw.bump(g.dma_start(
                    wg_sb[:].rearrange("p (kc m) -> p kc m", kc=DC),
                    WgT.rearrange("(kc p) m -> p kc m", p=128)), 16)
            w_v = P.dw.n

            xng = emit_ln_fm(P, block, x_fm, xn_bf, x_gate=x_gate)
            xn_rdy = (P.vs.h, xng)

            pst = contextlib.ExitStack()
            gps = P.ps("l2f_gps", [128, 8], F32, pst)
            rbps = P.ps("l2f_rbps", [128, 8], F32, pst)
            hps = [P.ps(f"l2f_hps{i}", [128, 512], F32, pst) for i in range(3)]
            eops = [P.ps(f"l2f_eops{i}", [128, 512], F32, pst) for i in range(2)]
            trps = P.ps("l2f_trps", [128, 128], BF16, pst)

            S = Sched(P)
            ident = P.consts["c_ident"]

            bv = S.add("tensor", lambda t: t.matmul(
                rbps[:], P.consts["c_ones_row_f"][:], rbr, start=True, stop=True),
                waits=[], bump=P.ts)
            S.ops["tensor"].pop()
            P.ts.n -= 1
            dv0 = S.add("gpsimd", lambda g: g.dma_start(rbs[:], rbr), bump=P.dw, k=16)
            bv = S.add("tensor", lambda t: t.matmul(
                rbps[:], P.consts["c_ones_row_f"][:], rbs[:], start=True, stop=True),
                waits=[(P.dw.h, dv0), xn_rdy], bump=P.ts)
            bcv = S.add("scalar", lambda s: s.copy(rb_bc[:], rbps[:]),
                        waits=[(P.ts.h, bv)], bump=P.ss)
            gate_gps = None
            last_gate = None
            for tt in range(16):
                def gmm(t, tt=tt):
                    for kc in range(DC):
                        ins = t.matmul(
                            gps[:],
                            xn_bf[:, kc * T + tt * 128: kc * T + tt * 128 + 128],
                            wg_sb[:, kc * 8:(kc + 1) * 8],
                            start=(kc == 0), stop=(kc == DC - 1))
                    return ins
                tv = S.add("tensor", gmm, waits=[xn_rdy, (P.dw.h, w_v), gate_gps],
                           bump=P.ts)
                gv = S.add("vector", lambda v, tt=tt: v.tensor_tensor(
                    bass.AP(g8, tt, [[128, 128], [16, 8]]),
                    gps[:], rb_bc[:], ALU.add),
                    waits=[(P.ts.h, tv), (P.ss.h, bcv)], bump=P.vs)
                gate_gps = (P.vs.h, gv)
                last_gate = gv

            def vchain(fn):
                pv = S.add("vector", fn, waits=[(P.vs.h, P.vs.n)], bump=P.vs)
                return pv
            S.add("vector", lambda v: v.tensor_copy(s8[:], g8[:]),
                  waits=[(P.vs.h, last_gate)], bump=P.vs)
            for (i, j) in BATCHER8:
                i0, i1 = i * 16, (i + 1) * 16
                j0, j1 = j * 16, (j + 1) * 16
                vchain(lambda v, i0=i0, i1=i1, j0=j0, j1=j1:
                       v.tensor_tensor(tmp16[:], s8[:, i0:i1], s8[:, j0:j1], ALU.min))
                vchain(lambda v, i0=i0, i1=i1, j0=j0, j1=j1:
                       v.tensor_tensor(s8[:, j0:j1], s8[:, i0:i1], s8[:, j0:j1], ALU.max))
                sv = vchain(lambda v, i0=i0, i1=i1: v.tensor_copy(s8[:, i0:i1], tmp16[:]))
            ev = S.add("scalar", lambda s: s.activation(w8[:, 0:128], g8[:], AF.Exp),
                       waits=[(P.vs.h, sv)], bump=P.ss)
            S.add("vector", lambda v: v.tensor_copy(w8[:, 8 * 16:9 * 16],
                                                    s8[:, 4 * 16:5 * 16]),
                  waits=[(P.ss.h, ev)], bump=P.vs)
            for e in range(8):
                vchain(lambda v, e=e: v.tensor_tensor(
                    s8[:, e * 16:(e + 1) * 16], g8[:, e * 16:(e + 1) * 16],
                    w8[:, 8 * 16:9 * 16], ALU.is_ge))
            for e in range(8):
                vchain(lambda v, e=e: v.tensor_tensor(
                    w8[:, e * 16:(e + 1) * 16], w8[:, e * 16:(e + 1) * 16],
                    s8[:, e * 16:(e + 1) * 16], ALU.mult))
            vchain(lambda v: v.tensor_copy(tmp16[:], w8[:, 0:16]))
            for e in range(1, 8):
                vchain(lambda v, e=e: v.tensor_tensor(
                    tmp16[:], tmp16[:], w8[:, e * 16:(e + 1) * 16], ALU.add))
            vchain(lambda v: v.reciprocal(rec16[:], tmp16[:]))
            for e in range(8):
                wv = vchain(lambda v, e=e: v.tensor_tensor(
                    w8[:, e * 16:(e + 1) * 16], w8[:, e * 16:(e + 1) * 16],
                    rec16[:], ALU.mult))
            w_rdy = (P.vs.h, wv)

            gate_acc = None
            gate_act = None
            gate_w3 = None
            gate_h1 = [None, None]
            gate_hps = [None, None, None]
            gate_eops = [None, None]
            gate_w12 = [None, None, None, None]
            gate_tr = None
            xdone = None
            for th_ in range(2):
                tok0 = th_ * 1024
                zv = S.add("vector", lambda v: v.memset(acc[:], 0.0),
                           waits=[gate_acc], bump=P.vs)
                gate_acc = (P.vs.h, zv)
                for e in L2F_EXPERTS:
                    for hf, (hc0, hc1) in enumerate(HALF):
                        nchunks = (hc1 - hc0 + 127) // 128
                        def w3load2(g, e=e, hc0=hc0, hc1=hc1):
                            full = (hc1 - hc0) // 128
                            rem = (hc1 - hc0) - full * 128
                            ins = g.dma_start(
                                w3h[:, 0:full * 1024]
                                .rearrange("p (c m) -> p c m", c=full),
                                W3T[e * HH + hc0: e * HH + hc0 + full * 128, :]
                                .rearrange("(c p) m -> p c m", p=128))
                            if rem == 0:
                                return ins
                            ins.then_inc(P.dw.h, 16)
                            return g.dma_start(
                                w3h[0:rem, full * 1024:(full + 1) * 1024],
                                W3T[e * HH + hc0 + full * 128: e * HH + hc1, :])
                        if (hc1 - hc0) % 128 != 0:
                            P.dw.reserve(16)
                        w3v = S.add("gpsimd", w3load2, waits=[gate_w3], bump=P.dw, k=16)
                        act_last = None
                        for m in range(nchunks):
                            mw = min(128, (hc1 - hc0) - m * 128)
                            par = m % 2
                            def wload(g, e=e, hc0=hc0, m=m, par=par, mw=mw):
                                ins = g.dma_start(
                                    w12b[par][:, 0:DC * mw]
                                    .rearrange("p (kc m) -> p kc m", kc=DC),
                                    W1T[e * D:(e + 1) * D,
                                        hc0 + m * 128: hc0 + m * 128 + mw]
                                    .rearrange("(kc p) m -> p kc m", p=128))
                                ins.then_inc(P.dw.h, 16)
                                return g.dma_start(
                                    w12b[2 + par][:, 0:DC * mw]
                                    .rearrange("p (kc m) -> p kc m", kc=DC),
                                    W2T[e * D:(e + 1) * D,
                                        hc0 + m * 128: hc0 + m * 128 + mw]
                                    .rearrange("(kc p) m -> p kc m", p=128))
                            P.dw.reserve(16)
                            wlv = S.add("gpsimd", wload,
                                        waits=[gate_w12[par]], bump=P.dw, k=16)
                            for nt in range(2):
                                bank = nt % 2
                                def h1mm(t, par=par, nt=nt, bank=bank, tok0=tok0, mw=mw):
                                    for kc in range(DC):
                                        ins = t.matmul(
                                            hps[bank][0:mw, :],
                                            w12b[par][:, kc * mw:(kc + 1) * mw],
                                            xn_bf[:, kc * T + tok0 + nt * 512:
                                                  kc * T + tok0 + nt * 512 + 512],
                                            start=(kc == 0), stop=(kc == DC - 1))
                                    return ins
                                t1 = S.add("tensor", h1mm,
                                           waits=[xn_rdy, (P.dw.h, wlv), gate_hps[bank]],
                                           bump=P.ts)
                                s1 = S.add("scalar", lambda s, bank=bank, mw=mw:
                                           s.activation(h1pp[0:mw, bank * 512:(bank + 1) * 512],
                                                        hps[bank][0:mw, :], AF.Silu),
                                           waits=[(P.ts.h, t1), gate_h1[bank]], bump=P.ss)
                                gate_hps[bank] = (P.ss.h, s1)
                                def h2mm(t, par=par, nt=nt, tok0=tok0, mw=mw):
                                    for kc in range(DC):
                                        ins = t.matmul(
                                            hps[2][0:mw, :],
                                            w12b[2 + par][:, kc * mw:(kc + 1) * mw],
                                            xn_bf[:, kc * T + tok0 + nt * 512:
                                                  kc * T + tok0 + nt * 512 + 512],
                                            start=(kc == 0), stop=(kc == DC - 1))
                                    return ins
                                t2 = S.add("tensor", h2mm,
                                           waits=[gate_hps[2]], bump=P.ts)
                                a1 = S.add("vector", lambda v, m=m, nt=nt, bank=bank, mw=mw:
                                           v.tensor_tensor(
                                               act[0:mw, m * 1024 + nt * 512:
                                                   m * 1024 + nt * 512 + 512],
                                               h1pp[0:mw, bank * 512:(bank + 1) * 512],
                                               hps[2][0:mw, :], ALU.mult),
                                           waits=[(P.ts.h, t2), (P.ss.h, s1), gate_act],
                                           bump=P.vs)
                                gate_hps[2] = (P.vs.h, a1)
                                gate_h1[bank] = (P.vs.h, a1)
                                act_last = a1
                            gate_w12[par] = (P.ts.h, P.ts.n)
                        eo_last = None
                        for tt8 in range(8):
                            gtt = th_ * 8 + tt8
                            for fh in range(2):
                                bank = fh % 2
                                def eomm(t, tt8=tt8, fh=fh, bank=bank, nchunks=nchunks,
                                         hc0=hc0, hc1=hc1):
                                    for c in range(nchunks):
                                        cw = min(128, (hc1 - hc0) - c * 128)
                                        ins = t.matmul(
                                            eops[bank][:],
                                            act[0:cw, c * 1024 + tt8 * 128:
                                                c * 1024 + tt8 * 128 + 128],
                                            w3h[0:cw, c * 1024 + fh * 512:
                                                c * 1024 + fh * 512 + 512],
                                            start=(c == 0), stop=(c == nchunks - 1))
                                    return ins
                                te = S.add("tensor", eomm,
                                           waits=[(P.vs.h, act_last), (P.dw.h, w3v),
                                                  gate_eops[bank]],
                                           bump=P.ts)
                                if e < 8:
                                    ae = S.add("vector", lambda v, e=e, gtt=gtt, tt8=tt8,
                                               fh=fh, bank=bank:
                                               v.scalar_tensor_tensor(
                                                   acc[:, tt8 * 1024 + fh * 512:
                                                       tt8 * 1024 + fh * 512 + 512],
                                                   eops[bank][:], w8[:, e * 16 + gtt:
                                                                     e * 16 + gtt + 1],
                                                   acc[:, tt8 * 1024 + fh * 512:
                                                       tt8 * 1024 + fh * 512 + 512],
                                                   ALU.mult, ALU.add),
                                               waits=[(P.ts.h, te), w_rdy, gate_acc],
                                               bump=P.vs)
                                else:
                                    ae = S.add("vector", lambda v, tt8=tt8, fh=fh, bank=bank:
                                               v.tensor_tensor(
                                                   acc[:, tt8 * 1024 + fh * 512:
                                                       tt8 * 1024 + fh * 512 + 512],
                                                   acc[:, tt8 * 1024 + fh * 512:
                                                       tt8 * 1024 + fh * 512 + 512],
                                                   eops[bank][:], ALU.add),
                                               waits=[(P.ts.h, te), gate_acc], bump=P.vs)
                                gate_eops[bank] = (P.vs.h, ae)
                                eo_last = ae
                        gate_act = (P.ts.h, P.ts.n)
                        gate_w3 = (P.ts.h, P.ts.n)
                for tt8 in range(8):
                    for fc in range(8):
                        cv = S.add("vector", lambda v, tt8=tt8, fc=fc: v.tensor_copy(
                            tr_bf[:], acc[:, tt8 * 1024 + fc * 128: tt8 * 1024 + fc * 128 + 128]),
                            waits=[gate_eops[0], gate_eops[1], gate_tr, gate_acc], bump=P.vs)
                        tv = S.add("tensor", lambda t: t.transpose(
                            trps[:], tr_bf[:], ident[:]),
                            waits=[(P.vs.h, cv)], bump=P.ts)
                        av = S.add("vector", lambda v, tt8=tt8, fc=fc, tok0=tok0:
                                   v.tensor_tensor(
                                       x_fm[:, fc * T + tok0 + tt8 * 128:
                                            fc * T + tok0 + tt8 * 128 + 128],
                                       x_fm[:, fc * T + tok0 + tt8 * 128:
                                            fc * T + tok0 + tt8 * 128 + 128],
                                       trps[:], ALU.add),
                                   waits=[(P.ts.h, tv)], bump=P.vs)
                        gate_tr = (P.vs.h, av)
                        xdone = av
                gate_acc = (P.vs.h, xdone)
            S.emit(block)
            pst.close()
            if L2F_DEBUG:
                P.tap("g8", g8[:])
                P.tap("w8", w8[:])
                P.tap("acc", acc[:])
    return




def emit_rms_fm(P, block, src_bf, dst_bf, w_col=None, n_feat=D, tokens=T,
                gate=None, eps=1e-6):
    FC = n_feat // 128
    width = min(512, tokens)
    with contextlib.ExitStack() as own:
        _LN_CTR[0] += 1
        tag = f"_r{_LN_CTR[0]}"
        pp = P.sb("rmspp" + tag, [128, 1024], BF16, own)
        row = P.sb("rmsrow" + tag, [1, width], F32, own)
        rowb = P.sb("rmsrowb" + tag, [1, width], BF16, own)
        ab = P.sb("rmsab" + tag, [128, width], BF16, own)
        psB = P.ps("rmspsB" + tag, [1, 512], F32, own)
        bcA = P.ps("rmsbcA" + tag, [128, 512], F32, own)
        for off in range(0, tokens, width):
            NTl = width // 512
            ts0 = P.ts.n
            ss0 = P.ss.n

            @block.scalar
            def _(s):
                if gate is not None:
                    s.wait_ge(gate[0], gate[1])
                for nt in range(NTl):
                    for kc in range(FC):
                        gi = nt * FC + kc
                        if gi >= 2:
                            s.wait_ge(P.ts.h, ts0 + gi - 1)
                        ins = s.activation(
                            pp[:, (gi % 2) * 512:(gi % 2) * 512 + 512],
                            src_bf[:, kc * tokens + off + nt * 512:
                                   kc * tokens + off + nt * 512 + 512], AF.Square)
                        P.ss.bump(ins, 1)
                    s.wait_ge(P.ts.h, ts0 + (nt + 1) * FC)
                    ins = s.copy(row[0:1, nt * 512:(nt + 1) * 512], psB[:])
                    P.ss.bump(ins, 1)

            @block.tensor
            def _(t):
                if gate is not None:
                    t.wait_ge(gate[0], gate[1])
                for nt in range(NTl):
                    for kc in range(FC):
                        gi = nt * FC + kc
                        t.wait_ge(P.ss.h, ss0 + nt * (FC + 1) + kc + 1)
                        ins = t.matmul(psB[:], P.consts["c_ones_col"][:],
                                       pp[:, (gi % 2) * 512:(gi % 2) * 512 + 512],
                                       start=(kc == 0), stop=(kc == FC - 1))
                        P.ts.bump(ins, 1)

            @block.scalar
            def _(s):
                s.mul(row[0:1, :], row[0:1, :], 1.0 / n_feat)
                s.add(row[0:1, :], row[0:1, :], eps)
                s.activation(row[0:1, :], row[0:1, :], AF.Ln)
                ins = s.activation(rowb[0:1, :], row[0:1, :], AF.Exp, scale=-0.5)
                P.ss.bump(ins, 1)
            s_rstd = P.ss.n

            tb0 = P.ts.n
            sb0 = P.ss.n

            @block.tensor
            def _(t):
                t.wait_ge(P.ss.h, s_rstd)
                for nt in range(NTl):
                    if nt >= 1:
                        t.wait_ge(P.ss.h, sb0 + nt)
                    ins = t.matmul(bcA[:], P.consts["c_ones_row"][:],
                                   rowb[0:1, nt * 512:(nt + 1) * 512],
                                   start=True, stop=True)
                    P.ts.bump(ins, 1)

            @block.scalar
            def _(s):
                for nt in range(NTl):
                    s.wait_ge(P.ts.h, tb0 + nt + 1)
                    ins = s.copy(ab[:, nt * 512:(nt + 1) * 512], bcA[:])
                    P.ss.bump(ins, 1)
            s_ab = P.ss.n

            @block.vector
            def _(v):
                v.wait_ge(P.ss.h, s_ab)
                for kc in range(FC):
                    sl = slice(kc * tokens + off, kc * tokens + off + width)
                    if w_col is not None:
                        v.tensor_tensor(dst_bf[:, sl], src_bf[:, sl], ab[:, 0:width],
                                        ALU.mult)
                        ins = v.tensor_scalar(dst_bf[:, sl], dst_bf[:, sl],
                                              w_col[:, kc:kc + 1], None, ALU.mult)
                    else:
                        ins = v.tensor_tensor(dst_bf[:, sl], src_bf[:, sl],
                                              ab[:, 0:width], ALU.mult)
                P.vs.bump(ins, 1)
    return P.vs.n



def emit_L1_attn(P, x_fm, xn_bf, x_gate):
    DK = 512
    DV = 1024
    dk = 128
    dv = 256
    NH = 4
    NCH = 32
    II = 2816
    IC = II // 128
    Wq = P.din("L1a_Wq", [D, DK])
    Wk = P.din("L1a_Wk", [D, DK])
    Wv = P.din("L1a_Wv", [D, DV])
    Wg = P.din("L1a_Wg", [D, DV])
    Wgk1 = P.din("L1a_Wgk1", [D, 16])
    Wgk2 = P.din("L1a_Wgk2", [16, DK])
    bgk2r = P.din("L1a_bgk2", [1, DK])
    anw = P.din("L1a_anw", [128, DC])
    gnwr = P.din("L1a_gnw", [1, dv])
    Wo = P.din("L1a_Wo", [DV, D])
    mnw = P.din("L1a_mnw", [128, DC])
    Wgate = P.din("L1a_Wgate", [D, II])
    Wup = P.din("L1a_Wup", [D, II])
    Wdn = P.din("L1a_Wdn", [II, D])

    with contextlib.ExitStack() as st:
        with P.nc.Block() as block:
            h_bf = P.sb("l1a_h", [128, DC * T], BF16, st)

            anw_sb = P.sb("l1a_anw", [128, DC], F32, st)
            mnw_sb = P.sb("l1a_mnw", [128, DC], F32, st)
            gnb = P.sb("l1a_gnb", [128, dv], BF16, st)
            bgk2b = P.sb("l1a_bgk2b", [128, DK], BF16, st)
            wgk1_sb = P.sb("l1a_wgk1", [128, DC * 16], BF16, st)
            wgk2_sb = P.sb("l1a_wgk2", [16, DK], BF16, st)
            gk_row = P.sb("l1a_gkrow", [1, dv], BF16, st)
            t1_bf = P.sb("l1a_t1", [16, T], BF16, st)
            zf = P.sb("l1a_zf", [128, 128], F32, st)
            zf2 = P.sb("l1a_zf2", [128, 2 * 128], F32, st)
            wbuf = [P.sb(f"l1a_wb{i}", [128, DC * 256], BF16, st) for i in range(2)]
            gla_st = contextlib.ExitStack()
            z_bf = P.sb("l1a_z", [128, 16 * dk], BF16, gla_st)
            q_fm = P.sb("l1a_q", [128, T], BF16, gla_st)
            k_fm = P.sb("l1a_k", [128, T], BF16, gla_st)
            v_tm = P.sb("l1a_v", [128, 16 * dv], BF16, gla_st)
            g_tm = P.sb("l1a_g", [128, 16 * dv], BF16, gla_st)
            o_tm = P.sb("l1a_o", [128, 16 * dv], BF16, gla_st)
            b_sb = P.sb("l1a_b", [128, 128], F32, gla_st)
            eb_sb = P.sb("l1a_eb", [128, 128], BF16, gla_st)
            enb_sb = P.sb("l1a_enb", [128, 128], BF16, gla_st)
            e2_sb = P.sb("l1a_e2", [128, 128], BF16, gla_st)
            qe_sb = P.sb("l1a_qe", [128, 128], BF16, gla_st)
            ke_sb = P.sb("l1a_ke", [128, 128], BF16, gla_st)
            kde_sb = P.sb("l1a_kde", [128, 128], BF16, gla_st)
            At_sb = P.sb("l1a_At", [128, 64], BF16, gla_st)
            kd_sb = P.sb("l1a_kd", [128, 128], BF16, gla_st)
            S_sb = P.sb("l1a_S", [128, dv], F32, gla_st)
            S_bf = P.sb("l1a_Sbf", [128, dv], BF16, gla_st)
            ebl = P.sb("l1a_ebl", [128, 2], F32, gla_st)
            rstd_o = P.sb("l1a_ro", [128, 1], F32, gla_st)
            sqo = P.sb("l1a_sqo", [128, dv], BF16, gla_st)
            of_bf = P.sb("l1a_of", [128, 2 * T], BF16, gla_st)

            @block.gpsimd
            def _(g):
                P.dw.bump(g.dma_start(anw_sb[:], anw), 16)
                P.dw.bump(g.dma_start(mnw_sb[:], mnw), 16)
                P.dw.bump(g.dma_start(bgk2b[0:1, :], bgk2r), 16)
                P.dw.bump(g.dma_start(gk_row[:], gnwr), 16)
                P.dw.bump(g.dma_start(
                    wgk1_sb[:].rearrange("p (kc m) -> p kc m", kc=DC),
                    Wgk1.rearrange("(kc p) m -> p kc m", p=128)), 16)
                P.dw.bump(g.dma_start(wgk2_sb[:], Wgk2), 16)
            w_v = P.dw.n

            xng = emit_ln_fm(P, block, x_fm, xn_bf, x_gate=x_gate)
            hg = emit_rms_fm(P, block, xn_bf, h_bf, w_col=anw_sb,
                             gate=(P.vs.h, xng), eps=1e-6)
            h_rdy = (P.vs.h, hg)

            pst = contextlib.ExitStack()
            ps1 = P.ps("l1a_ps1", [128, 512], F32, pst)
            ps2 = P.ps("l1a_ps2", [128, 512], F32, pst)
            bps = P.ps("l1a_bps", [128, 128], F32, pst)
            bfps = P.ps("l1a_bfps", [128, 128], F32, pst)
            ops_ = [P.ps(f"l1a_ops{i}", [64, dv], F32, pst) for i in range(2)]
            kdps = P.ps("l1a_kdps", [128, 128], BF16, pst)
            Sps = P.ps("l1a_Sps", [128, dv], F32, pst)

            S = Sched(P)
            onesb = P.consts["c_ones_row"]
            onesf = P.consts["c_ones_row_f"]
            ident = P.consts["c_ident"]
            triu2 = P.consts["c_triu2"]
            trium = P.consts["c_trium"]
            lnsc = P.consts["c_lnscale"]

            def vchain(fn, extra=()):
                return S.add("vector", fn, waits=[(P.vs.h, P.vs.n)] + list(extra),
                             bump=P.vs)

            def schain(fn, extra=()):
                return S.add("scalar", fn, waits=[(P.ss.h, P.ss.n)] + list(extra),
                             bump=P.ss)

            gate_ps = [None, None]
            for half in range(2):
                tv = S.add("tensor", lambda t, half=half: t.matmul(
                    ps1[:, 0:256], onesb[:], bgk2b[0:1, half * 256:(half + 1) * 256],
                    start=True, stop=True),
                    waits=[(P.dw.h, w_v), h_rdy, gate_ps[0]], bump=P.ts)
                sv = S.add("scalar", lambda s, half=half: s.copy(
                    bgk2b[:, half * 256:(half + 1) * 256], ps1[:, 0:256]),
                    waits=[(P.ts.h, tv)], bump=P.ss)
                gate_ps[0] = (P.ss.h, sv)
            tv = S.add("tensor", lambda t: t.matmul(
                ps1[:, 0:256], onesb[:], gk_row[0:1, :], start=True, stop=True),
                waits=[gate_ps[0]], bump=P.ts)
            sv = S.add("scalar", lambda s: s.copy(gnb[:], ps1[:, 0:dv]),
                       waits=[(P.ts.h, tv)], bump=P.ss)
            gate_ps[0] = (P.ss.h, sv)

            for nt in range(NT):
                def t1mm(t, nt=nt):
                    for kc in range(DC):
                        ins = t.matmul(ps1[0:16, :],
                                       wgk1_sb[:, kc * 16:(kc + 1) * 16],
                                       h_bf[:, kc * T + nt * 512: kc * T + nt * 512 + 512],
                                       start=(kc == 0), stop=(kc == DC - 1))
                    return ins
                tv = S.add("tensor", t1mm, waits=[(P.dw.h, w_v), h_rdy, gate_ps[0]],
                           bump=P.ts)
                sv = S.add("scalar", lambda s, nt=nt: s.copy(
                    t1_bf[:, nt * 512:(nt + 1) * 512], ps1[0:16, :]),
                    waits=[(P.ts.h, tv)], bump=P.ss)
                gate_ps[0] = (P.ss.h, sv)

            x2g = None
            for hh in range(NH):
                par = hh % 2
                for tt in range(16):
                    tv = S.add("tensor", lambda t, tt=tt, hh=hh: t.matmul(
                        ps2[:, 0:dk], t1_bf[0:16, tt * 128:(tt + 1) * 128],
                        wgk2_sb[:, hh * dk:(hh + 1) * dk],
                        start=True, stop=True),
                        waits=[(P.ss.h, P.ss.n), gate_ps[1]], bump=P.ts)
                    v1 = S.add("vector", lambda v, tt=tt, hh=hh: v.tensor_tensor(
                        zf[:], ps2[:, 0:dk], bgk2b[:, hh * dk:(hh + 1) * dk], ALU.add),
                        waits=[(P.ts.h, tv), (P.vs.h, P.vs.n)], bump=P.vs)
                    gate_ps[1] = (P.vs.h, v1)
                    s1 = schain(lambda s: s.activation(zf2[:, 0:dk], zf[:], AF.Abs),
                                extra=[(P.vs.h, v1)])
                    s2 = schain(lambda s: s.activation(zf2[:, 0:dk], zf2[:, 0:dk],
                                                       AF.Exp, scale=-1.0))
                    s3 = schain(lambda s: s.activation(zf2[:, 0:dk], zf2[:, 0:dk],
                                                       AF.Ln, bias=P.consts["c_p1"][:]))
                    v2 = vchain(lambda v: v.tensor_single_scalar(zf2[:, dk:2 * dk], zf[:],
                                                                 0.0, ALU.min),
                                extra=[(P.ss.h, s1)])
                    v3 = vchain(lambda v: v.tensor_tensor(zf2[:, dk:2 * dk],
                                                          zf2[:, dk:2 * dk],
                                                          zf2[:, 0:dk], ALU.subtract),
                                extra=[(P.ss.h, s3)])
                    s4 = schain(lambda s, tt=tt: s.mul(z_bf[:, tt * dk:(tt + 1) * dk],
                                                       zf2[:, dk:2 * dk], 1.0 / 16.0),
                                extra=[(P.vs.h, v3)])
                    gate_ps[1] = (P.ss.h, s4)
                def whload(g, hh=hh, par=par):
                    ins = g.dma_start(
                        wbuf[par][:, 0:DC * 128].rearrange("p (kc m) -> p kc m", kc=DC),
                        Wq[:, hh * dk:(hh + 1) * dk]
                        .rearrange("(kc p) m -> p kc m", p=128))
                    ins.then_inc(P.dw.h, 16)
                    return g.dma_start(
                        wbuf[1 - par][:, 0:DC * 128].rearrange("p (kc m) -> p kc m", kc=DC),
                        Wk[:, hh * dk:(hh + 1) * dk]
                        .rearrange("(kc p) m -> p kc m", p=128))
                P.dw.reserve(16)
                dv_ = S.add("gpsimd", whload, waits=[(P.ts.h, P.ts.n)], bump=P.dw, k=16)
                for which, wb_, dst in (("q", par, q_fm), ("k", 1 - par, k_fm)):
                    for nt in range(NT):
                        bank = [ps1, ps2][nt % 2]
                        def qmm(t, wb_=wb_, nt=nt, bank=bank):
                            for kc in range(DC):
                                ins = t.matmul(
                                    bank[:],
                                    wbuf[wb_][:, kc * 128:(kc + 1) * 128],
                                    h_bf[:, kc * T + nt * 512: kc * T + nt * 512 + 512],
                                    start=(kc == 0), stop=(kc == DC - 1))
                            return ins
                        tv = S.add("tensor", qmm,
                                   waits=[(P.dw.h, dv_), gate_ps[nt % 2]], bump=P.ts)
                        sv = S.add("scalar", lambda s, dst=dst, nt=nt, bank=bank:
                                   s.copy(dst[:, nt * 512:(nt + 1) * 512], bank[:]),
                                   waits=[(P.ts.h, tv)], bump=P.ss)
                        gate_ps[nt % 2] = (P.ss.h, sv)
                def wvload(g, hh=hh, par=par):
                    ins = g.dma_start(
                        wbuf[par][:, 0:DC * dv].rearrange("p (kc m) -> p kc m", kc=DC),
                        Wv[:, hh * dv:(hh + 1) * dv]
                        .rearrange("(kc p) m -> p kc m", p=128))
                    ins.then_inc(P.dw.h, 16)
                    return g.dma_start(
                        wbuf[1 - par][:, 0:DC * dv].rearrange("p (kc m) -> p kc m", kc=DC),
                        Wg[:, hh * dv:(hh + 1) * dv]
                        .rearrange("(kc p) m -> p kc m", p=128))
                P.dw.reserve(16)
                dv2 = S.add("gpsimd", wvload, waits=[(P.ts.h, P.ts.n)], bump=P.dw, k=16)
                for which, wb_, dst in (("v", par, v_tm), ("g", 1 - par, g_tm)):
                    for tt in range(16):
                        bank = [ps1, ps2][tt % 2]
                        def vmm(t, wb_=wb_, tt=tt, bank=bank):
                            for kc in range(DC):
                                ins = t.matmul(
                                    bank[0:128, 0:dv],
                                    h_bf[:, kc * T + tt * 128: kc * T + tt * 128 + 128],
                                    wbuf[wb_][:, kc * dv:(kc + 1) * dv],
                                    start=(kc == 0), stop=(kc == DC - 1))
                            return ins
                        tv = S.add("tensor", vmm,
                                   waits=[(P.dw.h, dv2), gate_ps[tt % 2]], bump=P.ts)
                        sv = S.add("scalar", lambda s, dst=dst, tt=tt, bank=bank:
                                   s.copy(dst[:, tt * dv:(tt + 1) * dv], bank[0:128, 0:dv]),
                                   waits=[(P.ts.h, tv)], bump=P.ss)
                        gate_ps[tt % 2] = (P.ss.h, sv)
                qkv_done = (P.ss.h, P.ss.n)

                sz = vchain(lambda v: v.memset(S_sb[:], 0.0), extra=[qkv_done])
                sz2 = vchain(lambda v: v.memset(S_bf[:], 0.0))
                gate_b = None
                gate_Atps = None
                gate_o = [None, None]
                gate_kdps = None
                gate_Sps = None
                for cp in range(16):
                    for ci in range(2):
                        c = 2 * cp + ci
                        base = (c % 2) * 64
                        tv = S.add("tensor", lambda t, cp=cp, ci=ci, base=base, hh=hh:
                                   t.matmul(bps[base:base + 64, :],
                                            triu2[base:base + 64, :],
                                            z_bf[base:base + 64,
                                                 cp * dk: cp * dk + dk],
                                            start=True, stop=True),
                                   waits=[qkv_done, gate_b] if gate_b else [qkv_done],
                                   bump=P.ts)
                    sv = S.add("scalar", lambda s: s.copy(b_sb[:], bps[:]),
                               waits=[(P.ts.h, tv)], bump=P.ss)
                    tv2 = S.add("tensor", lambda t: t.transpose(
                        bfps[:], b_sb[:], P.consts["c_identf"][:]),
                        waits=[(P.ss.h, sv), gate_Atps], bump=P.ts)
                    sv2 = S.add("scalar", lambda s: s.copy(b_sb[:], bfps[:]),
                                waits=[(P.ts.h, tv2)], bump=P.ss)
                    gate_b = (P.ss.h, sv2)
                    s_eb = schain(lambda s: s.activation(eb_sb[:], b_sb[:], AF.Exp,
                                                         bias=lnsc[:]),
                                  extra=[(P.ss.h, sv2)])
                    s_enb = schain(lambda s: s.activation(enb_sb[:], b_sb[:], AF.Exp,
                                                          scale=-1.0))
                    s_e2a = schain(lambda s: s.activation(e2_sb[:, 0:64], b_sb[:, 0:64],
                                                          AF.Exp, scale=-1.0,
                                                          bias=b_sb[:, 63:64]))
                    s_e2b = schain(lambda s: s.activation(e2_sb[:, 64:128], b_sb[:, 64:128],
                                                          AF.Exp, scale=-1.0,
                                                          bias=b_sb[:, 127:128]))
                    s_ebl = schain(lambda s: s.activation(ebl[:, 0:2], b_sb[:, 63:128:64],
                                                          AF.Exp))
                    eb_rdy = (P.ss.h, s_ebl)
                    v_qe = vchain(lambda v, cp=cp: v.tensor_tensor(
                        qe_sb[:], q_fm[:, cp * 128:(cp + 1) * 128], eb_sb[:], ALU.mult),
                        extra=[eb_rdy, gate_Atps])
                    v_ke = vchain(lambda v, cp=cp: v.tensor_tensor(
                        ke_sb[:], k_fm[:, cp * 128:(cp + 1) * 128], enb_sb[:], ALU.mult))
                    v_kde = vchain(lambda v, cp=cp: v.tensor_tensor(
                        kde_sb[:], k_fm[:, cp * 128:(cp + 1) * 128], e2_sb[:], ALU.mult))
                    for ci in range(2):
                        c = 2 * cp + ci
                        base = (c % 2) * 64
                        ob = c % 2
                        tva = S.add("tensor", lambda t, ci=ci:
                                    t.matmul(bps[0:64, 0:64],
                                             ke_sb[:, ci * 64:(ci + 1) * 64],
                                             qe_sb[:, ci * 64:(ci + 1) * 64],
                                             start=True, stop=True),
                                    waits=[(P.vs.h, v_kde), gate_b], bump=P.ts)
                        vmask = S.add("vector", lambda v, base=base: v.tensor_tensor(
                            At_sb[base:base + 64, :], bps[0:64, 0:64], trium[:], ALU.mult),
                            waits=[(P.ts.h, tva), (P.vs.h, P.vs.n)], bump=P.vs)
                        gate_b = (P.vs.h, vmask)
                        tvo1 = S.add("tensor", lambda t, base=base, cp=cp, ob=ob:
                                     t.matmul(ops_[ob][:],
                                              At_sb[base:base + 64, :],
                                              v_tm[base:base + 64, cp * dv: cp * dv + dv],
                                              start=True, stop=False),
                                     waits=[(P.vs.h, vmask), gate_o[ob]], bump=P.ts)
                        tvo2 = S.add("tensor", lambda t, ci=ci, ob=ob:
                                     t.matmul(ops_[ob][:],
                                              qe_sb[:, ci * 64:(ci + 1) * 64],
                                              S_bf[:],
                                              start=False, stop=True),
                                     waits=[(P.vs.h, P.vs.n)], bump=P.ts)
                        vo = S.add("vector", lambda v, base=base, cp=cp, ci=ci, ob=ob:
                                   v.tensor_copy(
                                       o_tm[base:base + 64,
                                            (2 * cp + ci) // 2 * dv:
                                            (2 * cp + ci) // 2 * dv + dv],
                                       ops_[ob][:]),
                                   waits=[(P.ts.h, tvo2), (P.vs.h, P.vs.n)], bump=P.vs)
                        gate_o[ob] = (P.vs.h, vo)
                        tkd = S.add("tensor", lambda t, ci=ci:
                                    t.transpose(kdps[0:64, 0:128],
                                                kde_sb[:, ci * 64:(ci + 1) * 64],
                                                ident[:]),
                                    waits=[(P.vs.h, vo), gate_kdps], bump=P.ts)
                        skd = S.add("scalar", lambda s, base=base: s.copy(
                            kd_sb[base:base + 64, :], kdps[0:64, 0:128]),
                            waits=[(P.ts.h, tkd), (P.ss.h, P.ss.n)], bump=P.ss)
                        gate_kdps = (P.ss.h, skd)
                        tS = S.add("tensor", lambda t, base=base, cp=cp, ci=ci:
                                   t.matmul(Sps[:],
                                            kd_sb[base:base + 64, :],
                                            v_tm[base:base + 64,
                                                 (2 * cp + ci) // 2 * dv:
                                                 (2 * cp + ci) // 2 * dv + dv],
                                            start=True, stop=True),
                                   waits=[(P.ss.h, skd), gate_Sps], bump=P.ts)
                        vS1 = vchain(lambda v, ci=ci: v.tensor_scalar(
                            S_sb[:], S_sb[:], ebl[:, ci:ci + 1], None, ALU.mult),
                            extra=[(P.ts.h, tS), (P.ss.h, s_ebl)])
                        vS2 = vchain(lambda v: v.tensor_tensor(S_sb[:], S_sb[:], Sps[:],
                                                               ALU.add))
                        vS3 = vchain(lambda v: v.tensor_copy(S_bf[:], S_sb[:]))
                        gate_Sps = (P.vs.h, vS3)
                    gate_Atps = (P.ts.h, P.ts.n)
                og_done = None
                for tt in range(16):
                    s_sq = schain(lambda s, tt=tt: s.activation(
                        sqo[:], o_tm[:, tt * dv:(tt + 1) * dv], AF.Square),
                        extra=[gate_o[0], gate_o[1]])
                    v_r = vchain(lambda v: v.reduce_sum(rstd_o[:], sqo[:], axis=AX.X),
                                 extra=[(P.ss.h, s_sq)])
                    s_r1 = schain(lambda s: s.mul(rstd_o[:], rstd_o[:], 1.0 / dv),
                                  extra=[(P.vs.h, v_r)])
                    s_r2 = schain(lambda s: s.add(rstd_o[:], rstd_o[:], 1e-6))
                    s_r3 = schain(lambda s: s.activation(rstd_o[:], rstd_o[:], AF.Ln))
                    s_r4 = schain(lambda s: s.activation(rstd_o[:], rstd_o[:], AF.Exp,
                                                         scale=-0.5))
                    v_o1 = vchain(lambda v, tt=tt: v.tensor_scalar(
                        o_tm[:, tt * dv:(tt + 1) * dv], o_tm[:, tt * dv:(tt + 1) * dv],
                        rstd_o[:], None, ALU.mult), extra=[(P.ss.h, s_r4)])
                    v_o2 = vchain(lambda v, tt=tt: v.tensor_tensor(
                        o_tm[:, tt * dv:(tt + 1) * dv], o_tm[:, tt * dv:(tt + 1) * dv],
                        gnb[:], ALU.mult))
                    s_g = schain(lambda s, tt=tt: s.activation(
                        sqo[:], g_tm[:, tt * dv:(tt + 1) * dv], AF.Silu),
                        extra=[(P.vs.h, v_o1)])
                    v_o3 = vchain(lambda v, tt=tt: v.tensor_tensor(
                        o_tm[:, tt * dv:(tt + 1) * dv], o_tm[:, tt * dv:(tt + 1) * dv],
                        sqo[:], ALU.mult), extra=[(P.ss.h, s_g)])
                    for half in range(2):
                        ttv = S.add("tensor", lambda t, tt=tt, half=half: t.transpose(
                            kdps[:, 0:128],
                            o_tm[:, tt * dv + half * 128: tt * dv + half * 128 + 128],
                            ident[:]),
                            waits=[(P.vs.h, v_o3), gate_kdps], bump=P.ts)
                        svv = S.add("scalar", lambda s, tt=tt, half=half: s.copy(
                            of_bf[:, half * T + tt * 128: half * T + tt * 128 + 128],
                            kdps[:, 0:128]),
                            waits=[(P.ts.h, ttv), (P.ss.h, P.ss.n)], bump=P.ss)
                        gate_kdps = (P.ss.h, svv)
                    og_done = (P.ss.h, P.ss.n)
                if hh == 0:
                    pass
                for m in range(DC):
                    par2 = m % 2
                    dmw = S.add("gpsimd", lambda g, hh=hh, m=m, par2=par2: g.dma_start(
                        wbuf[par2][:, 0:2 * 128].rearrange("p (c m) -> p c m", c=2),
                        Wo[hh * dv:(hh + 1) * dv, m * 128:(m + 1) * 128]
                        .rearrange("(c p) m -> p c m", p=128)),
                        waits=[(P.ts.h, P.ts.n)], bump=P.dw, k=16)
                    for nt in range(NT):
                        bank = [ps1, ps2][nt % 2]
                        def womm(t, par2=par2, nt=nt, bank=bank):
                            for c in range(2):
                                ins = t.matmul(
                                    bank[:],
                                    wbuf[par2][:, c * 128:(c + 1) * 128],
                                    of_bf[:, c * T + nt * 512: c * T + nt * 512 + 512],
                                    start=(c == 0), stop=(c == 1))
                            return ins
                        tv = S.add("tensor", womm,
                                   waits=[(P.dw.h, dmw), og_done, gate_ps[nt % 2]],
                                   bump=P.ts)
                        vv = S.add("vector", lambda v, m=m, nt=nt, bank=bank:
                                   v.tensor_tensor(
                                       xn_bf[:, m * T + nt * 512: m * T + nt * 512 + 512],
                                       xn_bf[:, m * T + nt * 512: m * T + nt * 512 + 512],
                                       bank[:], ALU.add),
                                   waits=[(P.ts.h, tv), (P.vs.h, P.vs.n)], bump=P.vs)
                        gate_ps[nt % 2] = (P.vs.h, vv)
                x2g = (P.vs.h, P.vs.n)
            S.emit(block)
            pst.close()

            S2 = Sched(P)
            for kc in range(DC):
                vv = S2.add("vector", lambda v, kc=kc: v.tensor_tensor(
                    x_fm[:, kc * T:(kc + 1) * T], x_fm[:, kc * T:(kc + 1) * T],
                    xn_bf[:, kc * T:(kc + 1) * T], ALU.add),
                    waits=[x2g, (P.vs.h, P.vs.n)], bump=P.vs)
            S2.emit(block)
            x_up = (P.vs.h, P.vs.n)

            gla_st.close()
            g1_sb = P.sb("l1a_g1", [128, IC * 512], BF16, st)
            pr_sb = P.sb("l1a_pr", [128, IC * 512], BF16, st)
            wdb = [P.sb(f"l1a_wd{i}", [128, IC * 128], BF16, st) for i in range(2)]
            ng = emit_rms_fm(P, block, xn_bf, h_bf, w_col=mnw_sb, gate=x_up, eps=1e-6)
            n_rdy = (P.vs.h, ng)
            S3 = Sched(P)
            pst2 = contextlib.ExitStack()
            mps = [P.ps(f"l1a_mps{i}", [128, 512], F32, pst2) for i in range(2)]
            dps = [P.ps(f"l1a_dps{i}", [128, 512], F32, pst2) for i in range(2)]
            gate_mps = [None, None]
            gate_dps = [None, None]
            gate_g1 = None
            gate_wd = [None, None]
            for tt in range(NT):
                prod_done = None
                for m in range(IC):
                    par = m % 2
                    def wgl(g, m=m, par=par, tt=tt):
                        ins = g.dma_start(
                            wbuf[par][:, 0:DC * 128].rearrange("p (kc m) -> p kc m", kc=DC),
                            Wgate[:, m * 128:(m + 1) * 128]
                            .rearrange("(kc p) m -> p kc m", p=128))
                        ins.then_inc(P.dw.h, 16)
                        return g.dma_start(
                            wdb[par][:, 0:DC * 128].rearrange("p (kc m) -> p kc m", kc=DC),
                            Wup[:, m * 128:(m + 1) * 128]
                            .rearrange("(kc p) m -> p kc m", p=128))
                    P.dw.reserve(16)
                    dmv = S3.add("gpsimd", wgl,
                                 waits=[n_rdy, (P.ts.h, gate_wd[par][1])
                                        if gate_wd[par] else None],
                                 bump=P.dw, k=16)
                    b0 = m % 2
                    def gmm(t, par=par, tt=tt, b0=b0):
                        for kc in range(DC):
                            ins = t.matmul(
                                mps[b0][:],
                                wbuf[par][:, kc * 128:(kc + 1) * 128],
                                h_bf[:, kc * T + tt * 512: kc * T + tt * 512 + 512],
                                start=(kc == 0), stop=(kc == DC - 1))
                        return ins
                    t1 = S3.add("tensor", gmm,
                                waits=[n_rdy, (P.dw.h, dmv), gate_mps[b0]], bump=P.ts)
                    s1 = S3.add("scalar", lambda s, m=m, b0=b0: s.activation(
                        g1_sb[:, m * 512:(m + 1) * 512], mps[b0][:], AF.Silu),
                        waits=[(P.ts.h, t1), (P.ss.h, P.ss.n), gate_g1], bump=P.ss)
                    def umm(t, par=par, tt=tt, b0=b0):
                        for kc in range(DC):
                            ins = t.matmul(
                                mps[b0][:],
                                wdb[par][:, kc * 128:(kc + 1) * 128],
                                h_bf[:, kc * T + tt * 512: kc * T + tt * 512 + 512],
                                start=(kc == 0), stop=(kc == DC - 1))
                        return ins
                    t2 = S3.add("tensor", umm, waits=[(P.ss.h, s1)], bump=P.ts)
                    v1 = S3.add("vector", lambda v, m=m, b0=b0: v.tensor_tensor(
                        pr_sb[:, m * 512:(m + 1) * 512],
                        g1_sb[:, m * 512:(m + 1) * 512], mps[b0][:], ALU.mult),
                        waits=[(P.ts.h, t2), (P.vs.h, P.vs.n)], bump=P.vs)
                    gate_mps[b0] = (P.vs.h, v1)
                    prod_done = v1
                    gate_wd[par] = (P.ts.h, P.ts.n)
                for m in range(DC):
                    par = m % 2
                    dmv = S3.add("gpsimd", lambda g, m=m, par=par: g.dma_start(
                        wdb[par][:, 0:IC * 128].rearrange("p (c mm) -> p c mm", c=IC),
                        Wdn[:, m * 128:(m + 1) * 128]
                        .rearrange("(c p) mm -> p c mm", p=128)),
                        waits=[n_rdy, (P.ts.h, P.ts.n)], bump=P.dw, k=16)
                    b0 = m % 2
                    def dmm(t, par=par, tt=tt, b0=b0):
                        for c in range(IC):
                            ins = t.matmul(
                                dps[b0][:],
                                wdb[par][:, c * 128:(c + 1) * 128],
                                pr_sb[:, c * 512:(c + 1) * 512],
                                start=(c == 0), stop=(c == IC - 1))
                        return ins
                    t3 = S3.add("tensor", dmm,
                                waits=[(P.vs.h, prod_done), (P.dw.h, P.dw.n),
                                       gate_dps[b0]], bump=P.ts)
                    v2 = S3.add("vector", lambda v, m=m, tt=tt, b0=b0: v.tensor_tensor(
                        x_fm[:, m * T + tt * 512: m * T + tt * 512 + 512],
                        x_fm[:, m * T + tt * 512: m * T + tt * 512 + 512],
                        dps[b0][:], ALU.add),
                        waits=[(P.ts.h, t3), (P.vs.h, P.vs.n)], bump=P.vs)
                    gate_dps[b0] = (P.vs.h, v2)
                gate_g1 = (P.ts.h, P.ts.n)
            S3.emit(block)
            pst2.close()
    return




L1F_DEPTH = 6


def emit_L1_ffn(P, x_fm, xn_bf, x_gate):
    DEPTH = L1F_DEPTH
    INNER = 1024
    HID = 2730
    HHALF = [(0, 1408), (1408, 2730)]
    Wqkv = P.din("L1f_Wqkv", [6 * D, 3 * INNER])
    WoA = P.din("L1f_Wo", [6 * INNER, D])
    W1A = P.din("L1f_W1", [6 * D, 2 * HID])
    W2A = P.din("L1f_W2", [6 * HID, D])
    qsA = P.din("L1f_qs", [128, 6])
    ksA = P.din("L1f_ks", [128, 6])
    lngA = P.din("L1f_lng", [128, 6 * DC])
    lnbA = P.din("L1f_lnb", [128, 6 * DC])
    b1aA = P.din("L1f_b1a", [128, 6 * 22])
    b1gA = P.din("L1f_b1g", [128, 6 * 22])
    b2A = P.din("L1f_b2", [128, 6 * DC])
    nc = P.nc
    xscr = nc.dram_tensor("l1f_xscr", [128, DC * T], F32).ap()

    rot2A = P.din("c_rot2", [128, 128])
    bd2A = P.din("c_bd2", [128, 128])
    wmA = P.din("c_l1f_wmask", [128, 384])
    csqA = P.din("c_l1f_csq", [128, 2048])
    ssqA = P.din("c_l1f_ssq", [128, 2048])
    cskA = P.din("c_l1f_csk", [128, 2048])
    sskA = P.din("c_l1f_ssk", [128, 2048])
    with contextlib.ExitStack() as st:
        with P.nc.Block() as block:
            rot2_t = P.sb("l1f_rot2", [128, 128], BF16, st)
            bd2_t = P.sb("l1f_bd2", [128, 128], BF16, st)
            wm_t = P.sb("l1f_wm", [128, 384], BF16, st)
            csq_t = P.sb("l1f_csq", [128, 2048], BF16, st)
            ssq_t = P.sb("l1f_ssq", [128, 2048], BF16, st)
            csk_t = P.sb("l1f_csk", [128, 2048], BF16, st)
            ssk_t = P.sb("l1f_ssk", [128, 2048], BF16, st)
            qs_sb = P.sb("l1f_qs", [128, 6], F32, st)
            ks_sb = P.sb("l1f_ks", [128, 6], F32, st)
            lng_sb = P.sb("l1f_lng", [128, 6 * DC], F32, st)
            lnb_sb = P.sb("l1f_lnb", [128, 6 * DC], F32, st)
            b1a_sb = P.sb("l1f_b1a", [128, 6 * 22], F32, st)
            b1g_sb = P.sb("l1f_b1g", [128, 6 * 22], F32, st)
            b2_sb = P.sb("l1f_b2", [128, 6 * DC], F32, st)
            wb = [P.sb(f"l1f_wb{i}", [128, DC * 128], BF16, st) for i in range(4)]
            wdb = [P.sb(f"l1f_wdb{i}", [128, 22 * 128], BF16, st) for i in range(2)]
            xadd = P.sb("l1f_xadd", [128, T], F32, st)

            @block.gpsimd
            def _(g):
                for dst, src in ((qs_sb, qsA), (ks_sb, ksA), (lng_sb, lngA),
                                 (lnb_sb, lnbA), (b1a_sb, b1aA), (b1g_sb, b1gA),
                                 (b2_sb, b2A), (csq_t, csqA), (ssq_t, ssqA),
                                 (csk_t, cskA), (ssk_t, sskA), (rot2_t, rot2A),
                                 (bd2_t, bd2A), (wm_t, wmA)):
                    P.dw.bump(g.dma_start(dst[:], src), 16)
            w_v = P.dw.n

            xng = emit_ln_fm(P, block, x_fm, xn_bf, x_gate=x_gate)

            @block.sync
            def _(sy):
                sy.wait_ge(P.vs.h, xng)
                P.dx.bump(sy.dma_start(xscr, x_fm[:]), 16)
            spill_v = P.dx.n

            S0 = Sched(P)
            for kc in range(DC):
                S0.add("vector", lambda v, kc=kc: v.tensor_copy(
                    x_fm[:, kc * T:(kc + 1) * T], xn_bf[:, kc * T:(kc + 1) * T]),
                    waits=[(P.dx.h, spill_v), (P.vs.h, P.vs.n)], bump=P.vs)
            S0.emit(block)
            y_rdy = (P.vs.h, P.vs.n)

            csq, ssq, csk, ssk = csq_t, ssq_t, csk_t, ssk_t
            rot2 = rot2_t
            bd2 = bd2_t
            wmask = wm_t
            ident = P.consts["c_ident"]
            onesf = P.consts["c_ones_row_f"]

            mem_gate = [y_rdy]
            for li in range(DEPTH):
                attn_st = contextlib.ExitStack()
                ql = P.sb(f"l1f_ql{li}", [128, 1536], BF16, attn_st)
                sq = P.sb(f"l1f_sq{li}", [128, 1536], BF16, attn_st)
                rsb = P.sb(f"l1f_rsb{li}", [128, 1536], BF16, attn_st)
                q_fm = P.sb(f"l1f_q{li}", [128, T], BF16, attn_st)
                k_fm = P.sb(f"l1f_k{li}", [128, T], BF16, attn_st)
                v_tm = P.sb(f"l1f_v{li}", [128, 16 * 130], BF16, attn_st)
                ptile = P.sb(f"l1f_p{li}", [128, 4 * 384], BF16, attn_st)
                zrow = P.sb(f"l1f_z{li}", [1, 512], F32, attn_st)
                rzv = P.sb(f"l1f_rz{li}", [1, 512], F32, attn_st)
                rzb = P.sb(f"l1f_rzb{li}", [64, 512], F32, attn_st)
                attn = P.sb(f"l1f_attn{li}", [128, DC * T], BF16, attn_st)
                pst = contextlib.ExitStack()
                ps1 = P.ps(f"l1f_ps1{li}", [128, 512], F32, pst)
                ps2 = P.ps(f"l1f_ps2{li}", [128, 512], F32, pst)
                ps3 = P.ps(f"l1f_ps3{li}", [128, 512], F32, pst)
                outp = [P.ps(f"l1f_out{i}_{li}", [65, 128], F32, pst) for i in range(4)]
                rzps = P.ps(f"l1f_rzps{li}", [64, 512], F32, pst)

                S = Sched(P)
                gate_ps = [None, None, None]
                gate_out = [None, None, None, None]
                gate_vps = [None, None]
                gate_rzps = None
                shift_done = None
                for kc in range(DC):
                    def shf(v, kc=kc):
                        if kc < 4:
                            return v.tensor_copy(xn_bf[:, kc * T:(kc + 1) * T],
                                                 x_fm[:, kc * T:(kc + 1) * T])
                        v.tensor_copy(xn_bf[:, kc * T + 1:(kc + 1) * T],
                                      x_fm[:, kc * T:(kc + 1) * T - 1])
                        return v.memset(xn_bf[:, kc * T: kc * T + 1], 0.0)
                    sv = S.add("vector", shf, waits=mem_gate + [(P.vs.h, P.vs.n)],
                               bump=P.vs)
                xs_rdy = (P.vs.h, P.vs.n)
                vset = S.add("vector", lambda v: v.memset(v_tm[:], 1.0),
                             waits=mem_gate + [(P.vs.h, P.vs.n)], bump=P.vs)

                oev_all = None
                for hg in range(8):
                    par = hg % 2
                    def wqk(g, li=li, hg=hg, par=par):
                        ins = g.dma_start(
                            wb[par][:].rearrange("p (kc m) -> p kc m", kc=DC),
                            Wqkv[li * D:(li + 1) * D, hg * 128: hg * 128 + 128]
                            .rearrange("(kc p) m -> p kc m", p=128))
                        ins.then_inc(P.dw.h, 16)
                        return g.dma_start(
                            wb[2 + par][:].rearrange("p (kc m) -> p kc m", kc=DC),
                            Wqkv[li * D:(li + 1) * D, INNER + hg * 128: INNER + hg * 128 + 128]
                            .rearrange("(kc p) m -> p kc m", p=128))
                    P.dw.reserve(16)
                    dv_ = S.add("gpsimd", wqk,
                                waits=mem_gate + [(P.ts.h, P.ts.n)], bump=P.dw, k=16)
                    for side, wslot, tabc, tabs, sc_sb, dstf in (
                            (0, par, csq, ssq, qs_sb, q_fm),
                            (1, 2 + par, csk, ssk, ks_sb, k_fm)):
                        psl = [ps1, ps2, ps3]
                        for nt in range(NT):
                            bank = psl[nt % 3]
                            bs = (nt % 3) * 512
                            sl = slice(bs, bs + 512)
                            def qmm(t, wslot=wslot, nt=nt, bank=bank):
                                for kc in range(DC):
                                    ins = t.matmul(
                                        bank[:],
                                        wb[wslot][:, kc * 128:(kc + 1) * 128],
                                        xn_bf[:, kc * T + nt * 512: kc * T + nt * 512 + 512],
                                        start=(kc == 0), stop=(kc == DC - 1))
                                return ins
                            tv = S.add("tensor", qmm,
                                       waits=[xs_rdy, (P.dw.h, dv_), gate_ps[nt % 3]],
                                       bump=P.ts)
                            sv = S.add("scalar", lambda s, sl=sl, bank=bank:
                                       s.copy(ql[:, sl], bank[:]),
                                       waits=[(P.ts.h, tv), (P.ss.h, P.ss.n)], bump=P.ss)
                            s2 = S.add("scalar", lambda s, sl=sl:
                                       s.activation(sq[:, sl], ql[:, sl], AF.Square),
                                       waits=[(P.ss.h, P.ss.n)], bump=P.ss)
                            tb = S.add("tensor", lambda t, sl=sl, bank=bank:
                                       t.matmul(bank[:], bd2[:], sq[:, sl],
                                                start=True, stop=True),
                                       waits=[(P.ss.h, s2)], bump=P.ts)
                            vc = S.add("vector", lambda v, sl=sl, bank=bank:
                                       v.tensor_single_scalar(rsb[:, sl], bank[:],
                                                              1e-24, ALU.max),
                                       waits=[(P.ts.h, tb), (P.vs.h, P.vs.n)], bump=P.vs)
                            gate_ps[nt % 3] = (P.vs.h, vc)
                            sl_ = S.add("scalar", lambda s, sl=sl: s.activation(
                                rsb[:, sl], rsb[:, sl], AF.Ln),
                                waits=[(P.vs.h, vc), (P.ss.h, P.ss.n)], bump=P.ss)
                            se_ = S.add("scalar", lambda s, sl=sl: s.activation(
                                rsb[:, sl], rsb[:, sl], AF.Exp, scale=-0.5),
                                waits=[(P.ss.h, P.ss.n)], bump=P.ss)
                            v1 = S.add("vector", lambda v, sl=sl, sc_sb=sc_sb, li=li:
                                       v.scalar_tensor_tensor(
                                           sq[:, sl], ql[:, sl], sc_sb[:, li:li + 1],
                                           rsb[:, sl], ALU.mult, ALU.mult),
                                       waits=[(P.ss.h, se_), (P.vs.h, P.vs.n)], bump=P.vs)
                            v2 = S.add("vector", lambda v, sl=sl, nt=nt, tabs=tabs:
                                       v.tensor_tensor(ql[:, sl], sq[:, sl],
                                                       tabs[:, nt * 512:(nt + 1) * 512],
                                                       ALU.mult),
                                       waits=[(P.vs.h, v1)], bump=P.vs)
                            v3 = S.add("vector", lambda v, sl=sl, nt=nt, tabc=tabc:
                                       v.tensor_tensor(sq[:, sl], sq[:, sl],
                                                       tabc[:, nt * 512:(nt + 1) * 512],
                                                       ALU.mult),
                                       waits=[(P.vs.h, v2)], bump=P.vs)
                            def rmm(t, sl=sl, bank=bank):
                                t.matmul(bank[:], rot2[:], ql[:, sl],
                                         start=True, stop=False)
                                return t.matmul(bank[:], ident[:], sq[:, sl],
                                                start=False, stop=True)
                            tr = S.add("tensor", rmm,
                                       waits=[(P.vs.h, v3), gate_ps[nt % 3]], bump=P.ts)
                            so = S.add("scalar", lambda s, nt=nt, bank=bank, dstf=dstf:
                                       s.copy(dstf[:, nt * 512:(nt + 1) * 512], bank[:]),
                                       waits=[(P.ts.h, tr), (P.ss.h, P.ss.n)], bump=P.ss)
                            gate_ps[nt % 3] = (P.ss.h, so)
                    def wvl(g, li=li, hg=hg, par=par):
                        return g.dma_start(
                            wb[par][:].rearrange("p (kc m) -> p kc m", kc=DC),
                            Wqkv[li * D:(li + 1) * D,
                                 2 * INNER + hg * 128: 2 * INNER + hg * 128 + 128]
                            .rearrange("(kc p) m -> p kc m", p=128))
                    dv2 = S.add("gpsimd", wvl, waits=[(P.ts.h, P.ts.n)], bump=P.dw, k=16)
                    for tt in range(16):
                        bank = tt % 2
                        def vmm(t, tt=tt, par=par, bank=bank):
                            for kc in range(DC):
                                ins = t.matmul(
                                    [ps1, ps2][bank][:, 0:128],
                                    xn_bf[:, kc * T + tt * 128: kc * T + tt * 128 + 128],
                                    wb[par][:, kc * 128:(kc + 1) * 128],
                                    start=(kc == 0), stop=(kc == DC - 1))
                            return ins
                        tv = S.add("tensor", vmm,
                                   waits=[xs_rdy, (P.dw.h, dv2), gate_ps[bank]],
                                   bump=P.ts)
                        def vev(v, tt=tt, bank=bank):
                            v.tensor_copy(v_tm[:, tt * 130: tt * 130 + 64],
                                          [ps1, ps2][bank][0:64, 0:128][:, 0:64]
                                          if False else [ps1, ps2][bank][:, 0:64])
                            return v.tensor_copy(v_tm[:, tt * 130 + 65: tt * 130 + 129],
                                                 [ps1, ps2][bank][:, 64:128])
                        vv = S.add("vector", vev,
                                   waits=[(P.ts.h, tv), (P.vs.h, vset), (P.vs.h, P.vs.n)],
                                   bump=P.vs)
                        gate_ps[bank] = (P.vs.h, vv)
                    for head in range(2):
                        hb = head * 64
                        pdone = [None] * 4
                        pexp = [None] * 4
                        blk_done = {}
                        def emit_pv(b, head=head):
                            ob = b % 4
                            srcs = [k2 for k2 in range(max(0, b - 2), b + 1)]
                            lastpv = None
                            for si, k2 in enumerate(srcs):
                                r2 = k2 % 4
                                qoff = (b - k2) * 128
                                pv_waits = [pexp[r2]]
                                if si == 0 and gate_out[ob] is not None:
                                    pv_waits.append(gate_out[ob])
                                lastpv = S.add(
                                    "tensor",
                                    lambda t, k2=k2, head=head, ob=ob, r2=r2, qoff=qoff,
                                    si=si, n=len(srcs):
                                    t.matmul(outp[ob][:],
                                             v_tm[:, k2 * 130 + head * 65:
                                                  k2 * 130 + head * 65 + 65],
                                             ptile[:, r2 * 384 + qoff: r2 * 384 + qoff + 128],
                                             start=(si == 0), stop=(si == n - 1)),
                                    waits=pv_waits, bump=P.ts)
                            blk_done[b] = lastpv

                        def emit_norm(g4, head=head):
                            zv = None
                            for j in range(4):
                                zv = S.add("scalar", lambda s, j=j, g4=g4: s.copy(
                                    zrow[0:1, j * 128:(j + 1) * 128],
                                    outp[j][64:65, :]),
                                    waits=[(P.ts.h, blk_done[g4 * 4 + j]),
                                           (P.ss.h, P.ss.n)], bump=P.ss)
                            rv = S.add("vector", lambda v: v.reciprocal(rzv[:], zrow[:]),
                                       waits=[(P.ss.h, zv), (P.vs.h, P.vs.n)], bump=P.vs)
                            bcv = S.add("tensor", lambda t: t.matmul(
                                rzps[:], onesf[0:1, 0:64], rzv[:], start=True, stop=True),
                                waits=[(P.vs.h, rv), gate_rzps], bump=P.ts)
                            rbv = S.add("scalar", lambda s: s.copy(rzb[:], rzps[:]),
                                        waits=[(P.ts.h, bcv), (P.ss.h, P.ss.n)],
                                        bump=P.ss)
                            hh = hg * 2 + head
                            oev = None
                            for j in range(4):
                                oev = S.add("vector", lambda v, hh=hh, g4=g4, j=j:
                                            v.tensor_tensor(
                                                attn[(hh % 2) * 64:(hh % 2) * 64 + 64,
                                                     (hh // 2) * T + g4 * 512 + j * 128:
                                                     (hh // 2) * T + g4 * 512 + j * 128 + 128],
                                                outp[j][0:64, :],
                                                rzb[:, j * 128:(j + 1) * 128], ALU.mult),
                                            waits=[(P.ss.h, rbv), (P.vs.h, P.vs.n)],
                                            bump=P.vs)
                            for j in range(4):
                                gate_out[j] = (P.vs.h, oev)
                            return (P.ss.h, rbv), oev

                        for kt in range(16):
                            sb_ = kt % 2
                            ri = kt % 4
                            qw = min(384, T - kt * 128)
                            tv = S.add("tensor", lambda t, hb=hb, kt=kt, qw=qw, sb_=sb_:
                                       t.matmul([ps1, ps2][sb_][:, 0:qw],
                                                k_fm[hb:hb + 64, kt * 128:(kt + 1) * 128],
                                                q_fm[hb:hb + 64, kt * 128: kt * 128 + qw],
                                                start=True, stop=True),
                                       waits=[gate_ps[sb_]], bump=P.ts)
                            ev = S.add("scalar", lambda s, sb_=sb_, ri=ri, qw=qw:
                                       s.activation(ptile[:, ri * 384: ri * 384 + qw],
                                                    [ps1, ps2][sb_][:, 0:qw], AF.Exp,
                                                    scale=8.0,
                                                    bias=P.consts["c_nb20"][:]),
                                       waits=[(P.ts.h, tv), (P.ss.h, P.ss.n)] +
                                             ([pdone[ri]] if pdone[ri] else []),
                                       bump=P.ss)
                            mv = S.add("vector", lambda v, ri=ri, qw=qw:
                                       v.tensor_tensor(
                                           ptile[:, ri * 384: ri * 384 + qw],
                                           ptile[:, ri * 384: ri * 384 + qw],
                                           wmask[:, 0:qw], ALU.mult),
                                       waits=[(P.ss.h, ev), (P.vs.h, P.vs.n)],
                                       bump=P.vs)
                            gate_ps[sb_] = (P.vs.h, mv)
                            pexp[ri] = (P.vs.h, mv)
                            if kt >= 1:
                                emit_pv(kt - 1)
                                pdone[(kt - 3) % 4] = (P.ts.h, P.ts.n)
                                if (kt - 1) % 4 == 3:
                                    _, oev_all = emit_norm((kt - 1) // 4)
                        emit_pv(15)
                        gr, oev_all = emit_norm(3)
                        gate_rzps = gr
                S.emit(block)
                evo = Evac(P, "resid", dst=x_fm, bias=None, dtokens=T)
                emit_proj(P, block, WoA[li * INNER:(li + 1) * INNER, :], INNER, D,
                          attn, T, evo, rhs_ready=(P.vs.h, oev_all),
                          wtag=f"l1fwo{li}", stack=attn_st, wb=[wdb[0], wdb[1]],
                          banks=[ps1, ps2], w_gate=mem_gate[0])
                pst.close()
                attn_st.close()
                mem_gate = [(P.ts.h, P.ts.n), (P.vs.h, P.vs.n)]

                ffn_st = contextlib.ExitStack()
                a_sb = P.sb(f"l1f_a{li}", [128, 4 * 512], BF16, ffn_st)
                pr = P.sb(f"l1f_pr{li}", [128, 11 * T], BF16, ffn_st)

                S2 = Sched(P)
                for kc in range(DC):
                    def shf2(v, kc=kc):
                        if kc < 4:
                            return v.tensor_copy(xn_bf[:, kc * T:(kc + 1) * T],
                                                 x_fm[:, kc * T:(kc + 1) * T])
                        v.tensor_copy(xn_bf[:, kc * T + 1:(kc + 1) * T],
                                      x_fm[:, kc * T:(kc + 1) * T - 1])
                        return v.memset(xn_bf[:, kc * T: kc * T + 1], 0.0)
                    S2.add("vector", shf2, waits=mem_gate + [(P.vs.h, P.vs.n)], bump=P.vs)
                S2.emit(block)
                lg = emit_ln_fm(P, block, None, xn_bf, eps=1e-5,
                                gamma_col=lng_sb[:, li * DC:(li + 1) * DC],
                                beta_col=lnb_sb[:, li * DC:(li + 1) * DC],
                                x_gate=(P.vs.h, P.vs.n))
                n_rdy = (P.vs.h, lg)
                fst = contextlib.ExitStack()
                fps = [P.ps(f"l1f_fps{i}_{li}", [128, 512], F32, fst) for i in range(4)]
                dps = [P.ps(f"l1f_dps{i}_{li}", [128, 512], F32, fst) for i in range(2)]
                S3 = Sched(P)
                gate_fps = [None, None, None, None]
                gate_dps = [None, None]
                gate_w12 = [None, None]
                gate_pr = None
                gate_wd2 = [None, None]
                for hf, (h0, h1) in enumerate(HHALF):
                    nch = (h1 - h0 + 127) // 128
                    prod_done = None
                    for m in range(nch):
                        mw = min(128, (h1 - h0) - m * 128)
                        par = m % 2
                        mc_glob = (h0 + m * 128) // 128
                        def w12l(g, li=li, h0=h0, m=m, par=par, mw=mw):
                            ins = g.dma_start(
                                wb[par][:, 0:DC * mw]
                                .rearrange("p (kc m) -> p kc m", kc=DC),
                                W1A[li * D:(li + 1) * D, h0 + m * 128: h0 + m * 128 + mw]
                                .rearrange("(kc p) m -> p kc m", p=128))
                            ins.then_inc(P.dw.h, 16)
                            return g.dma_start(
                                wb[2 + par][:, 0:DC * mw]
                                .rearrange("p (kc m) -> p kc m", kc=DC),
                                W1A[li * D:(li + 1) * D,
                                    HID + h0 + m * 128: HID + h0 + m * 128 + mw]
                                .rearrange("(kc p) m -> p kc m", p=128))
                        P.dw.reserve(16)
                        dmv = S3.add("gpsimd", w12l,
                                     waits=[n_rdy, gate_w12[par]], bump=P.dw, k=16)
                        for nt in range(NT):
                            bank = nt % 2
                            gb = 2 + bank
                            def amm(t, par=par, nt=nt, bank=bank, mw=mw):
                                for kc in range(DC):
                                    ins = t.matmul(
                                        fps[bank][0:mw, :],
                                        wb[par][:, kc * mw:(kc + 1) * mw],
                                        xn_bf[:, kc * T + nt * 512: kc * T + nt * 512 + 512],
                                        start=(kc == 0), stop=(kc == DC - 1))
                                return ins
                            t1 = S3.add("tensor", amm,
                                        waits=[n_rdy, (P.dw.h, dmv), gate_fps[bank]],
                                        bump=P.ts)
                            def gmm(t, par=par, nt=nt, gb=gb, mw=mw):
                                for kc in range(DC):
                                    ins = t.matmul(
                                        fps[gb][0:mw, :],
                                        wb[2 + par][:, kc * mw:(kc + 1) * mw],
                                        xn_bf[:, kc * T + nt * 512: kc * T + nt * 512 + 512],
                                        start=(kc == 0), stop=(kc == DC - 1))
                                return ins
                            t2 = S3.add("tensor", gmm, waits=[gate_fps[gb]], bump=P.ts)
                            av = S3.add("vector", lambda v, nt=nt, bank=bank, mw=mw,
                                        li=li, mc_glob=mc_glob:
                                        v.tensor_scalar(
                                            a_sb[0:mw, bank * 512:(bank + 1) * 512],
                                            fps[bank][0:mw, :],
                                            b1a_sb[0:mw, li * 22 + mc_glob:
                                                   li * 22 + mc_glob + 1],
                                            None, ALU.add),
                                        waits=[(P.ts.h, t1), (P.vs.h, P.vs.n)], bump=P.vs)
                            gate_fps[bank] = (P.vs.h, av)
                            gv = S3.add("scalar", lambda s, nt=nt, gb=gb, bank=bank, mw=mw,
                                        li=li, mc_glob=mc_glob:
                                        s.activation(
                                            a_sb[0:mw, (2 + bank) * 512:(3 + bank) * 512],
                                            fps[gb][0:mw, :], AF.Gelu,
                                            bias=b1g_sb[0:mw, li * 22 + mc_glob:
                                                        li * 22 + mc_glob + 1]),
                                        waits=[(P.ts.h, t2), (P.ss.h, P.ss.n)], bump=P.ss)
                            prv = S3.add("vector", lambda v, m=m, nt=nt, bank=bank, mw=mw:
                                         v.tensor_tensor(
                                             pr[0:mw, m * T + nt * 512: m * T + nt * 512 + 512],
                                             a_sb[0:mw, bank * 512:(bank + 1) * 512],
                                             a_sb[0:mw, (2 + bank) * 512:(3 + bank) * 512],
                                             ALU.mult),
                                         waits=[(P.ss.h, gv), (P.vs.h, P.vs.n), gate_pr],
                                         bump=P.vs)
                            gate_fps[gb] = (P.vs.h, prv)
                            prod_done = prv
                        gate_w12[par] = (P.ts.h, P.ts.n)
                    for mo in range(DC):
                        par = mo % 2
                        def w2l(g, li=li, h0=h0, h1=h1, mo=mo, par=par):
                            full = (h1 - h0) // 128
                            rem = (h1 - h0) - full * 128
                            ins = g.dma_start(
                                wdb[par][:, 0:full * 128]
                                .rearrange("p (c m) -> p c m", c=full),
                                W2A[li * HID + h0: li * HID + h0 + full * 128,
                                    mo * 128:(mo + 1) * 128]
                                .rearrange("(c p) m -> p c m", p=128))
                            if rem == 0:
                                return ins
                            ins.then_inc(P.dw.h, 16)
                            return g.dma_start(
                                wdb[par][0:rem, full * 128:(full + 1) * 128],
                                W2A[li * HID + h0 + full * 128: li * HID + h1,
                                    mo * 128:(mo + 1) * 128])
                        if (h1 - h0) % 128 != 0:
                            P.dw.reserve(16)
                        dm2 = S3.add("gpsimd", w2l,
                                     waits=[(P.vs.h, prod_done), gate_wd2[par]],
                                     bump=P.dw, k=16)
                        for nt in range(NT):
                            bank = nt % 2
                            def dmm(t, par=par, nt=nt, bank=bank, h0=h0, h1=h1, nch=nch):
                                for c in range(nch):
                                    cw = min(128, (h1 - h0) - c * 128)
                                    ins = t.matmul(
                                        dps[bank][:],
                                        wdb[par][0:cw, c * 128:(c + 1) * 128],
                                        pr[0:cw, c * T + nt * 512: c * T + nt * 512 + 512],
                                        start=(c == 0), stop=(c == nch - 1))
                                return ins
                            t3 = S3.add("tensor", dmm,
                                        waits=[(P.vs.h, prod_done), (P.dw.h, dm2),
                                               gate_dps[bank]],
                                        bump=P.ts)
                            if hf == 0:
                                vv = S3.add("vector", lambda v, mo=mo, nt=nt, bank=bank,
                                            li=li:
                                            v.scalar_tensor_tensor(
                                                x_fm[:, mo * T + nt * 512:
                                                     mo * T + nt * 512 + 512],
                                                dps[bank][:],
                                                b2_sb[:, li * DC + mo: li * DC + mo + 1],
                                                x_fm[:, mo * T + nt * 512:
                                                     mo * T + nt * 512 + 512],
                                                ALU.add, ALU.add),
                                            waits=[(P.ts.h, t3), (P.vs.h, P.vs.n)],
                                            bump=P.vs)
                            else:
                                vv = S3.add("vector", lambda v, mo=mo, nt=nt, bank=bank:
                                            v.tensor_tensor(
                                                x_fm[:, mo * T + nt * 512:
                                                     mo * T + nt * 512 + 512],
                                                x_fm[:, mo * T + nt * 512:
                                                     mo * T + nt * 512 + 512],
                                                dps[bank][:], ALU.add),
                                            waits=[(P.ts.h, t3), (P.vs.h, P.vs.n)],
                                            bump=P.vs)
                            gate_dps[bank] = (P.vs.h, vv)
                        gate_wd2[par] = (P.ts.h, P.ts.n)
                    gate_pr = (P.ts.h, P.ts.n)
                S3.emit(block)
                fst.close()
                ffn_st.close()
                mem_gate = [(P.ts.h, P.ts.n), (P.vs.h, P.vs.n)]

            S4 = Sched(P)
            for kc in range(DC):
                lv = S4.add("sync", lambda sy, kc=kc: sy.dma_start(
                    xadd[:], xscr[:, kc * T:(kc + 1) * T]),
                    waits=mem_gate + [(P.vs.h, P.vs.n)] if kc == 0 else
                    [(P.vs.h, P.vs.n)], bump=P.dx, k=16)
                av = S4.add("vector", lambda v, kc=kc: v.tensor_tensor(
                    x_fm[:, kc * T:(kc + 1) * T], x_fm[:, kc * T:(kc + 1) * T],
                    xadd[:], ALU.add),
                    waits=[(P.dx.h, lv), (P.vs.h, P.vs.n)], bump=P.vs)
            S4.emit(block)
    return


def emit_L0_ffn(P, x_fm, xn_bf, x_gate):
    WT = P.din("L0f_WT", [D, D])
    Vr = P.din("L0f_Vr", [D, 32])
    CT = P.din("L0f_CT", [32, D])
    bb = P.din("L0f_b", [128, DC])

    with contextlib.ExitStack() as st:
        with P.nc.Block() as block:
            vr_sb = P.sb("l0f_vr", [128, DC * 32], BF16, st)
            ct_sb = P.sb("l0f_ct", [32, D], BF16, st)
            b_sb = P.sb("l0f_b", [128, DC], F32, st)
            t_bf = P.sb("l0f_t", [32, T], BF16, st)
            wbufs = [P.sb(f"l0fw{i}", [128, DC * 128], BF16, st) for i in range(2)]

            xn_gate = emit_ln_fm(P, block, x_fm, xn_bf, x_gate=x_gate)
            ps_t = P.ps("l0f_pst", [32, 512], F32, st)

            @block.gpsimd
            def _(g):
                P.dw.bump(g.dma_start(
                    vr_sb[:].rearrange("p (kc r) -> p kc r", kc=DC),
                    Vr.rearrange("(kc p) r -> p kc r", p=128)), 16)
                P.dw.bump(g.dma_start(ct_sb[:], CT), 16)
                P.dw.bump(g.dma_start(b_sb[:], bb), 16)
            small_loaded = P.dw.n

            tt0 = P.ts.n
            st0 = P.ss.n

            @block.tensor
            def _(t):
                t.wait_ge(P.dw.h, small_loaded)
                t.wait_ge(P.vs.h, xn_gate)
                for nt in range(NT):
                    if nt >= 1:
                        t.wait_ge(P.ss.h, st0 + nt)
                    for k in range(DC):
                        ins = t.matmul(ps_t[:],
                                       vr_sb[:, k * 32:(k + 1) * 32],
                                       xn_bf[:, k * T + nt * 512: k * T + nt * 512 + 512],
                                       start=(k == 0), stop=(k == DC - 1))
                    P.ts.bump(ins, 1)

            @block.scalar
            def _(s):
                for nt in range(NT):
                    s.wait_ge(P.ts.h, tt0 + nt + 1)
                    ins = s.copy(t_bf[:, nt * 512:(nt + 1) * 512], ps_t[:])
                    P.ss.bump(ins, 1)
            t_ready = P.ss.n

            def extra(t, m, nt, ps_ap):
                return t.matmul(ps_ap, ct_sb[:, m * 128:(m + 1) * 128],
                                t_bf[:, nt * 512:(nt + 1) * 512], start=False, stop=True)

            P.tap("tbf", t_bf[:])
            ev = Evac(P, "resid", dst=x_fm, bias=b_sb, dtokens=T)
            emit_proj(P, block, WT, D, D, xn_bf, T, ev, extra_mm=extra,
                      rhs_ready=(P.vs.h, xn_gate), wait_extra=(P.ss.h, t_ready),
                      wtag="l0fw", stack=st, wb=wbufs)
    return



def emit_dbg_ln(P, x_fm, xn_bf, x_gate):
    with P.nc.Block() as block:
        emit_ln_fm(P, block, x_fm, xn_bf, x_gate=x_gate)
    P.tap("xn", xn_bf[:])
    return


PHASE_EMITTERS = {"L0_ffn": emit_L0_ffn, "dbg_ln": emit_dbg_ln, "L3_attn": emit_L3_attn,
                  "L0_attn": emit_L0_attn, "L2_attn": emit_L2_attn,
                  "L3_ffn": emit_L3_ffn, "L2_ffn": emit_L2_ffn,
                  "L1_attn": emit_L1_attn, "L1_ffn": emit_L1_ffn}


def build_program(phases):
    nc = bass.Bass("TRN2", target_bir_lowering=False, debug=False)
    with contextlib.ExitStack() as st:
        P = Prog(nc, st)
        XIN = nc.dram_tensor("XIN", [D, T], F32, kind="ExternalInput").ap()
        XOUT = nc.dram_tensor("XOUT", [D, T], F32, kind="ExternalOutput").ap()
        P.input_names.append("XIN")
        x_fm = P.sb("x_fm", [128, DC * T], F32)
        xn_bf = P.sb("xn_bf", [128, DC * T], BF16)

        needed = set(CONST_BASE)
        for ph in phases:
            needed |= CONST_NEED.get(ph, set())
        with nc.Block() as block:
            const_aps = []
            for name, shape, dt in CONST_SPECS:
                if name not in needed:
                    continue
                P.consts[name] = P.sb(name + "_sb", shape, dt)
                const_aps.append((name, P.din(name, shape)))

            @block.gpsimd
            def _(g):
                for name, ap in const_aps:
                    P.dw.bump(g.dma_start(P.consts[name][:], ap), 16)
            if "c_eps6" in P.consts:
                nc.const_aps.register(F32, 1e-06, P.consts["c_eps6"][:])

            @block.sync
            def _(sy):
                P.dx.bump(sy.dma_start(
                    x_fm[:].rearrange("p (kc t) -> p kc t", kc=DC),
                    XIN.rearrange("(kc p) t -> p kc t", p=128)), 16)

            @block.gpsimd
            def _(g):
                g.wait_ge(P.dw.h, P.dw.n)
        x_gate = (P.dx.h, P.dx.n)

        for ph in phases:
            PHASE_EMITTERS[ph](P, x_fm, xn_bf, x_gate)
            x_gate = None

        with nc.Block() as block:
            tap_aps = []
            for i, (name, tile_ap) in enumerate(P.taps):
                shp = list(tile_ap.shape)
                tap_aps.append(nc.dram_tensor(f"DBG_{name}", shp, F32,
                                              kind="ExternalOutput").ap())

            @block.gpsimd
            def _(g):
                P.dx.bump(g.dma_start(
                    XOUT.rearrange("(kc p) t -> p kc t", p=128),
                    x_fm[:].rearrange("p (kc t) -> p kc t", kc=DC)), 16)
                for (name, tile_ap), dap in zip(P.taps, tap_aps):
                    P.dx.bump(g.dma_start(dap, tile_ap), 16)
                g.wait_ge(P.dx.h, P.dx.n)
    return nc



def _colsplit(v):
    v = np.asarray(v, dtype=np.float32)
    n = v.shape[-1]
    return np.ascontiguousarray(v.reshape(n // 128, 128).T)


def prep_weights(inp):
    f32 = np.float32
    w = {}
    w["L0f_WT"] = np.ascontiguousarray(np.asarray(inp["L0_ffn_W"], f32).T)
    w["L0f_Vr"] = np.ascontiguousarray(np.asarray(inp["L0_ffn_V_r"], f32))
    w["L0f_CT"] = np.ascontiguousarray(np.asarray(inp["L0_ffn_C"], f32).T)
    w["L0f_b"] = _colsplit(inp["L0_ffn_b"])
    w["L3a_Wqkv"] = np.ascontiguousarray(np.asarray(inp["L3_attn_W_qkv"], f32))
    w["L3a_Wo"] = np.ascontiguousarray(np.asarray(inp["L3_attn_W_proj"], f32))
    bqkv = np.asarray(inp["L3_attn_b_qkv"], f32)
    w["L3a_bqk"] = np.concatenate([_colsplit(bqkv[0:D]), _colsplit(bqkv[D:2 * D])], axis=1)
    w["L3a_bv"] = np.ascontiguousarray(bqkv[2 * D:3 * D].reshape(1, D))
    w["L3a_bo"] = _colsplit(inp["L3_attn_b_proj"])
    w["L3a_g"] = _colsplit(inp["L3_attn_ln_gamma"])
    w["L3a_b"] = _colsplit(inp["L3_attn_ln_beta"])
    Wqkv = np.asarray(inp["L0_attn_Wqkv"], f32)
    bqkv0 = np.asarray(inp["L0_attn_bqkv"], f32)
    qidx = np.array([c16 * 192 + d for c16 in range(16) for d in range(64)])
    kidx = qidx + 64
    vidx = qidx + 128
    w["L0a_Wq"] = np.ascontiguousarray(Wqkv[qidx, :].T)
    w["L0a_Wk"] = np.ascontiguousarray(Wqkv[kidx, :].T)
    w["L0a_Wv"] = np.ascontiguousarray(Wqkv[vidx, :].T)
    w["L0a_WoT"] = np.ascontiguousarray(np.asarray(inp["L0_attn_Wo"], f32).T)
    w["L0a_qb"] = _colsplit(bqkv0[qidx])
    w["L0a_kb"] = _colsplit(bqkv0[kidx])
    w["L0a_bv"] = np.ascontiguousarray(bqkv0[vidx].reshape(1, D))
    w["L0a_bo"] = _colsplit(inp["L0_attn_bo"])
    w["L2a_wq"] = np.ascontiguousarray(np.asarray(inp["L2_attn_wq"], f32))
    w["L2a_wkd"] = np.ascontiguousarray(np.asarray(inp["L2_attn_wk_down"], f32))
    w["L2a_wvd"] = np.ascontiguousarray(np.asarray(inp["L2_attn_wv_down"], f32))
    w["L2a_wku"] = np.ascontiguousarray(np.asarray(inp["L2_attn_wk_up"], f32))
    w["L2a_wvu"] = np.ascontiguousarray(np.asarray(inp["L2_attn_wv_up"], f32))
    w["L2a_wo"] = np.ascontiguousarray(np.asarray(inp["L2_attn_wo"], f32))
    w["L3f_WfcT"] = np.ascontiguousarray(np.asarray(inp["L3_ffn_W_fc"], f32).T)
    w["L3f_WprT"] = np.ascontiguousarray(np.asarray(inp["L3_ffn_W_proj"], f32).T)
    w["L3f_lnw"] = _colsplit(inp["L3_ffn_ln_w"])
    w["L3f_lnb"] = _colsplit(inp["L3_ffn_ln_b"])
    w["L2f_WgT"] = np.ascontiguousarray(np.asarray(inp["L2_ffn_Wg"], f32).T)
    w["L2f_rb"] = np.ascontiguousarray(np.asarray(inp["L2_ffn_routing_bias"], f32)
                                       .reshape(1, 8))
    W1 = np.asarray(inp["L2_ffn_W1"], f32)
    W2 = np.asarray(inp["L2_ffn_W2"], f32)
    W3 = np.asarray(inp["L2_ffn_W3"], f32)
    w1t = np.concatenate([W1.transpose(0, 2, 1).reshape(8 * 1024, 2730),
                          np.asarray(inp["L2_ffn_Ws1"], f32).T], axis=0)
    w2t = np.concatenate([W2.transpose(0, 2, 1).reshape(8 * 1024, 2730),
                          np.asarray(inp["L2_ffn_Ws2"], f32).T], axis=0)
    w3t = np.concatenate([W3.transpose(0, 2, 1).reshape(8 * 2730, 1024),
                          np.asarray(inp["L2_ffn_Ws3"], f32).T], axis=0)
    w["L2f_W1T"] = np.ascontiguousarray(w1t)
    w["L2f_W2T"] = np.ascontiguousarray(w2t)
    w["L2f_W3T"] = np.ascontiguousarray(w3t)
    w["L1a_Wq"] = np.ascontiguousarray(np.asarray(inp["L1_attn_Wq"], f32))
    w["L1a_Wk"] = np.ascontiguousarray(np.asarray(inp["L1_attn_Wk"], f32))
    w["L1a_Wv"] = np.ascontiguousarray(np.asarray(inp["L1_attn_Wv"], f32))
    w["L1a_Wg"] = np.ascontiguousarray(np.asarray(inp["L1_attn_Wg"], f32))
    w["L1a_Wgk1"] = np.ascontiguousarray(np.asarray(inp["L1_attn_Wgk1"], f32))
    w["L1a_Wgk2"] = np.ascontiguousarray(np.asarray(inp["L1_attn_Wgk2"], f32))
    w["L1a_bgk2"] = np.ascontiguousarray(np.asarray(inp["L1_attn_bgk2"], f32).reshape(1, 512))
    w["L1a_anw"] = _colsplit(inp["L1_attn_attn_norm_w"])
    w["L1a_gnw"] = np.ascontiguousarray(np.asarray(inp["L1_attn_gnorm_w"], f32).reshape(1, 256))
    w["L1a_Wo"] = np.ascontiguousarray(np.asarray(inp["L1_attn_Wo"], f32))
    w["L1a_mnw"] = _colsplit(inp["L1_attn_mlp_norm_w"])
    w["L1a_Wgate"] = np.ascontiguousarray(np.asarray(inp["L1_attn_W_gate"], f32))
    w["L1a_Wup"] = np.ascontiguousarray(np.asarray(inp["L1_attn_W_up"], f32))
    w["L1a_Wdn"] = np.ascontiguousarray(np.asarray(inp["L1_attn_W_down"], f32))
    w["L1f_Wqkv"] = np.ascontiguousarray(
        np.asarray(inp["L1_ffn_Wqkv"], f32).reshape(6 * D, 3 * 1024))
    w["L1f_Wo"] = np.ascontiguousarray(
        np.asarray(inp["L1_ffn_Wo"], f32).reshape(6 * 1024, D))
    w["L1f_W1"] = np.ascontiguousarray(
        np.asarray(inp["L1_ffn_W1"], f32).reshape(6 * D, 5460))
    w["L1f_W2"] = np.ascontiguousarray(
        np.asarray(inp["L1_ffn_W2"], f32).reshape(6 * 2730, D))
    w["L1f_qs"] = np.ascontiguousarray(
        np.tile(np.asarray(inp["L1_ffn_q_scale"], f32).T, (2, 1)))
    w["L1f_ks"] = np.ascontiguousarray(
        np.tile(np.asarray(inp["L1_ffn_k_scale"], f32).T, (2, 1)))
    w["L1f_lng"] = np.concatenate(
        [_colsplit(np.asarray(inp["L1_ffn_ln_g"], f32)[i]) for i in range(6)], axis=1)
    w["L1f_lnb"] = np.concatenate(
        [_colsplit(np.asarray(inp["L1_ffn_ln_b"], f32)[i]) for i in range(6)], axis=1)
    b1 = np.asarray(inp["L1_ffn_b1"], f32)
    def padsplit(v):
        vp = np.zeros(22 * 128, dtype=f32)
        vp[:2730] = v
        return np.ascontiguousarray(vp.reshape(22, 128).T)
    w["L1f_b1a"] = np.concatenate([padsplit(b1[i, :2730]) for i in range(6)], axis=1)
    w["L1f_b1g"] = np.concatenate([padsplit(b1[i, 2730:]) for i in range(6)], axis=1)
    w["L1f_b2"] = np.concatenate(
        [_colsplit(np.asarray(inp["L1_ffn_b2"], f32)[i]) for i in range(6)], axis=1)
    return w


ALL_PHASES = ["L0_attn", "L0_ffn", "L1_attn", "L1_ffn",
              "L2_attn", "L2_ffn", "L3_attn", "L3_ffn"]


def run_phases(x, weights, consts, phases):
    nc = build_program(phases)
    in_maps = []
    for b in range(BATCH):
        m = dict(XIN=np.ascontiguousarray(x[b].T))
        m.update(consts)
        m.update(weights)
        in_maps.append(m)
    res = run_bass_kernel_spmd(nc, in_maps, core_ids=list(range(BATCH)))
    return np.stack([np.asarray(r["XOUT"]).T for r in res.results], axis=0)


def kernel(**inputs):
    x = np.asarray(inputs["L0_attn_x"], dtype=np.float32)
    consts = host_constants()
    weights = prep_weights(inputs)
    x = run_phases(x, weights, consts, ALL_PHASES)
    return x.astype(np.float32)


# version 07 (final)
# speedup vs baseline: 1.1341x; 1.0198x over previous
import contextlib
import numpy as np
import ml_dtypes
import concourse.bass as bass
import concourse.mybir as mybir
from concourse.bass_utils import run_bass_kernel_spmd

F32 = mybir.dt.float32
BF16 = mybir.dt.bfloat16
AF = mybir.ActivationFunctionType
ALU = mybir.AluOpType
AX = mybir.AxisListType

BATCH = 8
T = 2048
D = 1024
DC = D // 128
NT = T // 512
BF = ml_dtypes.bfloat16


def host_constants():
    c = {}
    c["c_ident"] = np.eye(128, dtype=np.float32)
    c["c_ones_col"] = np.ones((128, 1), dtype=np.float32)
    c["c_ones_row"] = np.ones((1, 128), dtype=np.float32)
    c["c_ones_row_f"] = np.ones((1, 128), dtype=np.float32)
    c["c_ones_col_f"] = np.ones((128, 1), dtype=np.float32)
    c["c_nb10"] = np.full((128, 1), -10.0, dtype=np.float32)
    c["c_nb20"] = np.full((128, 1), -20.0, dtype=np.float32)
    tri = np.triu(np.ones((64, 64), dtype=np.float32))
    c["c_triu2"] = np.concatenate([tri, tri], axis=0)
    c["c_trium"] = tri.copy()
    c["c_lnscale"] = np.full((128, 1), np.log(128 ** -0.5), dtype=np.float32)
    c["c_p1"] = np.full((128, 1), 1.0, dtype=np.float32)
    c["c_eps6"] = np.full((128, 1), 1e-06, dtype=np.float32)
    c["c_identf"] = np.eye(128, dtype=np.float32)
    DH = 64
    half = np.arange(0, DH, 2, dtype=np.float64)
    inv_freq = 1.0 / 10000.0 ** (half / DH)
    tpos = np.arange(2048, dtype=np.float64)
    fr = tpos[:, None] * inv_freq[None, :]
    fr = np.concatenate([fr, fr], -1)
    base_scale = (half + 0.4 * DH) / (1.4 * DH)
    power = (tpos - 1024) / 128.0
    xsc = np.concatenate([base_scale[None, :] ** power[:, None]] * 2, -1)
    cosf = np.cos(fr)
    sinf = np.sin(fr)
    sig = np.concatenate([np.arange(32, 64), np.arange(0, 32)])
    def t2tab(a):
        return np.ascontiguousarray(np.tile(a.T.astype(np.float32), (2, 1)))
    c["c_l1f_csq"] = t2tab(cosf * xsc)
    c["c_l1f_ssq"] = t2tab(sinf[:, sig] * xsc[:, sig])
    c["c_l1f_csk"] = t2tab(cosf / xsc)
    c["c_l1f_ssk"] = t2tab(sinf[:, sig] / xsc[:, sig])
    rot = np.zeros((64, 64), dtype=np.float32)
    for mcol in range(32):
        rot[mcol + 32, mcol] = -1.0
    for mcol in range(32, 64):
        rot[mcol - 32, mcol] = 1.0
    c["c_rot2"] = np.ascontiguousarray(np.block(
        [[rot, np.zeros((64, 64), np.float32)],
         [np.zeros((64, 64), np.float32), rot]]))
    bd = np.zeros((128, 128), dtype=np.float32)
    bd[0:64, 0:64] = 1.0
    bd[64:128, 64:128] = 1.0
    c["c_bd2"] = bd
    wm = np.zeros((128, 384), dtype=np.float32)
    for kp in range(128):
        diff = np.arange(384) - kp
        wm[kp, :] = (diff >= 0) & (diff <= 256)
    c["c_l1f_wmask"] = wm
    ext = np.zeros((128, 896), dtype=np.float32)
    for kp in range(128):
        ext[kp, :] = (np.arange(896) >= kp + 384)
    c["c_l2a_cmask"] = ext
    return c


CONST_NEED = {
    "L1_attn": {"c_triu2", "c_trium", "c_lnscale", "c_p1", "c_identf", "c_eps6"},
    "L1_ffn": {"c_p1", "c_eps6"},
}
CONST_BASE = {"c_ones_col", "c_ones_row", "c_ident", "c_ones_row_f", "c_nb10",
              "c_nb20", "c_ones_col_f"}
CONST_SPECS = [("c_ones_col", [128, 1], BF16), ("c_ones_row", [1, 128], BF16),
               ("c_ident", [128, 128], BF16), ("c_ones_row_f", [1, 128], F32),
               ("c_nb10", [128, 1], F32), ("c_nb20", [128, 1], F32),
               ("c_ones_col_f", [128, 1], F32),

               ("c_triu2", [128, 64], BF16), ("c_trium", [64, 64], BF16),
               ("c_lnscale", [128, 1], F32), ("c_p1", [128, 1], F32),
               ("c_eps6", [128, 1], F32),

               ("c_identf", [128, 128], F32)]


class Sem:
    def __init__(self, st, nc, name):
        self.h = st.enter_context(nc.semaphore(name))
        self.n = 0

    def bump(self, instr, k=1):
        instr.then_inc(self.h, k)
        self.n += k
        return self.n

    def reserve(self, k=1):
        self.n += k
        return self.n


class Prog:
    def __init__(self, nc, st):
        self.nc = nc
        self.st = st
        self.dw = Sem(st, nc, "dw")
        self.dx = Sem(st, nc, "dx")
        self.ts = Sem(st, nc, "ts")
        self.vs = Sem(st, nc, "vs")
        self.ss = Sem(st, nc, "ss")
        self.gs = Sem(st, nc, "gs")
        self.consts = {}
        self.input_names = []
        self.taps = []

    def sb(self, name, shape, dtype=F32, stack=None):
        return (stack or self.st).enter_context(self.nc.sbuf_tensor(name, shape, dtype))

    def ps(self, name, shape, dtype=F32, stack=None):
        return (stack or self.st).enter_context(self.nc.psum_tensor(name, shape, dtype))

    def din(self, name, shape):
        self.input_names.append(name)
        return self.nc.dram_tensor(name, list(shape), F32, kind="ExternalInput").ap()

    def tap(self, name, tile_ap):
        self.taps.append((name, tile_ap))



_LN_CTR = [0]


def emit_ln_fm(P, block, x_fm, xn_bf, eps=1e-5, gamma_col=None, beta_col=None,
               n_feat=D, tokens=T, x_gate=None, scr=None, stack=None):
    _LN_CTR[0] += 1
    width = min(512, tokens)
    own_stack = contextlib.ExitStack()
    if scr is None:
        with own_stack:
            scr = ln_scratch(P, own_stack, width=width, tag=f"_{_LN_CTR[0]}")
            r = None
            for off in range(0, tokens, width):
                r = _emit_ln_fm_inner(P, block, x_fm, xn_bf, scr, eps, gamma_col,
                                      beta_col, n_feat, tokens, x_gate, off, width)
            return r
    r = None
    for off in range(0, tokens, width):
        r = _emit_ln_fm_inner(P, block, x_fm, xn_bf, scr, eps, gamma_col, beta_col,
                              n_feat, tokens, x_gate, off, width)
    return r


def _emit_ln_fm_inner(P, block, x_fm, xn_bf, scr, eps, gamma_col, beta_col,
                      n_feat, tokens, x_gate, off=0, width=None):
    if width is None:
        width = tokens
    FC = n_feat // 128
    NTl = width // 512
    ones_col = P.consts["c_ones_col"]
    ones_row = P.consts["c_ones_row"]
    pp = scr["pp"]
    row = scr["row"]
    rowb = scr["rowb"]
    ab = scr["ab"]
    psA = scr["psA"]
    psB = scr["psB"]
    bcA = scr["bcA"]
    bcB = scr["bcB"]

    if x_fm is not None and off == 0:
        @block.vector
        def _(v):
            if x_gate is not None:
                v.wait_ge(x_gate[0], x_gate[1])
            for kc in range(FC):
                ins = v.tensor_copy(xn_bf[:, kc * tokens:(kc + 1) * tokens],
                                    x_fm[:, kc * tokens:(kc + 1) * tokens])
            P.vs.bump(ins, 1)
        gate0 = (P.vs.h, P.vs.n)
        P.lngate = gate0
    elif x_fm is not None:
        gate0 = P.lngate
    else:
        gate0 = x_gate
    v_cast = gate0[1]

    ts0 = P.ts.n
    ss0 = P.ss.n

    @block.scalar
    def _(s):
        s.wait_ge(gate0[0], gate0[1])
        for nt in range(NTl):
            for kc in range(FC):
                gi = nt * FC + kc
                if gi >= 2:
                    s.wait_ge(P.ts.h, ts0 + gi - 1)
                ins = s.activation(pp[:, (gi % 2) * 512:(gi % 2) * 512 + 512],
                                   xn_bf[:, kc * tokens + off + nt * 512: kc * tokens + off + nt * 512 + 512],
                                   AF.Square)
                P.ss.bump(ins, 1)
            s.wait_ge(P.ts.h, ts0 + (nt + 1) * FC)
            s.copy(row[0:1, nt * 512:(nt + 1) * 512], psA[:])
            ins = s.copy(row[0:1, width + nt * 512: width + nt * 512 + 512], psB[:])
            P.ss.bump(ins, 1)

    @block.tensor
    def _(t):
        t.wait_ge(gate0[0], gate0[1])
        for nt in range(NTl):
            for kc in range(FC):
                gi = nt * FC + kc
                t.wait_ge(P.ss.h, ss0 + nt * (FC + 1) + kc + 1)
                t.matmul(psA[:], ones_col[:],
                         xn_bf[:, kc * tokens + off + nt * 512: kc * tokens + off + nt * 512 + 512],
                         start=(kc == 0), stop=(kc == FC - 1))
                ins = t.matmul(psB[:], ones_col[:],
                               pp[:, (gi % 2) * 512:(gi % 2) * 512 + 512],
                               start=(kc == 0), stop=(kc == FC - 1))
                P.ts.bump(ins, 1)
    assert P.ts.n == ts0 + NTl * FC

    @block.scalar
    def _(s):
        s.wait_ge(P.ss.h, ss0 + NTl * (FC + 1))
        s.mul(row[0:1, 0:width], row[0:1, 0:width], 1.0 / n_feat)
        ins = s.mul(row[0:1, width:2 * width], row[0:1, width:2 * width], 1.0 / n_feat)
        P.ss.bump(ins, 1)
    s_rows = P.ss.n

    @block.vector
    def _(v):
        v.wait_ge(P.ss.h, s_rows)
        v.tensor_tensor(rowb[0:1, 0:width], row[0:1, 0:width],
                        row[0:1, 0:width], ALU.mult)
        ins = v.scalar_tensor_tensor(row[0:1, width:2 * width], row[0:1, width:2 * width],
                                     eps, rowb[0:1, 0:width],
                                     ALU.add, ALU.subtract)
        P.vs.bump(ins, 1)
    v_var = P.vs.n

    @block.scalar
    def _(s):
        s.wait_ge(P.vs.h, v_var)
        s.activation(row[0:1, width:2 * width], row[0:1, width:2 * width], AF.Ln)
        ins = s.activation(rowb[0:1, 0:width], row[0:1, width:2 * width], AF.Exp,
                           scale=-0.5)
        P.ss.bump(ins, 1)
    s_rstd = P.ss.n

    @block.vector
    def _(v):
        v.wait_ge(P.ss.h, s_rstd)
        ins = v.scalar_tensor_tensor(rowb[0:1, width:2 * width], row[0:1, 0:width], -1.0,
                                     rowb[0:1, 0:width], ALU.mult, ALU.mult)
        P.vs.bump(ins, 1)
    v_brow = P.vs.n

    tb0 = P.ts.n
    sb0 = P.ss.n

    @block.tensor
    def _(t):
        t.wait_ge(P.vs.h, v_brow)
        for nt in range(NTl):
            if nt >= 1:
                t.wait_ge(P.ss.h, sb0 + nt)
            t.matmul(bcA[:], ones_row[:], rowb[0:1, nt * 512:(nt + 1) * 512],
                     start=True, stop=True)
            ins = t.matmul(bcB[:], ones_row[:],
                           rowb[0:1, width + nt * 512: width + nt * 512 + 512],
                           start=True, stop=True)
            P.ts.bump(ins, 1)

    @block.scalar
    def _(s):
        for nt in range(NTl):
            s.wait_ge(P.ts.h, tb0 + nt + 1)
            s.copy(ab[:, nt * 512:(nt + 1) * 512], bcA[:])
            ins = s.copy(ab[:, width + nt * 512: width + nt * 512 + 512], bcB[:])
            P.ss.bump(ins, 1)
    s_ab = P.ss.n

    @block.vector
    def _(v):
        v.wait_ge(P.ss.h, s_ab)
        for kc in range(FC):
            sl = slice(kc * tokens + off, kc * tokens + off + width)
            v.tensor_tensor(xn_bf[:, sl], xn_bf[:, sl], ab[:, 0:width], ALU.mult)
            ins = v.tensor_tensor(xn_bf[:, sl], xn_bf[:, sl],
                                  ab[:, width:2 * width], ALU.add)
            if gamma_col is not None:
                ins = v.tensor_scalar(xn_bf[:, sl], xn_bf[:, sl],
                                      gamma_col[:, kc:kc + 1], beta_col[:, kc:kc + 1],
                                      ALU.mult, ALU.add)
        P.vs.bump(ins, 1)
    return P.vs.n


def ln_scratch(P, stack, width=1024, tag=""):
    return dict(
        pp=P.sb("lnpp" + tag, [128, 1024], BF16, stack),
        row=P.sb("lnrow" + tag, [1, 2 * width], F32, stack),
        rowb=P.sb("lnrowb" + tag, [1, 2 * width], BF16, stack),
        ab=P.sb("lnab" + tag, [128, 2 * width], BF16, stack),
        psA=P.ps("lnpsA" + tag, [1, 512], F32, stack),
        psB=P.ps("lnpsB" + tag, [1, 512], F32, stack),
        bcA=P.ps("lnbcA" + tag, [128, 512], F32, stack),
        bcB=P.ps("lnbcB" + tag, [128, 512], F32, stack),
    )



class Evac:

    def __init__(self, P, kind, dst=None, bias=None, act=AF.Copy, scale=1.0,
                 dtokens=None, engine="scalar", dst_col0=0):
        self.P = P
        self.kind = kind
        self.dst = dst
        self.bias = bias
        self.act = act
        self.scale = scale
        self.dtok = dtokens
        self.plan_list = []
        self.engine = "vector" if kind == "resid" else engine
        self.sem = P.vs if self.engine == "vector" else P.ss
        self.dst_col0 = dst_col0

    def plan(self, m, nt, ps_ap, mmv):
        v = self.sem.reserve(1)
        self.plan_list.append((m, nt, ps_ap, mmv, v))
        return (self.sem.h, v)

    def emit(self, block, tile_n=512):
        P = self.P
        if self.engine == "vector":
            @block.vector
            def _(v):
                for (m, nt, ps_ap, mmv, val) in self.plan_list:
                    v.wait_ge(P.ts.h, mmv)
                    c0 = self.dst_col0 + m * self.dtok + nt * tile_n
                    if self.kind == "resid":
                        ins = v.scalar_tensor_tensor(
                            self.dst[:, c0:c0 + tile_n], ps_ap,
                            self.bias[:, m:m + 1] if self.bias is not None else 0.0,
                            self.dst[:, c0:c0 + tile_n], ALU.add, ALU.add)
                    else:
                        ins = v.tensor_copy(self.dst[:, c0:c0 + tile_n], ps_ap)
                    ins.then_inc(self.sem.h, 1)
        else:
            @block.scalar
            def _(s):
                for (m, nt, ps_ap, mmv, val) in self.plan_list:
                    s.wait_ge(P.ts.h, mmv)
                    c0 = self.dst_col0 + m * self.dtok + nt * tile_n
                    if self.bias is not None:
                        ins = s.activation(self.dst[:, c0:c0 + tile_n], ps_ap, self.act,
                                           bias=self.bias[:, m:m + 1], scale=self.scale)
                    else:
                        ins = s.activation(self.dst[:, c0:c0 + tile_n], ps_ap, self.act,
                                           scale=self.scale)
                    ins.then_inc(self.sem.h, 1)


def emit_proj(P, block, w_ap, din, dout, rhs_bf, tokens, evac, extra_mm=None,
              rhs_ready=None, wtag="w", n_banks=4, stack=None, tile_n=512,
              wait_extra=None, wb=None, w_gate=None, rhs_stride=None, rhs_col0=0,
              banks=None):
    KC = din // 128
    MC = (dout + 127) // 128
    NTl = tokens // tile_n
    if rhs_stride is None:
        rhs_stride = tokens
    st = stack or P.st
    if wb is None:
        wb = [P.sb(f"{wtag}{i}", [128, KC * 128], BF16, st) for i in range(2)]
    if banks is None:
        banks = [P.ps(f"{wtag}ps{i}", [128, tile_n], F32, st) for i in range(n_banks)]
    n_banks = len(banks)

    ts0 = P.ts.n

    @block.gpsimd
    def _(g):
        if w_gate is not None:
            g.wait_ge(w_gate[0], w_gate[1])
        for m in range(MC):
            par = m % 2
            mw = min(128, dout - m * 128)
            if m >= 2:
                g.wait_ge(P.ts.h, ts0 + (m - 1) * NTl)
            src = w_ap[:, m * 128:m * 128 + mw]
            if KC > 1:
                src = src.rearrange("(kc p) m -> p kc m", p=128)
                dst = wb[par][:, 0:KC * mw].rearrange("p (kc m) -> p kc m", kc=KC)
            else:
                dst = wb[par][0:din, 0:mw]
            P.dw.bump(g.dma_start(dst, src), 16)
    w_base = P.dw.n - 16 * MC

    bank_gate = [None] * n_banks

    @block.tensor
    def _(t):
        if rhs_ready is not None:
            t.wait_ge(rhs_ready[0], rhs_ready[1])
        if wait_extra is not None:
            t.wait_ge(wait_extra[0], wait_extra[1])
        for m in range(MC):
            par = m % 2
            mw = min(128, dout - m * 128)
            t.wait_ge(P.dw.h, w_base + 16 * (m + 1))
            for nt in range(NTl):
                gi = m * NTl + nt
                bank = gi % n_banks
                if bank_gate[bank] is not None:
                    t.wait_ge(*bank_gate[bank])
                for k in range(KC):
                    last = (k == KC - 1) and extra_mm is None
                    c0 = k * rhs_stride + rhs_col0 + nt * tile_n
                    ins = t.matmul(
                        banks[bank][0:mw, :],
                        wb[par][:, k * mw:(k + 1) * mw],
                        rhs_bf[:, c0: c0 + tile_n],
                        start=(k == 0), stop=last)
                if extra_mm is not None:
                    ins = extra_mm(t, m, nt, banks[bank][0:mw, :])
                P.ts.bump(ins, 1)
                bank_gate[bank] = evac.plan(m, nt, banks[bank][0:mw, :], P.ts.n)
    evac.emit(block, tile_n=tile_n)



class Sched:
    ENGS = ("tensor", "vector", "scalar", "gpsimd", "sync")

    def __init__(self, P):
        self.P = P
        self.ops = {e: [] for e in self.ENGS}

    def add(self, eng, fn, waits=(), bump=None, k=1):
        val = bump.reserve(k) if bump is not None else None
        self.ops[eng].append((fn, [w for w in waits if w is not None],
                              bump.h if bump is not None else None, k))
        return val

    def emit(self, block):
        for eng in self.ENGS:
            if not self.ops[eng]:
                continue

            def section(e, eng=eng):
                for fn, waits, bh, k in self.ops[eng]:
                    for (h, v) in waits:
                        e.wait_ge(h, v)
                    ins = fn(e)
                    if bh is not None:
                        ins.then_inc(bh, k)
            getattr(block, eng)(section)


def emit_join(P, block, eng, waits, sem):
    def section(e):
        for (h, v) in waits:
            e.wait_ge(h, v)
        e.nop().then_inc(sem.h, 1)
    getattr(block, eng)(section)
    sem.n += 1
    return (sem.h, sem.n)



def emit_L3_attn(P, x_fm, xn_bf, x_gate):
    Wqkv = P.din("L3a_Wqkv", [D, 3 * D])
    Wo = P.din("L3a_Wo", [D, D])
    bqk = P.din("L3a_bqk", [128, 16])
    bv = P.din("L3a_bv", [1, D])
    bo = P.din("L3a_bo", [128, DC])
    gg = P.din("L3a_g", [128, DC])
    bb = P.din("L3a_b", [128, DC])

    with contextlib.ExitStack() as st:
        with P.nc.Block() as block:
            bqk_sb = P.sb("l3a_bqk", [128, 16], F32, st)
            bo_sb = P.sb("l3a_bo", [128, DC], F32, st)
            g_sb = P.sb("l3a_g", [128, DC], F32, st)
            b_sb = P.sb("l3a_b", [128, DC], F32, st)
            bvrow = P.sb("l3a_bvrow", [1, D], BF16, st)
            bvbc = P.sb("l3a_bvbc", [128, D], BF16, st)
            wqb = [P.sb(f"l3a_wq{i}", [128, DC * 128], BF16, st) for i in range(2)]
            wkb = [P.sb(f"l3a_wk{i}", [128, DC * 128], BF16, st) for i in range(2)]
            wvb = [P.sb(f"l3a_wv{i}", [128, DC * 128], BF16, st) for i in range(2)]
            q_sb = P.sb("l3a_q", [128, T], BF16, st)
            k_sb = P.sb("l3a_k", [128, T], BF16, st)
            v_tm = P.sb("l3a_v", [128, 16 * 130], BF16, st)
            ptile = P.sb("l3a_p", [128, 2 * 512], BF16, st)
            zrow = P.sb("l3a_z", [1, 512], F32, st)
            rz = P.sb("l3a_rz", [1, 512], F32, st)
            rzb = P.sb("l3a_rzb", [64, 512], F32, st)
            attn = P.sb("l3a_attn", [128, DC * T], BF16, st)
            wob = [P.sb(f"l3a_wo{i}", [128, DC * 128], BF16, st) for i in range(2)]

            @block.gpsimd
            def _(g):
                P.dw.bump(g.dma_start(bqk_sb[:], bqk), 16)
                P.dw.bump(g.dma_start(bo_sb[:], bo), 16)
                P.dw.bump(g.dma_start(g_sb[:], gg), 16)
                P.dw.bump(g.dma_start(b_sb[:], bb), 16)
                P.dw.bump(g.dma_start(bvrow[:], bv), 16)
            small_v = P.dw.n

            xng = emit_ln_fm(P, block, x_fm, xn_bf, x_gate=x_gate)
            gate_inner = emit_join(P, block, "vector",
                                   [(P.vs.h, xng), (P.dw.h, small_v)], P.vs)
            xng2 = emit_ln_fm(P, block, None, xn_bf, eps=1e-5, gamma_col=g_sb,
                              beta_col=b_sb, x_gate=gate_inner)
            xn_rdy = (P.vs.h, xng2)

            pst = contextlib.ExitStack()
            scps = [P.ps(f"l3a_sc{i}", [128, 512], F32, pst) for i in range(2)]
            outps = [P.ps(f"l3a_out{i}", [65, 512], F32, pst) for i in range(2)]
            rzps = P.ps("l3a_rzps", [64, 512], F32, pst)
            vps = [P.ps(f"l3a_vps{i}", [128, 128], F32, pst) for i in range(2)]

            S = Sched(P)
            ones_row = P.consts["c_ones_row"]

            gate_scps = [None, None]
            gate_outps = [None, None]
            gate_rzps = None
            gate_vps = [None, None]
            for half in range(2):
                tv = S.add("tensor", lambda t, half=half: t.matmul(
                    scps[0][:], ones_row[:], bvrow[0:1, half * 512:(half + 1) * 512],
                    start=True, stop=True),
                    waits=[(P.dw.h, small_v), xn_rdy, gate_scps[0]], bump=P.ts)
                sv = S.add("scalar", lambda s, half=half: s.copy(
                    bvbc[:, half * 512:(half + 1) * 512], scps[0][:]),
                    waits=[(P.ts.h, tv)], bump=P.ss)
                gate_scps[0] = (P.ss.h, sv)
            vset = S.add("vector", lambda v: v.memset(v_tm[:], 1.0), bump=P.vs)

            w_loaded = []
            qkv_done = []
            oevac_last = None
            for hg in range(8):
                par = hg % 2
                def wload(g, hg=hg, par=par):
                    srcs = ((wqb[par], 0), (wkb[par], D), (wvb[par], 2 * D))
                    for i, (wbuf, c0) in enumerate(srcs):
                        ins = g.dma_start(
                            wbuf[:].rearrange("p (kc m) -> p kc m", kc=DC),
                            Wqkv[:, c0 + hg * 128: c0 + hg * 128 + 128]
                            .rearrange("(kc p) m -> p kc m", p=128))
                        if i < 2:
                            ins.then_inc(P.dw.h, 16)
                    return ins
                P.dw.reserve(32)
                dv = S.add("gpsimd", wload,
                           waits=[(P.ts.h, qkv_done[hg - 2])] if hg >= 2 else [],
                           bump=P.dw, k=16)
                w_loaded.append(dv)

                for which, wbuf, dst, bcol in (("q", wqb[par], q_sb, hg),
                                               ("k", wkb[par], k_sb, 8 + hg)):
                    for nt in range(NT):
                        bank = nt % 2
                        def mmf(t, wbuf=wbuf, nt=nt, bank=bank):
                            for kc in range(DC):
                                ins = t.matmul(
                                    scps[bank][:],
                                    wbuf[:, kc * 128:(kc + 1) * 128],
                                    xn_bf[:, kc * T + nt * 512: kc * T + nt * 512 + 512],
                                    start=(kc == 0), stop=(kc == DC - 1))
                            return ins
                        tv = S.add("tensor", mmf,
                                   waits=[xn_rdy, (P.dw.h, dv), gate_scps[bank]],
                                   bump=P.ts)
                        sv = S.add("vector", lambda v, dst=dst, nt=nt, bank=bank, bcol=bcol:
                                   v.tensor_scalar(dst[:, nt * 512:(nt + 1) * 512],
                                                   scps[bank][:],
                                                   bqk_sb[:, bcol:bcol + 1], None, ALU.add),
                                   waits=[(P.ts.h, tv)], bump=P.vs)
                        gate_scps[bank] = (P.vs.h, sv)

                for tt in range(16):
                    bank = tt % 2
                    def vmmf(t, tt=tt, bank=bank, par=par):
                        for kc in range(DC):
                            ins = t.matmul(
                                vps[bank][:],
                                xn_bf[:, kc * T + tt * 128: kc * T + tt * 128 + 128],
                                wvb[par][:, kc * 128:(kc + 1) * 128],
                                start=(kc == 0), stop=(kc == DC - 1))
                        return ins
                    tv = S.add("tensor", vmmf,
                               waits=[xn_rdy, (P.dw.h, dv), gate_vps[bank]],
                               bump=P.ts)
                    def vev(v, tt=tt, bank=bank, hg=hg):
                        v.tensor_tensor(v_tm[:, tt * 130: tt * 130 + 64],
                                        vps[bank][:, 0:64],
                                        bvbc[:, hg * 128: hg * 128 + 64], ALU.add)
                        return v.tensor_tensor(v_tm[:, tt * 130 + 65: tt * 130 + 129],
                                               vps[bank][:, 64:128],
                                               bvbc[:, hg * 128 + 64: hg * 128 + 128],
                                               ALU.add)
                    vv = S.add("vector", vev,
                               waits=[(P.ts.h, tv), (P.vs.h, vset)], bump=P.vs)
                    gate_vps[bank] = (P.vs.h, vv)
                qkv_done.append(P.ts.n)

                for head in range(2):
                    hb = head * 64
                    for qg in range(NT):
                        grp = head * NT + qg
                        ob = grp % 2
                        pv_last = None
                        exps = {}
                        def pv_emit(kt, head=head, ob=ob):
                            sb_ = kt % 2
                            pv_waits = [(P.ss.h, exps[kt])]
                            if kt == 0:
                                pv_waits.append(gate_outps[ob])
                            return S.add("tensor",
                                         lambda t, kt=kt, head=head, ob=ob, sb_=sb_:
                                         t.matmul(outps[ob][:],
                                                  v_tm[:, kt * 130 + head * 65:
                                                       kt * 130 + head * 65 + 65],
                                                  ptile[:, sb_ * 512:(sb_ + 1) * 512],
                                                  start=(kt == 0), stop=(kt == 15)),
                                         waits=pv_waits, bump=P.ts)
                        for kt in range(16):
                            sb_ = kt % 2
                            tv = S.add("tensor", lambda t, hb=hb, kt=kt, qg=qg, sb_=sb_:
                                       t.matmul(scps[sb_][:],
                                                k_sb[hb:hb + 64, kt * 128:(kt + 1) * 128],
                                                q_sb[hb:hb + 64, qg * 512:(qg + 1) * 512],
                                                start=True, stop=True),
                                       waits=[gate_scps[sb_]], bump=P.ts)
                            ev = S.add("scalar", lambda s, sb_=sb_:
                                       s.activation(ptile[:, sb_ * 512:(sb_ + 1) * 512],
                                                    scps[sb_][:], AF.Exp,
                                                    scale=0.125, bias=P.consts["c_nb10"][:]),
                                       waits=[(P.ts.h, tv)], bump=P.ss)
                            exps[kt] = ev
                            gate_scps[sb_] = (P.ss.h, ev)
                            if kt >= 1:
                                pv_last = pv_emit(kt - 1)
                                gate_scps[(kt - 1) % 2] = (P.ts.h, P.ts.n)
                        pv_last = pv_emit(15)
                        gate_scps[1] = (P.ts.h, P.ts.n)
                        zv = S.add("scalar", lambda s, ob=ob: s.copy(zrow[:], outps[ob][64:65, :]),
                                   waits=[(P.ts.h, pv_last)], bump=P.ss)
                        rv = S.add("vector", lambda v: v.reciprocal(rz[:], zrow[:]),
                                   waits=[(P.ss.h, zv)], bump=P.vs)
                        bcv = S.add("tensor", lambda t: t.matmul(
                            rzps[:], P.consts["c_ones_row_f"][0:1, 0:64], rz[:],
                            start=True, stop=True),
                            waits=[(P.vs.h, rv), gate_rzps], bump=P.ts)
                        rbv = S.add("scalar", lambda s: s.copy(rzb[:], rzps[:]),
                                    waits=[(P.ts.h, bcv)], bump=P.ss)
                        gate_rzps = (P.ss.h, rbv)
                        oev = S.add("vector", lambda v, hb=hb, hg=hg, qg=qg, ob=ob:
                                    v.tensor_tensor(
                                        attn[hb:hb + 64, hg * T + qg * 512: hg * T + qg * 512 + 512],
                                        outps[ob][0:64, :], rzb[:], ALU.mult),
                                    waits=[(P.ss.h, rbv)], bump=P.vs)
                        gate_outps[ob] = (P.vs.h, oev)
                        oevac_last = oev
            S.emit(block)
            pst.close()

            ev = Evac(P, "resid", dst=x_fm, bias=bo_sb, dtokens=T)
            emit_proj(P, block, Wo, D, D, attn, T, ev,
                      rhs_ready=(P.vs.h, oevac_last), wtag="l3aw", stack=st, wb=wob)
    return



def emit_L0_attn(P, x_fm, xn_bf, x_gate):
    Wq = P.din("L0a_Wq", [D, D])
    Wk = P.din("L0a_Wk", [D, D])
    Wv = P.din("L0a_Wv", [D, D])
    WoT = P.din("L0a_WoT", [D, D])
    qb = P.din("L0a_qb", [128, DC])
    kb = P.din("L0a_kb", [128, DC])
    bv = P.din("L0a_bv", [1, D])
    bo = P.din("L0a_bo", [128, DC])
    porder = [2 * i for i in range(8)] + [2 * i + 1 for i in range(8)]

    with contextlib.ExitStack() as st:
        with P.nc.Block() as block:
            wq_sb = P.sb("l0a_wq", [128, DC * 1024], BF16, st)
            wk_sb = P.sb("l0a_wk", [128, DC * 1024], BF16, st)
            wv_sb = P.sb("l0a_wv", [128, DC * 1024], BF16, st)
            wo_sb = P.sb("l0a_wo", [128, DC * 1024], BF16, st)
            qb_sb = P.sb("l0a_qbs", [128, DC], F32, st)
            kb_sb = P.sb("l0a_kbs", [128, DC], F32, st)
            bo_sb = P.sb("l0a_bos", [128, DC], F32, st)
            bvrow = P.sb("l0a_bvrow", [1, D], BF16, st)
            bvbc = P.sb("l0a_bvbc", [128, D], BF16, st)
            qt = P.sb("l0a_qt", [64, 2048], BF16, st)
            kt = P.sb("l0a_kt", [64, 2048], BF16, st)
            v_tm = P.sb("l0a_v", [128, 16 * 65], BF16, st)
            ptile = P.sb("l0a_p", [128, 2 * 512], BF16, st)
            zrow = P.sb("l0a_z", [1, 512], F32, st)
            rz = P.sb("l0a_rz", [1, 512], F32, st)
            rzb = P.sb("l0a_rzb", [64, 512], F32, st)
            attn_h = P.sb("l0a_attn", [128, DC * 128], BF16, st)

            @block.gpsimd
            def _(g):
                for dst, src in ((wq_sb, Wq), (wk_sb, Wk), (wv_sb, Wv), (wo_sb, WoT)):
                    P.dw.bump(g.dma_start(
                        dst[:].rearrange("p (kc m) -> p kc m", kc=DC),
                        src.rearrange("(kc p) m -> p kc m", p=128)), 16)
                P.dw.bump(g.dma_start(qb_sb[:], qb), 16)
                P.dw.bump(g.dma_start(kb_sb[:], kb), 16)
                P.dw.bump(g.dma_start(bo_sb[:], bo), 16)
                P.dw.bump(g.dma_start(bvrow[:], bv), 16)
            w_v = P.dw.n

            xng = emit_ln_fm(P, block, x_fm, xn_bf, x_gate=x_gate)
            xn_rdy = (P.vs.h, xng)

            pst = contextlib.ExitStack()
            scps = [P.ps(f"l0a_sc{i}", [128, 512], F32, pst) for i in range(2)]
            outps = [P.ps(f"l0a_out{i}", [65, 512], F32, pst) for i in range(2)]
            rzps = P.ps("l0a_rzps", [64, 512], F32, pst)
            vps = [P.ps(f"l0a_vps{i}", [128, 512], F32, pst) for i in range(2)]
            wops = P.ps("l0a_wops", [128, 128], F32, pst)

            S = Sched(P)
            ones_row = P.consts["c_ones_row"]
            onesf = P.consts["c_ones_row_f"]

            gate_scps = [None, None]
            gate_outps = [None, None]
            gate_rzps = None
            gate_vps = [None, None]
            gate_wops = None
            for half in range(2):
                tv = S.add("tensor", lambda t, half=half: t.matmul(
                    scps[0][:], ones_row[:], bvrow[0:1, half * 512:(half + 1) * 512],
                    start=True, stop=True),
                    waits=[(P.dw.h, w_v), xn_rdy, gate_scps[0]], bump=P.ts)
                sv = S.add("scalar", lambda s, half=half: s.copy(
                    bvbc[:, half * 512:(half + 1) * 512], scps[0][:]),
                    waits=[(P.ts.h, tv)], bump=P.ss)
                gate_scps[0] = (P.ss.h, sv)
            vset = S.add("vector", lambda v: v.memset(v_tm[:], 1.0), bump=P.vs)

            for h in range(16):
                for wsb, dst, bcol_sb, order in ((wq_sb, qt, qb_sb, "porder"),
                                                 (wk_sb, kt, kb_sb, "nat")):
                    for j in range(8):
                        bank = j % 2
                        def mmf(t, wsb=wsb, j=j, bank=bank, h=h):
                            for kc in range(DC):
                                ins = t.matmul(
                                    scps[bank][:, 0:128],
                                    wsb[:, kc * 1024 + j * 128: kc * 1024 + j * 128 + 128],
                                    xn_bf[:, kc * T + h * 128: kc * T + h * 128 + 128],
                                    start=(kc == 0), stop=(kc == DC - 1))
                            return ins
                        tv = S.add("tensor", mmf,
                                   waits=[xn_rdy, (P.dw.h, w_v), gate_scps[bank]],
                                   bump=P.ts)
                        if order == "porder":
                            p0, p1 = j, 8 + j
                        else:
                            p0, p1 = 2 * j, 2 * j + 1
                        def qev(v, dst=dst, j=j, bank=bank, bcol_sb=bcol_sb, p0=p0, p1=p1):
                            v.tensor_scalar(dst[:, p0 * 128:(p0 + 1) * 128],
                                            scps[bank][0:64, 0:128],
                                            bcol_sb[0:64, j:j + 1], None, ALU.add)
                            return v.tensor_scalar(dst[:, p1 * 128:(p1 + 1) * 128],
                                                   scps[bank][64:128, 0:128],
                                                   bcol_sb[64:128, j:j + 1], None, ALU.add)
                        vv = S.add("vector", qev, waits=[(P.ts.h, tv)], bump=P.vs)
                        gate_scps[bank] = (P.vs.h, vv)
                for half in range(2):
                    bank = half % 2
                    def vmmf(t, half=half, bank=bank, h=h):
                        for kc in range(DC):
                            ins = t.matmul(
                                vps[bank][:],
                                xn_bf[:, kc * T + h * 128: kc * T + h * 128 + 128],
                                wv_sb[:, kc * 1024 + half * 512: kc * 1024 + half * 512 + 512],
                                start=(kc == 0), stop=(kc == DC - 1))
                        return ins
                    tv = S.add("tensor", vmmf,
                               waits=[xn_rdy, (P.dw.h, w_v), gate_vps[bank]],
                               bump=P.ts)
                    def vev(v, half=half, bank=bank):
                        for i in range(8):
                            c16 = half * 8 + i
                            ins = v.tensor_tensor(
                                v_tm[:, c16 * 65: c16 * 65 + 64],
                                vps[bank][:, i * 64: i * 64 + 64],
                                bvbc[:, c16 * 64: c16 * 64 + 64], ALU.add)
                        return ins
                    vv = S.add("vector", vev,
                               waits=[(P.ts.h, tv), (P.vs.h, vset)], bump=P.vs)
                    gate_vps[bank] = (P.vs.h, vv)
                oevac_h = None
                for qg in range(4):
                    ob = qg % 2
                    pv_last = None
                    exps = {}
                    def pv_emit(c16k, ob=ob):
                        sb_ = c16k % 2
                        pv_waits = [(P.ss.h, exps[c16k])]
                        if c16k == 0:
                            pv_waits.append(gate_outps[ob])
                        return S.add("tensor", lambda t, c16k=c16k, ob=ob, sb_=sb_:
                                     t.matmul(outps[ob][:],
                                              v_tm[:, c16k * 65: c16k * 65 + 65],
                                              ptile[:, sb_ * 512:(sb_ + 1) * 512],
                                              start=(c16k == 0), stop=(c16k == 15)),
                                     waits=pv_waits, bump=P.ts)
                    for c16k in range(16):
                        sb_ = c16k % 2
                        tv = S.add("tensor", lambda t, c16k=c16k, qg=qg, sb_=sb_:
                                   t.matmul(scps[sb_][:],
                                            kt[:, c16k * 128:(c16k + 1) * 128],
                                            qt[:, qg * 512:(qg + 1) * 512],
                                            start=True, stop=True),
                                   waits=[gate_scps[sb_]], bump=P.ts)
                        ev = S.add("scalar", lambda s, sb_=sb_:
                                   s.activation(ptile[:, sb_ * 512:(sb_ + 1) * 512],
                                                scps[sb_][:], AF.Exp,
                                                scale=0.125, bias=P.consts["c_nb10"][:]),
                                   waits=[(P.ts.h, tv)], bump=P.ss)
                        exps[c16k] = ev
                        gate_scps[sb_] = (P.ss.h, ev)
                        if c16k >= 1:
                            pv_last = pv_emit(c16k - 1)
                            gate_scps[(c16k - 1) % 2] = (P.ts.h, P.ts.n)
                    pv_last = pv_emit(15)
                    gate_scps[1] = (P.ts.h, P.ts.n)
                    zv = S.add("scalar", lambda s, ob=ob: s.copy(zrow[:], outps[ob][64:65, :]),
                               waits=[(P.ts.h, pv_last)], bump=P.ss)
                    rv = S.add("vector", lambda v: v.reciprocal(rz[:], zrow[:]),
                               waits=[(P.ss.h, zv)], bump=P.vs)
                    bcv = S.add("tensor", lambda t: t.matmul(
                        rzps[:], onesf[0:1, 0:64], rz[:], start=True, stop=True),
                        waits=[(P.vs.h, rv), gate_rzps], bump=P.ts)
                    rbv = S.add("scalar", lambda s: s.copy(rzb[:], rzps[:]),
                                waits=[(P.ts.h, bcv)], bump=P.ss)
                    gate_rzps = (P.ss.h, rbv)
                    def oev_f(v, qg=qg, ob=ob):
                        for i in range(4):
                            c16q = porder[qg * 4 + i]
                            ins = v.tensor_tensor(
                                attn_h[(c16q % 2) * 64:(c16q % 2) * 64 + 64,
                                       (c16q // 2) * 128:(c16q // 2) * 128 + 128],
                                outps[ob][0:64, i * 128:(i + 1) * 128],
                                rzb[:, i * 128:(i + 1) * 128], ALU.mult)
                        return ins
                    oev = S.add("vector", oev_f, waits=[(P.ss.h, rbv)], bump=P.vs)
                    gate_outps[ob] = (P.vs.h, oev)
                    oevac_h = oev
                for m in range(8):
                    def womm(t, m=m, h=h):
                        for kc in range(DC):
                            ins = t.matmul(
                                wops[:],
                                wo_sb[:, kc * 1024 + m * 128: kc * 1024 + m * 128 + 128],
                                attn_h[:, kc * 128:(kc + 1) * 128],
                                start=(kc == 0), stop=(kc == DC - 1))
                        return ins
                    tv = S.add("tensor", womm,
                               waits=[(P.vs.h, oevac_h), gate_wops], bump=P.ts)
                    vv = S.add("vector", lambda v, m=m, h=h:
                               v.scalar_tensor_tensor(
                                   x_fm[:, m * T + h * 128: m * T + h * 128 + 128],
                                   wops[:], bo_sb[:, m:m + 1],
                                   x_fm[:, m * T + h * 128: m * T + h * 128 + 128],
                                   ALU.add, ALU.add),
                               waits=[(P.ts.h, tv)], bump=P.vs)
                    gate_wops = (P.vs.h, vv)
            S.emit(block)
            pst.close()
    return




def emit_L2_attn(P, x_fm, xn_bf, x_gate):
    Wqm = P.din("L2a_wq", [D, D])
    Wkd = P.din("L2a_wkd", [D, 64])
    Wvd = P.din("L2a_wvd", [D, 64])
    Wku = P.din("L2a_wku", [32, 128])
    Wvu = P.din("L2a_wvu", [32, 128])
    Wom = P.din("L2a_wo", [D, D])
    CMA = P.din("c_l2a_cmask", [128, 896])

    with contextlib.ExitStack() as st:
        with P.nc.Block() as block:
            wkd_sb = P.sb("l2a_wkd", [128, DC * 64], BF16, st)
            wvd_sb = P.sb("l2a_wvd", [128, DC * 64], BF16, st)
            wku_sb = P.sb("l2a_wku", [32, 128], BF16, st)
            wvu_sb = P.sb("l2a_wvu", [32, 128], BF16, st)
            cm_sb = P.sb("l2a_cmloc", [128, 896], BF16, st)
            wqb = [P.sb(f"l2a_wq{i}", [128, DC * 128], BF16, st) for i in range(2)]
            klat = P.sb("l2a_klat", [32, 2 * T], BF16, st)
            vlat = P.sb("l2a_vlat", [32, 2 * T], BF16, st)
            k_sb = P.sb("l2a_k", [128, 2 * T], BF16, st)
            v_tm = P.sb("l2a_v", [128, 16 * 129], BF16, st)
            q_h = P.sb("l2a_q", [128, T], BF16, st)
            ptile = P.sb("l2a_p", [128, 2 * 512], BF16, st)
            zrow = P.sb("l2a_z", [1, 512], F32, st)
            rz = P.sb("l2a_rz", [1, 512], F32, st)
            rzb = P.sb("l2a_rzb", [64, 512], F32, st)
            attn = P.sb("l2a_attn", [128, DC * T], BF16, st)
            wob = [P.sb(f"l2a_wo{i}", [128, DC * 128], BF16, st) for i in range(2)]

            @block.gpsimd
            def _(g):
                P.dw.bump(g.dma_start(
                    wkd_sb[:].rearrange("p (kc m) -> p kc m", kc=DC),
                    Wkd.rearrange("(kc p) m -> p kc m", p=128)), 16)
                P.dw.bump(g.dma_start(
                    wvd_sb[:].rearrange("p (kc m) -> p kc m", kc=DC),
                    Wvd.rearrange("(kc p) m -> p kc m", p=128)), 16)
                P.dw.bump(g.dma_start(wku_sb[:], Wku), 16)
                P.dw.bump(g.dma_start(wvu_sb[:], Wvu), 16)
                P.dw.bump(g.dma_start(cm_sb[:], CMA), 16)
            w_v = P.dw.n

            xng = emit_ln_fm(P, block, x_fm, xn_bf, x_gate=x_gate)
            xn_rdy = (P.vs.h, xng)

            pst = contextlib.ExitStack()
            scps = [P.ps(f"l2a_sc{i}", [128, 512], F32, pst) for i in range(2)]
            out1 = [P.ps(f"l2a_o1{i}", [65, 512], F32, pst) for i in range(2)]
            out2 = [P.ps(f"l2a_o2{i}", [64, 512], F32, pst) for i in range(2)]
            rzps = P.ps("l2a_rzps", [64, 512], F32, pst)
            latps = P.ps("l2a_lat", [64, 512], F32, pst)

            S = Sched(P)
            onesf = P.consts["c_ones_row_f"]
            gate_scps = [None, None]
            gate_out = [None, None]
            gate_rzps = None
            gate_latps = None

            for which, wsb, dst in (("k", wkd_sb, klat), ("v", wvd_sb, vlat)):
                for nt in range(NT):
                    def lmm(t, wsb=wsb, nt=nt):
                        for kc in range(DC):
                            ins = t.matmul(
                                latps[:],
                                wsb[:, kc * 64:(kc + 1) * 64],
                                xn_bf[:, kc * T + nt * 512: kc * T + nt * 512 + 512],
                                start=(kc == 0), stop=(kc == DC - 1))
                        return ins
                    tv = S.add("tensor", lmm,
                               waits=[xn_rdy, (P.dw.h, w_v), gate_latps], bump=P.ts)
                    def lev(v, dst=dst, nt=nt):
                        v.tensor_copy(dst[:, nt * 512:(nt + 1) * 512], latps[0:32, :])
                        return v.tensor_copy(dst[:, T + nt * 512: T + nt * 512 + 512],
                                             latps[32:64, :])
                    vv = S.add("vector", lev, waits=[(P.ts.h, tv)], bump=P.vs)
                    gate_latps = (P.vs.h, vv)

            kv_done = []
            for hk in range(2):
                for nt in range(NT):
                    def kmm(t, hk=hk, nt=nt):
                        return t.matmul(
                            scps[nt % 2][:],
                            wku_sb[:],
                            klat[:, hk * T + nt * 512: hk * T + nt * 512 + 512],
                            start=True, stop=True)
                    tv = S.add("tensor", kmm,
                               waits=[(P.dw.h, w_v), gate_scps[nt % 2]], bump=P.ts)
                    sv = S.add("scalar", lambda s, hk=hk, nt=nt:
                               s.copy(k_sb[:, hk * T + nt * 512: hk * T + nt * 512 + 512],
                                      scps[nt % 2][:]),
                               waits=[(P.ts.h, tv)], bump=P.ss)
                    gate_scps[nt % 2] = (P.ss.h, sv)
            vset = S.add("vector", lambda v: v.memset(v_tm[:], 1.0), bump=P.vs)

            oev_last = None
            for hk in range(2):
                for tt in range(16):
                    def vmm(t, hk=hk, tt=tt):
                        return t.matmul(
                            scps[tt % 2][:, 0:128],
                            vlat[:, hk * T + tt * 128: hk * T + tt * 128 + 128],
                            wvu_sb[:],
                            start=True, stop=True)
                    tv = S.add("tensor", vmm,
                               waits=[(P.vs.h, vset), gate_scps[tt % 2]], bump=P.ts)
                    def vev(v, tt=tt):
                        v.tensor_copy(v_tm[:, tt * 129: tt * 129 + 64],
                                      scps[tt % 2][:, 0:64])
                        return v.tensor_copy(v_tm[:, tt * 129 + 65: tt * 129 + 129],
                                             scps[tt % 2][:, 64:128])
                    vv = S.add("vector", vev, waits=[(P.ts.h, tv)], bump=P.vs)
                    gate_scps[tt % 2] = (P.vs.h, vv)

                for r in range(4):
                    par = r % 2
                    hh = hk * 4 + r
                    dv = S.add("gpsimd", lambda g, hk=hk, r=r, par=par: g.dma_start(
                        wqb[par][:].rearrange("p (kc m) -> p kc m", kc=DC),
                        Wqm[:, hk * 512 + r * 128: hk * 512 + r * 128 + 128]
                        .rearrange("(kc p) m -> p kc m", p=128)),
                        waits=[(P.ts.h, kv_done[hh - 2])] if hh >= 2 else [],
                        bump=P.dw, k=16)
                    for nt in range(NT):
                        def qmm(t, par=par, nt=nt):
                            for kc in range(DC):
                                ins = t.matmul(
                                    scps[nt % 2][:],
                                    wqb[par][:, kc * 128:(kc + 1) * 128],
                                    xn_bf[:, kc * T + nt * 512: kc * T + nt * 512 + 512],
                                    start=(kc == 0), stop=(kc == DC - 1))
                            return ins
                        tv = S.add("tensor", qmm,
                                   waits=[(P.dw.h, dv), gate_scps[nt % 2]], bump=P.ts)
                        sv = S.add("scalar", lambda s, nt=nt:
                                   s.copy(q_h[:, nt * 512:(nt + 1) * 512], scps[nt % 2][:]),
                                   waits=[(P.ts.h, tv)], bump=P.ss)
                        gate_scps[nt % 2] = (P.ss.h, sv)
                    kv_done.append(P.ts.n)

                    for qg in range(NT):
                        ob = qg % 2
                        nkt = 4 * qg + 4
                        pv_last = None
                        gates_p = {}
                        def pv_emit(kt, ob=ob, nkt=nkt):
                            sb_ = kt % 2
                            pv_waits = [gates_p[kt]]
                            if kt == 0:
                                pv_waits.append(gate_out[ob])
                            S.add("tensor", lambda t, kt=kt, ob=ob, sb_=sb_, nkt=nkt:
                                  t.matmul(out1[ob][:],
                                           v_tm[:, kt * 129: kt * 129 + 65],
                                           ptile[:, sb_ * 512:(sb_ + 1) * 512],
                                           start=(kt == 0), stop=(kt == nkt - 1)),
                                  waits=pv_waits, bump=P.ts)
                            return S.add("tensor",
                                         lambda t, kt=kt, ob=ob, sb_=sb_, nkt=nkt:
                                         t.matmul(out2[ob][:],
                                                  v_tm[:, kt * 129 + 65: kt * 129 + 129],
                                                  ptile[:, sb_ * 512:(sb_ + 1) * 512],
                                                  start=(kt == 0), stop=(kt == nkt - 1)),
                                         waits=[], bump=P.ts)
                        for kt in range(nkt):
                            sb_ = kt % 2
                            tv = S.add("tensor", lambda t, hk=hk, kt=kt, qg=qg, sb_=sb_:
                                       t.matmul(scps[sb_][:],
                                                k_sb[:, hk * T + kt * 128: hk * T + kt * 128 + 128],
                                                q_h[:, qg * 512:(qg + 1) * 512],
                                                start=True, stop=True),
                                       waits=[gate_scps[sb_]], bump=P.ts)
                            ev = S.add("scalar", lambda s, sb_=sb_:
                                       s.activation(ptile[:, sb_ * 512:(sb_ + 1) * 512],
                                                    scps[sb_][:], AF.Exp,
                                                    scale=0.0883883476483184,
                                                    bias=P.consts["c_nb10"][:]),
                                       waits=[(P.ts.h, tv)], bump=P.ss)
                            gate_scps[sb_] = (P.ss.h, ev)
                            gates_p[kt] = (P.ss.h, ev)
                            if kt >= 4 * qg:
                                j = kt - 4 * qg
                                mv = S.add("vector", lambda v, sb_=sb_, j=j:
                                           v.tensor_tensor(
                                               ptile[:, sb_ * 512:(sb_ + 1) * 512],
                                               ptile[:, sb_ * 512:(sb_ + 1) * 512],
                                               cm_sb[:, 384 - j * 128: 896 - j * 128],
                                               ALU.mult),
                                           waits=[(P.ss.h, ev)], bump=P.vs)
                                gate_scps[sb_] = (P.vs.h, mv)
                                gates_p[kt] = (P.vs.h, mv)
                            if kt >= 1:
                                pv_last = pv_emit(kt - 1)
                                gate_scps[(kt - 1) % 2] = (P.ts.h, P.ts.n)
                        pv_last = pv_emit(nkt - 1)
                        gate_scps[(nkt - 1) % 2] = (P.ts.h, P.ts.n)
                        zv = S.add("scalar", lambda s, ob=ob: s.copy(zrow[:], out1[ob][64:65, :]),
                                   waits=[(P.ts.h, pv_last)], bump=P.ss)
                        rv = S.add("vector", lambda v: v.reciprocal(rz[:], zrow[:]),
                                   waits=[(P.ss.h, zv)], bump=P.vs)
                        bcv = S.add("tensor", lambda t: t.matmul(
                            rzps[:], onesf[0:1, 0:64], rz[:], start=True, stop=True),
                            waits=[(P.vs.h, rv), gate_rzps], bump=P.ts)
                        rbv = S.add("scalar", lambda s: s.copy(rzb[:], rzps[:]),
                                    waits=[(P.ts.h, bcv)], bump=P.ss)
                        gate_rzps = (P.ss.h, rbv)
                        ch = hk * 4 + r
                        def oev_f(v, ch=ch, qg=qg, ob=ob):
                            v.tensor_tensor(
                                attn[0:64, ch * T + qg * 512: ch * T + qg * 512 + 512],
                                out1[ob][0:64, :], rzb[:], ALU.mult)
                            return v.tensor_tensor(
                                attn[64:128, ch * T + qg * 512: ch * T + qg * 512 + 512],
                                out2[ob][0:64, :], rzb[:], ALU.mult)
                        oev = S.add("vector", oev_f, waits=[(P.ss.h, rbv)], bump=P.vs)
                        gate_out[ob] = (P.vs.h, oev)
                        oev_last = oev
            S.emit(block)
            pst.close()

            ev = Evac(P, "resid", dst=x_fm, bias=None, dtokens=T)
            emit_proj(P, block, Wom, D, D, attn, T, ev,
                      rhs_ready=(P.vs.h, oev_last), wtag="l2aw", stack=st, wb=wob)
    return




def emit_quant(P, block, S, w_ap, kchunks, dout, dram_out, bufs, gates, tag):
    qbuf, absb, sgnb, qout, partials, thr, thcol, sps, bcps = bufs
    nslice = dout // 512
    nelem = kchunks * 128 * dout
    loaded = []
    for kc in range(kchunks):
        par = kc % 2
        gate = gates["qbuf"][par]
        dv = S.add("gpsimd", lambda g, kc=kc, par=par: g.dma_start(
            qbuf[par][:, 0:dout], w_ap[kc * 128:(kc + 1) * 128, :]),
            waits=[gate], bump=P.dw, k=16)
        av = S.add("scalar", lambda s, par=par: s.activation(
            absb[:, 0:dout], qbuf[par][:, 0:dout], AF.Abs),
            waits=[(P.dw.h, dv), gates["absb"]], bump=P.ss)
        gates["qbuf"][par] = (P.ss.h, av)

        def red(v, kc=kc):
            for sl in range(nslice):
                ins = v.reduce_sum(partials[:, kc * nslice + sl: kc * nslice + sl + 1],
                                   absb[:, sl * 512:(sl + 1) * 512], axis=AX.X)
            return ins
        rv = S.add("vector", red, waits=[(P.ss.h, av)], bump=P.vs)
        gates["absb"] = (P.vs.h, rv)
    tv = S.add("tensor", lambda t: t.matmul(
        sps[:], P.consts["c_ones_col_f"][:], partials[:, 0:kchunks * nslice],
        start=True, stop=True), waits=[gates["absb"]], bump=P.ts)
    sv = S.add("vector", lambda v: v.reduce_sum(thr[0:1, 0:1], sps[0:1, 0:kchunks * nslice],
                                                axis=AX.X),
               waits=[(P.ts.h, tv)], bump=P.vs)
    s2 = S.add("vector", lambda v: v.tensor_single_scalar(
        thr[0:1, 1:2], thr[0:1, 0:1], 0.5 / nelem, ALU.mult),
        waits=[(P.vs.h, sv)], bump=P.vs)
    s3 = S.add("vector", lambda v: v.tensor_single_scalar(
        thr[0:1, 1:2], thr[0:1, 1:2], 5e-7, ALU.add),
        waits=[(P.vs.h, s2)], bump=P.vs)
    bv = S.add("tensor", lambda t: t.matmul(
        bcps[:], P.consts["c_ones_row_f"][:], thr[0:1, 1:2], start=True, stop=True),
        waits=[(P.vs.h, s3)], bump=P.ts)
    cv = S.add("scalar", lambda s: s.copy(thcol[:], bcps[:]),
               waits=[(P.ts.h, bv)], bump=P.ss)
    th_gate = (P.ss.h, cv)
    for kc in range(kchunks):
        par = kc % 2
        dv = S.add("gpsimd", lambda g, kc=kc, par=par: g.dma_start(
            qbuf[par][:, 0:dout], w_ap[kc * 128:(kc + 1) * 128, :]),
            waits=[gates["qbuf"][par]], bump=P.dw, k=16)
        av = S.add("scalar", lambda s, par=par: s.activation(
            absb[:, 0:dout], qbuf[par][:, 0:dout], AF.Abs),
            waits=[(P.dw.h, dv), gates["absb"], th_gate], bump=P.ss)
        gv = S.add("scalar", lambda s, par=par: s.activation(
            sgnb[:, 0:dout], qbuf[par][:, 0:dout], AF.Sign),
            waits=[gates["sgnb"]], bump=P.ss)
        gates["qbuf"][par] = (P.ss.h, gv)
        mv = S.add("vector", lambda v: v.tensor_scalar(
            absb[:, 0:dout], absb[:, 0:dout], thcol[:], None, ALU.is_ge),
            waits=[(P.ss.h, av)], bump=P.vs)
        gates["absb"] = (P.vs.h, mv)
        qv = S.add("vector", lambda v: v.tensor_tensor(
            qout[:, 0:dout], sgnb[:, 0:dout], absb[:, 0:dout], ALU.mult),
            waits=[(P.vs.h, mv), (P.ss.h, gv), gates["qout"]], bump=P.vs)
        gates["sgnb"] = (P.vs.h, qv)
        gates["absb"] = (P.vs.h, qv)
        ov = S.add("sync", lambda sy, kc=kc: sy.dma_start(
            dram_out[kc * 128:(kc + 1) * 128, :], qout[:, 0:dout]),
            waits=[(P.vs.h, qv)], bump=P.dx, k=16)
        gates["qout"] = (P.dx.h, ov)
    return P.dx.n


def emit_L3_ffn(P, x_fm, xn_bf, x_gate):
    H3 = 4096
    HC = H3 // 128
    WfcT = P.din("L3f_WfcT", [D, H3])
    WprT = P.din("L3f_WprT", [H3, D])
    lnw = P.din("L3f_lnw", [128, HC])
    lnb = P.din("L3f_lnb", [128, HC])
    nc = P.nc
    qfc = nc.dram_tensor("l3f_qfc", [D, H3], BF16).ap()
    qpr = nc.dram_tensor("l3f_qpr", [H3, D], BF16).ap()

    with contextlib.ExitStack() as st:
        qsb = contextlib.ExitStack()
        with P.nc.Block() as block:
            qbuf = [P.sb(f"l3f_qb{i}", [128, H3], F32, qsb) for i in range(2)]
            absb = P.sb("l3f_abs", [128, H3], F32, qsb)
            sgnb = P.sb("l3f_sgn", [128, H3], F32, qsb)
            qout = P.sb("l3f_qo", [128, H3], BF16, qsb)
            partials = P.sb("l3f_part", [128, 64], F32, qsb)
            thr = P.sb("l3f_thr", [1, 2], F32, qsb)
            thcol = P.sb("l3f_thc", [128, 1], F32, qsb)
            qst = contextlib.ExitStack()
            sps = P.ps("l3f_sps", [1, 64], F32, qst)
            bcps = P.ps("l3f_bc", [128, 1], F32, qst)
            with nc.allow_low_precision("bitnet quantized weights are exact in bf16"):
                S = Sched(P)
                gates = dict(qbuf=[None, None], absb=None, sgnb=None, qout=None)
                emit_quant(P, block, S, WfcT, DC, H3, qfc,
                           (qbuf, absb, sgnb, qout, partials, thr, thcol, sps, bcps),
                           gates, "fc")
                S.emit(block)
                S2 = Sched(P)
                emit_quant(P, block, S2, WprT, HC, D, qpr,
                           (qbuf, absb, sgnb, qout, partials, thr, thcol, sps, bcps),
                           gates, "pr")
                S2.emit(block)
            qst.close()
        qsb.close()
        quant_done = P.dx.n

        with P.nc.Block() as block:
            lnw_sb = P.sb("l3f_lnw", [128, HC], F32, st)
            lnb_sb = P.sb("l3f_lnb", [128, HC], F32, st)
            h_bf = P.sb("l3f_h", [128, HC * 512], BF16, st)
            wbh = [P.sb(f"l3f_wh{i}", [128, DC * 128], BF16, st) for i in range(2)]
            wbp = [P.sb(f"l3f_wp{i}", [128, HC * 128], BF16, st) for i in range(2)]

            @block.gpsimd
            def _(g):
                P.dw.bump(g.dma_start(lnw_sb[:], lnw), 16)
                P.dw.bump(g.dma_start(lnb_sb[:], lnb), 16)
            small_v = P.dw.n

            xng = emit_ln_fm(P, block, x_fm, xn_bf, x_gate=x_gate)
            xn_rdy = (P.vs.h, xng)

            bank_st = contextlib.ExitStack()
            banksA = [P.ps(f"l3f_psA{i}", [128, 512], F32, bank_st) for i in range(2)]
            banksB = [P.ps(f"l3f_psB{i}", [128, 512], F32, bank_st) for i in range(2)]

            for tt in range(NT):
                evh = Evac(P, "copy", dst=h_bf, act=AF.Gelu, dtokens=512, engine="scalar")
                emit_proj(P, block, qfc, D, H3, xn_bf, 512, evh,
                          rhs_ready=xn_rdy, wtag=f"l3fh{tt}", stack=st,
                          wb=wbh, banks=banksA, rhs_stride=T, rhs_col0=tt * 512,
                          w_gate=(P.dx.h, quant_done))
                h_gate = (evh.sem.h, evh.sem.n)
                jg = emit_join(P, block, "scalar",
                               [h_gate, (P.dw.h, small_v)], P.ss)
                lng = emit_ln_fm(P, block, None, h_bf, eps=1e-5, gamma_col=lnw_sb,
                                 beta_col=lnb_sb, n_feat=H3, tokens=512, x_gate=jg)
                evp = Evac(P, "resid", dst=x_fm, bias=None, dtokens=T,
                           dst_col0=tt * 512)
                emit_proj(P, block, qpr, H3, D, h_bf, 512, evp,
                          rhs_ready=(P.vs.h, lng), wtag=f"l3fp{tt}", stack=st,
                          wb=wbp, banks=banksB, w_gate=(P.dx.h, quant_done))
            bank_st.close()
    return




BATCHER8 = [(0, 1), (2, 3), (4, 5), (6, 7), (0, 2), (1, 3), (4, 6), (5, 7),
            (1, 2), (5, 6), (0, 4), (1, 5), (2, 6), (3, 7), (2, 4), (3, 5),
            (1, 2), (3, 4), (5, 6)]
L2F_EXPERTS = list(range(9))
L2F_DEBUG = False


def emit_L2_ffn(P, x_fm, xn_bf, x_gate):
    HH = 2730
    HALF = [(0, 1408), (1408, 2730)]
    WgT = P.din("L2f_WgT", [D, 8])
    rbr = P.din("L2f_rb", [1, 8])
    W1T = P.din("L2f_W1T", [9 * D, HH])
    W2T = P.din("L2f_W2T", [9 * D, HH])
    W3T = P.din("L2f_W3T", [9 * HH, D])

    with contextlib.ExitStack() as st:
        with P.nc.Block() as block:
            dst_ = P.st if L2F_DEBUG else st
            g8 = P.sb("l2f_g8", [128, 128], F32, dst_)
            s8 = P.sb("l2f_s8", [128, 128], F32, dst_)
            w8 = P.sb("l2f_w8", [128, 144], F32, dst_)
            tmp16 = P.sb("l2f_tmp", [128, 16], F32, dst_)
            rec16 = P.sb("l2f_rec", [128, 16], F32, dst_)
            rbs = P.sb("l2f_rbrow", [1, 8], F32, dst_)
            acc = P.sb("l2f_acc", [128, 8 * 1024], F32, dst_)
            wg_sb = P.sb("l2f_wg", [128, DC * 8], BF16, st)
            rb_bc = P.sb("l2f_rbbc", [128, 8], F32, st)
            act = P.sb("l2f_act", [128, 11 * 1024], BF16, st)
            h1pp = P.sb("l2f_h1", [128, 2 * 512], BF16, st)
            w3h = P.sb("l2f_w3", [128, 11 * 1024], BF16, st)
            w12b = [P.sb(f"l2f_w12{i}", [128, DC * 128], BF16, st) for i in range(4)]
            tr_bf = P.sb("l2f_trb", [128, 128], BF16, st)

            @block.gpsimd
            def _(g):
                P.dw.bump(g.dma_start(
                    wg_sb[:].rearrange("p (kc m) -> p kc m", kc=DC),
                    WgT.rearrange("(kc p) m -> p kc m", p=128)), 16)
            w_v = P.dw.n

            xng = emit_ln_fm(P, block, x_fm, xn_bf, x_gate=x_gate)
            xn_rdy = (P.vs.h, xng)

            pst = contextlib.ExitStack()
            gps = P.ps("l2f_gps", [128, 8], F32, pst)
            rbps = P.ps("l2f_rbps", [128, 8], F32, pst)
            hps = [P.ps(f"l2f_hps{i}", [128, 512], F32, pst) for i in range(3)]
            eops = [P.ps(f"l2f_eops{i}", [128, 512], F32, pst) for i in range(2)]
            trps = P.ps("l2f_trps", [128, 128], BF16, pst)

            S = Sched(P)
            ident = P.consts["c_ident"]

            bv = S.add("tensor", lambda t: t.matmul(
                rbps[:], P.consts["c_ones_row_f"][:], rbr, start=True, stop=True),
                waits=[], bump=P.ts)
            S.ops["tensor"].pop()
            P.ts.n -= 1
            dv0 = S.add("gpsimd", lambda g: g.dma_start(rbs[:], rbr), bump=P.dw, k=16)
            bv = S.add("tensor", lambda t: t.matmul(
                rbps[:], P.consts["c_ones_row_f"][:], rbs[:], start=True, stop=True),
                waits=[(P.dw.h, dv0), xn_rdy], bump=P.ts)
            bcv = S.add("scalar", lambda s: s.copy(rb_bc[:], rbps[:]),
                        waits=[(P.ts.h, bv)], bump=P.ss)
            gate_gps = None
            last_gate = None
            for tt in range(16):
                def gmm(t, tt=tt):
                    for kc in range(DC):
                        ins = t.matmul(
                            gps[:],
                            xn_bf[:, kc * T + tt * 128: kc * T + tt * 128 + 128],
                            wg_sb[:, kc * 8:(kc + 1) * 8],
                            start=(kc == 0), stop=(kc == DC - 1))
                    return ins
                tv = S.add("tensor", gmm, waits=[xn_rdy, (P.dw.h, w_v), gate_gps],
                           bump=P.ts)
                gv = S.add("vector", lambda v, tt=tt: v.tensor_tensor(
                    bass.AP(g8, tt, [[128, 128], [16, 8]]),
                    gps[:], rb_bc[:], ALU.add),
                    waits=[(P.ts.h, tv), (P.ss.h, bcv)], bump=P.vs)
                gate_gps = (P.vs.h, gv)
                last_gate = gv

            def vchain(fn):
                pv = S.add("vector", fn, waits=[(P.vs.h, P.vs.n)], bump=P.vs)
                return pv
            S.add("vector", lambda v: v.tensor_copy(s8[:], g8[:]),
                  waits=[(P.vs.h, last_gate)], bump=P.vs)
            for (i, j) in BATCHER8:
                i0, i1 = i * 16, (i + 1) * 16
                j0, j1 = j * 16, (j + 1) * 16
                vchain(lambda v, i0=i0, i1=i1, j0=j0, j1=j1:
                       v.tensor_tensor(tmp16[:], s8[:, i0:i1], s8[:, j0:j1], ALU.min))
                vchain(lambda v, i0=i0, i1=i1, j0=j0, j1=j1:
                       v.tensor_tensor(s8[:, j0:j1], s8[:, i0:i1], s8[:, j0:j1], ALU.max))
                sv = vchain(lambda v, i0=i0, i1=i1: v.tensor_copy(s8[:, i0:i1], tmp16[:]))
            ev = S.add("scalar", lambda s: s.activation(w8[:, 0:128], g8[:], AF.Exp),
                       waits=[(P.vs.h, sv)], bump=P.ss)
            S.add("vector", lambda v: v.tensor_copy(w8[:, 8 * 16:9 * 16],
                                                    s8[:, 4 * 16:5 * 16]),
                  waits=[(P.ss.h, ev)], bump=P.vs)
            for e in range(8):
                vchain(lambda v, e=e: v.tensor_tensor(
                    s8[:, e * 16:(e + 1) * 16], g8[:, e * 16:(e + 1) * 16],
                    w8[:, 8 * 16:9 * 16], ALU.is_ge))
            for e in range(8):
                vchain(lambda v, e=e: v.tensor_tensor(
                    w8[:, e * 16:(e + 1) * 16], w8[:, e * 16:(e + 1) * 16],
                    s8[:, e * 16:(e + 1) * 16], ALU.mult))
            vchain(lambda v: v.tensor_copy(tmp16[:], w8[:, 0:16]))
            for e in range(1, 8):
                vchain(lambda v, e=e: v.tensor_tensor(
                    tmp16[:], tmp16[:], w8[:, e * 16:(e + 1) * 16], ALU.add))
            vchain(lambda v: v.reciprocal(rec16[:], tmp16[:]))
            for e in range(8):
                wv = vchain(lambda v, e=e: v.tensor_tensor(
                    w8[:, e * 16:(e + 1) * 16], w8[:, e * 16:(e + 1) * 16],
                    rec16[:], ALU.mult))
            w_rdy = (P.vs.h, wv)

            gate_acc = None
            gate_act = None
            gate_w3 = None
            gate_h1 = [None, None]
            gate_hps = [None, None, None]
            gate_eops = [None, None]
            gate_w12 = [None, None, None, None]
            gate_tr = None
            xdone = None
            for th_ in range(2):
                tok0 = th_ * 1024
                zv = S.add("vector", lambda v: v.memset(acc[:], 0.0),
                           waits=[gate_acc], bump=P.vs)
                gate_acc = (P.vs.h, zv)
                for e in L2F_EXPERTS:
                    for hf, (hc0, hc1) in enumerate(HALF):
                        nchunks = (hc1 - hc0 + 127) // 128
                        def w3load2(g, e=e, hc0=hc0, hc1=hc1):
                            full = (hc1 - hc0) // 128
                            rem = (hc1 - hc0) - full * 128
                            ins = g.dma_start(
                                w3h[:, 0:full * 1024]
                                .rearrange("p (c m) -> p c m", c=full),
                                W3T[e * HH + hc0: e * HH + hc0 + full * 128, :]
                                .rearrange("(c p) m -> p c m", p=128))
                            if rem == 0:
                                return ins
                            ins.then_inc(P.dw.h, 16)
                            return g.dma_start(
                                w3h[0:rem, full * 1024:(full + 1) * 1024],
                                W3T[e * HH + hc0 + full * 128: e * HH + hc1, :])
                        if (hc1 - hc0) % 128 != 0:
                            P.dw.reserve(16)
                        w3v = S.add("gpsimd", w3load2, waits=[gate_w3], bump=P.dw, k=16)
                        act_last = None
                        for m in range(nchunks):
                            mw = min(128, (hc1 - hc0) - m * 128)
                            par = m % 2
                            def wload(g, e=e, hc0=hc0, m=m, par=par, mw=mw):
                                ins = g.dma_start(
                                    w12b[par][:, 0:DC * mw]
                                    .rearrange("p (kc m) -> p kc m", kc=DC),
                                    W1T[e * D:(e + 1) * D,
                                        hc0 + m * 128: hc0 + m * 128 + mw]
                                    .rearrange("(kc p) m -> p kc m", p=128))
                                ins.then_inc(P.dw.h, 16)
                                return g.dma_start(
                                    w12b[2 + par][:, 0:DC * mw]
                                    .rearrange("p (kc m) -> p kc m", kc=DC),
                                    W2T[e * D:(e + 1) * D,
                                        hc0 + m * 128: hc0 + m * 128 + mw]
                                    .rearrange("(kc p) m -> p kc m", p=128))
                            P.dw.reserve(16)
                            wlv = S.add("gpsimd", wload,
                                        waits=[gate_w12[par]], bump=P.dw, k=16)
                            for nt in range(2):
                                bank = nt % 2
                                def h1mm(t, par=par, nt=nt, bank=bank, tok0=tok0, mw=mw):
                                    for kc in range(DC):
                                        ins = t.matmul(
                                            hps[bank][0:mw, :],
                                            w12b[par][:, kc * mw:(kc + 1) * mw],
                                            xn_bf[:, kc * T + tok0 + nt * 512:
                                                  kc * T + tok0 + nt * 512 + 512],
                                            start=(kc == 0), stop=(kc == DC - 1))
                                    return ins
                                t1 = S.add("tensor", h1mm,
                                           waits=[xn_rdy, (P.dw.h, wlv), gate_hps[bank]],
                                           bump=P.ts)
                                s1 = S.add("scalar", lambda s, bank=bank, mw=mw:
                                           s.activation(h1pp[0:mw, bank * 512:(bank + 1) * 512],
                                                        hps[bank][0:mw, :], AF.Silu),
                                           waits=[(P.ts.h, t1), gate_h1[bank]], bump=P.ss)
                                gate_hps[bank] = (P.ss.h, s1)
                                def h2mm(t, par=par, nt=nt, tok0=tok0, mw=mw):
                                    for kc in range(DC):
                                        ins = t.matmul(
                                            hps[2][0:mw, :],
                                            w12b[2 + par][:, kc * mw:(kc + 1) * mw],
                                            xn_bf[:, kc * T + tok0 + nt * 512:
                                                  kc * T + tok0 + nt * 512 + 512],
                                            start=(kc == 0), stop=(kc == DC - 1))
                                    return ins
                                t2 = S.add("tensor", h2mm,
                                           waits=[gate_hps[2]], bump=P.ts)
                                a1 = S.add("vector", lambda v, m=m, nt=nt, bank=bank, mw=mw:
                                           v.tensor_tensor(
                                               act[0:mw, m * 1024 + nt * 512:
                                                   m * 1024 + nt * 512 + 512],
                                               h1pp[0:mw, bank * 512:(bank + 1) * 512],
                                               hps[2][0:mw, :], ALU.mult),
                                           waits=[(P.ts.h, t2), (P.ss.h, s1), gate_act],
                                           bump=P.vs)
                                gate_hps[2] = (P.vs.h, a1)
                                gate_h1[bank] = (P.vs.h, a1)
                                act_last = a1
                            gate_w12[par] = (P.ts.h, P.ts.n)
                        eo_last = None
                        for tt8 in range(8):
                            gtt = th_ * 8 + tt8
                            for fh in range(2):
                                bank = fh % 2
                                def eomm(t, tt8=tt8, fh=fh, bank=bank, nchunks=nchunks,
                                         hc0=hc0, hc1=hc1):
                                    for c in range(nchunks):
                                        cw = min(128, (hc1 - hc0) - c * 128)
                                        ins = t.matmul(
                                            eops[bank][:],
                                            act[0:cw, c * 1024 + tt8 * 128:
                                                c * 1024 + tt8 * 128 + 128],
                                            w3h[0:cw, c * 1024 + fh * 512:
                                                c * 1024 + fh * 512 + 512],
                                            start=(c == 0), stop=(c == nchunks - 1))
                                    return ins
                                te = S.add("tensor", eomm,
                                           waits=[(P.vs.h, act_last), (P.dw.h, w3v),
                                                  gate_eops[bank]],
                                           bump=P.ts)
                                if e < 8:
                                    ae = S.add("vector", lambda v, e=e, gtt=gtt, tt8=tt8,
                                               fh=fh, bank=bank:
                                               v.scalar_tensor_tensor(
                                                   acc[:, tt8 * 1024 + fh * 512:
                                                       tt8 * 1024 + fh * 512 + 512],
                                                   eops[bank][:], w8[:, e * 16 + gtt:
                                                                     e * 16 + gtt + 1],
                                                   acc[:, tt8 * 1024 + fh * 512:
                                                       tt8 * 1024 + fh * 512 + 512],
                                                   ALU.mult, ALU.add),
                                               waits=[(P.ts.h, te), w_rdy, gate_acc],
                                               bump=P.vs)
                                else:
                                    ae = S.add("vector", lambda v, tt8=tt8, fh=fh, bank=bank:
                                               v.tensor_tensor(
                                                   acc[:, tt8 * 1024 + fh * 512:
                                                       tt8 * 1024 + fh * 512 + 512],
                                                   acc[:, tt8 * 1024 + fh * 512:
                                                       tt8 * 1024 + fh * 512 + 512],
                                                   eops[bank][:], ALU.add),
                                               waits=[(P.ts.h, te), gate_acc], bump=P.vs)
                                gate_eops[bank] = (P.vs.h, ae)
                                eo_last = ae
                        gate_act = (P.ts.h, P.ts.n)
                        gate_w3 = (P.ts.h, P.ts.n)
                for tt8 in range(8):
                    for fc in range(8):
                        cv = S.add("vector", lambda v, tt8=tt8, fc=fc: v.tensor_copy(
                            tr_bf[:], acc[:, tt8 * 1024 + fc * 128: tt8 * 1024 + fc * 128 + 128]),
                            waits=[gate_eops[0], gate_eops[1], gate_tr, gate_acc], bump=P.vs)
                        tv = S.add("tensor", lambda t: t.transpose(
                            trps[:], tr_bf[:], ident[:]),
                            waits=[(P.vs.h, cv)], bump=P.ts)
                        av = S.add("vector", lambda v, tt8=tt8, fc=fc, tok0=tok0:
                                   v.tensor_tensor(
                                       x_fm[:, fc * T + tok0 + tt8 * 128:
                                            fc * T + tok0 + tt8 * 128 + 128],
                                       x_fm[:, fc * T + tok0 + tt8 * 128:
                                            fc * T + tok0 + tt8 * 128 + 128],
                                       trps[:], ALU.add),
                                   waits=[(P.ts.h, tv)], bump=P.vs)
                        gate_tr = (P.vs.h, av)
                        xdone = av
                gate_acc = (P.vs.h, xdone)
            S.emit(block)
            pst.close()
            if L2F_DEBUG:
                P.tap("g8", g8[:])
                P.tap("w8", w8[:])
                P.tap("acc", acc[:])
    return




def emit_rms_fm(P, block, src_bf, dst_bf, w_col=None, n_feat=D, tokens=T,
                gate=None, eps=1e-6):
    FC = n_feat // 128
    width = min(512, tokens)
    with contextlib.ExitStack() as own:
        _LN_CTR[0] += 1
        tag = f"_r{_LN_CTR[0]}"
        pp = P.sb("rmspp" + tag, [128, 1024], BF16, own)
        row = P.sb("rmsrow" + tag, [1, width], F32, own)
        rowb = P.sb("rmsrowb" + tag, [1, width], BF16, own)
        ab = P.sb("rmsab" + tag, [128, width], BF16, own)
        psB = P.ps("rmspsB" + tag, [1, 512], F32, own)
        bcA = P.ps("rmsbcA" + tag, [128, 512], F32, own)
        for off in range(0, tokens, width):
            NTl = width // 512
            ts0 = P.ts.n
            ss0 = P.ss.n

            @block.scalar
            def _(s):
                if gate is not None:
                    s.wait_ge(gate[0], gate[1])
                for nt in range(NTl):
                    for kc in range(FC):
                        gi = nt * FC + kc
                        if gi >= 2:
                            s.wait_ge(P.ts.h, ts0 + gi - 1)
                        ins = s.activation(
                            pp[:, (gi % 2) * 512:(gi % 2) * 512 + 512],
                            src_bf[:, kc * tokens + off + nt * 512:
                                   kc * tokens + off + nt * 512 + 512], AF.Square)
                        P.ss.bump(ins, 1)
                    s.wait_ge(P.ts.h, ts0 + (nt + 1) * FC)
                    ins = s.copy(row[0:1, nt * 512:(nt + 1) * 512], psB[:])
                    P.ss.bump(ins, 1)

            @block.tensor
            def _(t):
                if gate is not None:
                    t.wait_ge(gate[0], gate[1])
                for nt in range(NTl):
                    for kc in range(FC):
                        gi = nt * FC + kc
                        t.wait_ge(P.ss.h, ss0 + nt * (FC + 1) + kc + 1)
                        ins = t.matmul(psB[:], P.consts["c_ones_col"][:],
                                       pp[:, (gi % 2) * 512:(gi % 2) * 512 + 512],
                                       start=(kc == 0), stop=(kc == FC - 1))
                        P.ts.bump(ins, 1)

            @block.scalar
            def _(s):
                s.wait_ge(P.ss.h, ss0 + NTl * (FC + 1))
                s.mul(row[0:1, :], row[0:1, :], 1.0 / n_feat)
                s.add(row[0:1, :], row[0:1, :], eps)
                s.activation(row[0:1, :], row[0:1, :], AF.Ln)
                ins = s.activation(rowb[0:1, :], row[0:1, :], AF.Exp, scale=-0.5)
                P.ss.bump(ins, 1)
            s_rstd = P.ss.n

            tb0 = P.ts.n
            sb0 = P.ss.n

            @block.tensor
            def _(t):
                t.wait_ge(P.ss.h, s_rstd)
                for nt in range(NTl):
                    if nt >= 1:
                        t.wait_ge(P.ss.h, sb0 + nt)
                    ins = t.matmul(bcA[:], P.consts["c_ones_row"][:],
                                   rowb[0:1, nt * 512:(nt + 1) * 512],
                                   start=True, stop=True)
                    P.ts.bump(ins, 1)

            @block.scalar
            def _(s):
                for nt in range(NTl):
                    s.wait_ge(P.ts.h, tb0 + nt + 1)
                    ins = s.copy(ab[:, nt * 512:(nt + 1) * 512], bcA[:])
                    P.ss.bump(ins, 1)
            s_ab = P.ss.n

            @block.vector
            def _(v):
                v.wait_ge(P.ss.h, s_ab)
                for kc in range(FC):
                    sl = slice(kc * tokens + off, kc * tokens + off + width)
                    if w_col is not None:
                        v.tensor_tensor(dst_bf[:, sl], src_bf[:, sl], ab[:, 0:width],
                                        ALU.mult)
                        ins = v.tensor_scalar(dst_bf[:, sl], dst_bf[:, sl],
                                              w_col[:, kc:kc + 1], None, ALU.mult)
                    else:
                        ins = v.tensor_tensor(dst_bf[:, sl], src_bf[:, sl],
                                              ab[:, 0:width], ALU.mult)
                P.vs.bump(ins, 1)
    return P.vs.n



def emit_L1_attn(P, x_fm, xn_bf, x_gate):
    DK = 512
    DV = 1024
    dk = 128
    dv = 256
    NH = 4
    NCH = 32
    II = 2816
    IC = II // 128
    Wq = P.din("L1a_Wq", [D, DK])
    Wk = P.din("L1a_Wk", [D, DK])
    Wv = P.din("L1a_Wv", [D, DV])
    Wg = P.din("L1a_Wg", [D, DV])
    Wgk1 = P.din("L1a_Wgk1", [D, 16])
    Wgk2 = P.din("L1a_Wgk2", [16, DK])
    bgk2r = P.din("L1a_bgk2", [1, DK])
    anw = P.din("L1a_anw", [128, DC])
    gnwr = P.din("L1a_gnw", [1, dv])
    Wo = P.din("L1a_Wo", [DV, D])
    mnw = P.din("L1a_mnw", [128, DC])
    Wgate = P.din("L1a_Wgate", [D, II])
    Wup = P.din("L1a_Wup", [D, II])
    Wdn = P.din("L1a_Wdn", [II, D])

    with contextlib.ExitStack() as st:
        with P.nc.Block() as block:
            h_bf = P.sb("l1a_h", [128, DC * T], BF16, st)

            anw_sb = P.sb("l1a_anw", [128, DC], F32, st)
            mnw_sb = P.sb("l1a_mnw", [128, DC], F32, st)
            gnb = P.sb("l1a_gnb", [128, dv], BF16, st)
            bgk2b = P.sb("l1a_bgk2b", [128, DK], BF16, st)
            wgk1_sb = P.sb("l1a_wgk1", [128, DC * 16], BF16, st)
            wgk2_sb = P.sb("l1a_wgk2", [16, DK], BF16, st)
            gk_row = P.sb("l1a_gkrow", [1, dv], BF16, st)
            t1_bf = P.sb("l1a_t1", [16, T], BF16, st)
            zf = P.sb("l1a_zf", [128, 128], F32, st)
            zf2 = P.sb("l1a_zf2", [128, 2 * 128], F32, st)
            wbuf = [P.sb(f"l1a_wb{i}", [128, DC * 256], BF16, st) for i in range(2)]
            gla_st = contextlib.ExitStack()
            z_bf = P.sb("l1a_z", [128, 16 * dk], BF16, gla_st)
            q_fm = P.sb("l1a_q", [128, T], BF16, gla_st)
            k_fm = P.sb("l1a_k", [128, T], BF16, gla_st)
            v_tm = P.sb("l1a_v", [128, 16 * dv], BF16, gla_st)
            g_tm = P.sb("l1a_g", [128, 16 * dv], BF16, gla_st)
            o_tm = P.sb("l1a_o", [128, 16 * dv], BF16, gla_st)
            b_sb = P.sb("l1a_b", [128, 128], F32, gla_st)
            eb_sb = P.sb("l1a_eb", [128, 128], BF16, gla_st)
            enb_sb = P.sb("l1a_enb", [128, 128], BF16, gla_st)
            e2_sb = P.sb("l1a_e2", [128, 128], BF16, gla_st)
            qe_sb = P.sb("l1a_qe", [128, 128], BF16, gla_st)
            ke_sb = P.sb("l1a_ke", [128, 128], BF16, gla_st)
            kde_sb = P.sb("l1a_kde", [128, 128], BF16, gla_st)
            At_sb = P.sb("l1a_At", [128, 64], BF16, gla_st)
            kd_sb = P.sb("l1a_kd", [128, 128], BF16, gla_st)
            S_sb = P.sb("l1a_S", [128, dv], F32, gla_st)
            S_bf = P.sb("l1a_Sbf", [128, dv], BF16, gla_st)
            ebl = P.sb("l1a_ebl", [128, 2], F32, gla_st)
            rstd_o = P.sb("l1a_ro", [128, 1], F32, gla_st)
            sqo = P.sb("l1a_sqo", [128, dv], BF16, gla_st)
            of_bf = P.sb("l1a_of", [128, 2 * T], BF16, gla_st)

            @block.gpsimd
            def _(g):
                P.dw.bump(g.dma_start(anw_sb[:], anw), 16)
                P.dw.bump(g.dma_start(mnw_sb[:], mnw), 16)
                P.dw.bump(g.dma_start(bgk2b[0:1, :], bgk2r), 16)
                P.dw.bump(g.dma_start(gk_row[:], gnwr), 16)
                P.dw.bump(g.dma_start(
                    wgk1_sb[:].rearrange("p (kc m) -> p kc m", kc=DC),
                    Wgk1.rearrange("(kc p) m -> p kc m", p=128)), 16)
                P.dw.bump(g.dma_start(wgk2_sb[:], Wgk2), 16)
            w_v = P.dw.n

            xng = emit_ln_fm(P, block, x_fm, xn_bf, x_gate=x_gate)
            hg = emit_rms_fm(P, block, xn_bf, h_bf, w_col=anw_sb,
                             gate=(P.vs.h, xng), eps=1e-6)
            h_rdy = (P.vs.h, hg)

            pst = contextlib.ExitStack()
            ps1 = P.ps("l1a_ps1", [128, 512], F32, pst)
            ps2 = P.ps("l1a_ps2", [128, 512], F32, pst)
            bps = P.ps("l1a_bps", [128, 128], F32, pst)
            bfps = P.ps("l1a_bfps", [128, 128], F32, pst)
            ops_ = [P.ps(f"l1a_ops{i}", [64, dv], F32, pst) for i in range(2)]
            kdps = P.ps("l1a_kdps", [128, 128], BF16, pst)
            Sps = P.ps("l1a_Sps", [128, dv], F32, pst)

            S = Sched(P)
            onesb = P.consts["c_ones_row"]
            onesf = P.consts["c_ones_row_f"]
            ident = P.consts["c_ident"]
            triu2 = P.consts["c_triu2"]
            trium = P.consts["c_trium"]
            lnsc = P.consts["c_lnscale"]

            def vchain(fn, extra=()):
                return S.add("vector", fn, waits=[(P.vs.h, P.vs.n)] + list(extra),
                             bump=P.vs)

            def schain(fn, extra=()):
                return S.add("scalar", fn, waits=[(P.ss.h, P.ss.n)] + list(extra),
                             bump=P.ss)

            gate_ps = [None, None]
            for half in range(2):
                tv = S.add("tensor", lambda t, half=half: t.matmul(
                    ps1[:, 0:256], onesb[:], bgk2b[0:1, half * 256:(half + 1) * 256],
                    start=True, stop=True),
                    waits=[(P.dw.h, w_v), h_rdy, gate_ps[0]], bump=P.ts)
                sv = S.add("scalar", lambda s, half=half: s.copy(
                    bgk2b[:, half * 256:(half + 1) * 256], ps1[:, 0:256]),
                    waits=[(P.ts.h, tv)], bump=P.ss)
                gate_ps[0] = (P.ss.h, sv)
            tv = S.add("tensor", lambda t: t.matmul(
                ps1[:, 0:256], onesb[:], gk_row[0:1, :], start=True, stop=True),
                waits=[gate_ps[0]], bump=P.ts)
            sv = S.add("scalar", lambda s: s.copy(gnb[:], ps1[:, 0:dv]),
                       waits=[(P.ts.h, tv)], bump=P.ss)
            gate_ps[0] = (P.ss.h, sv)

            for nt in range(NT):
                def t1mm(t, nt=nt):
                    for kc in range(DC):
                        ins = t.matmul(ps1[0:16, :],
                                       wgk1_sb[:, kc * 16:(kc + 1) * 16],
                                       h_bf[:, kc * T + nt * 512: kc * T + nt * 512 + 512],
                                       start=(kc == 0), stop=(kc == DC - 1))
                    return ins
                tv = S.add("tensor", t1mm, waits=[(P.dw.h, w_v), h_rdy, gate_ps[0]],
                           bump=P.ts)
                sv = S.add("scalar", lambda s, nt=nt: s.copy(
                    t1_bf[:, nt * 512:(nt + 1) * 512], ps1[0:16, :]),
                    waits=[(P.ts.h, tv)], bump=P.ss)
                gate_ps[0] = (P.ss.h, sv)

            x2g = None
            for hh in range(NH):
                par = hh % 2
                for tt in range(16):
                    tv = S.add("tensor", lambda t, tt=tt, hh=hh: t.matmul(
                        ps2[:, 0:dk], t1_bf[0:16, tt * 128:(tt + 1) * 128],
                        wgk2_sb[:, hh * dk:(hh + 1) * dk],
                        start=True, stop=True),
                        waits=[(P.ss.h, P.ss.n), gate_ps[1]], bump=P.ts)
                    v1 = S.add("vector", lambda v, tt=tt, hh=hh: v.tensor_tensor(
                        zf[:], ps2[:, 0:dk], bgk2b[:, hh * dk:(hh + 1) * dk], ALU.add),
                        waits=[(P.ts.h, tv), (P.vs.h, P.vs.n)], bump=P.vs)
                    gate_ps[1] = (P.vs.h, v1)
                    s1 = schain(lambda s: s.activation(zf2[:, 0:dk], zf[:], AF.Abs),
                                extra=[(P.vs.h, v1)])
                    s2 = schain(lambda s: s.activation(zf2[:, 0:dk], zf2[:, 0:dk],
                                                       AF.Exp, scale=-1.0))
                    s3 = schain(lambda s: s.activation(zf2[:, 0:dk], zf2[:, 0:dk],
                                                       AF.Ln, bias=P.consts["c_p1"][:]))
                    v2 = vchain(lambda v: v.tensor_single_scalar(zf2[:, dk:2 * dk], zf[:],
                                                                 0.0, ALU.min),
                                extra=[(P.ss.h, s1)])
                    v3 = vchain(lambda v: v.tensor_tensor(zf2[:, dk:2 * dk],
                                                          zf2[:, dk:2 * dk],
                                                          zf2[:, 0:dk], ALU.subtract),
                                extra=[(P.ss.h, s3)])
                    s4 = schain(lambda s, tt=tt: s.mul(z_bf[:, tt * dk:(tt + 1) * dk],
                                                       zf2[:, dk:2 * dk], 1.0 / 16.0),
                                extra=[(P.vs.h, v3)])
                    gate_ps[1] = (P.ss.h, s4)
                def whload(g, hh=hh, par=par):
                    ins = g.dma_start(
                        wbuf[par][:, 0:DC * 128].rearrange("p (kc m) -> p kc m", kc=DC),
                        Wq[:, hh * dk:(hh + 1) * dk]
                        .rearrange("(kc p) m -> p kc m", p=128))
                    ins.then_inc(P.dw.h, 16)
                    return g.dma_start(
                        wbuf[1 - par][:, 0:DC * 128].rearrange("p (kc m) -> p kc m", kc=DC),
                        Wk[:, hh * dk:(hh + 1) * dk]
                        .rearrange("(kc p) m -> p kc m", p=128))
                P.dw.reserve(16)
                dv_ = S.add("gpsimd", whload, waits=[(P.ts.h, P.ts.n)], bump=P.dw, k=16)
                for which, wb_, dst in (("q", par, q_fm), ("k", 1 - par, k_fm)):
                    for nt in range(NT):
                        bank = [ps1, ps2][nt % 2]
                        def qmm(t, wb_=wb_, nt=nt, bank=bank):
                            for kc in range(DC):
                                ins = t.matmul(
                                    bank[:],
                                    wbuf[wb_][:, kc * 128:(kc + 1) * 128],
                                    h_bf[:, kc * T + nt * 512: kc * T + nt * 512 + 512],
                                    start=(kc == 0), stop=(kc == DC - 1))
                            return ins
                        tv = S.add("tensor", qmm,
                                   waits=[(P.dw.h, dv_), gate_ps[nt % 2]], bump=P.ts)
                        sv = S.add("scalar", lambda s, dst=dst, nt=nt, bank=bank:
                                   s.copy(dst[:, nt * 512:(nt + 1) * 512], bank[:]),
                                   waits=[(P.ts.h, tv)], bump=P.ss)
                        gate_ps[nt % 2] = (P.ss.h, sv)
                def wvload(g, hh=hh, par=par):
                    ins = g.dma_start(
                        wbuf[par][:, 0:DC * dv].rearrange("p (kc m) -> p kc m", kc=DC),
                        Wv[:, hh * dv:(hh + 1) * dv]
                        .rearrange("(kc p) m -> p kc m", p=128))
                    ins.then_inc(P.dw.h, 16)
                    return g.dma_start(
                        wbuf[1 - par][:, 0:DC * dv].rearrange("p (kc m) -> p kc m", kc=DC),
                        Wg[:, hh * dv:(hh + 1) * dv]
                        .rearrange("(kc p) m -> p kc m", p=128))
                P.dw.reserve(16)
                dv2 = S.add("gpsimd", wvload, waits=[(P.ts.h, P.ts.n)], bump=P.dw, k=16)
                for which, wb_, dst in (("v", par, v_tm), ("g", 1 - par, g_tm)):
                    for tt in range(16):
                        bank = [ps1, ps2][tt % 2]
                        def vmm(t, wb_=wb_, tt=tt, bank=bank):
                            for kc in range(DC):
                                ins = t.matmul(
                                    bank[0:128, 0:dv],
                                    h_bf[:, kc * T + tt * 128: kc * T + tt * 128 + 128],
                                    wbuf[wb_][:, kc * dv:(kc + 1) * dv],
                                    start=(kc == 0), stop=(kc == DC - 1))
                            return ins
                        tv = S.add("tensor", vmm,
                                   waits=[(P.dw.h, dv2), gate_ps[tt % 2]], bump=P.ts)
                        sv = S.add("scalar", lambda s, dst=dst, tt=tt, bank=bank:
                                   s.copy(dst[:, tt * dv:(tt + 1) * dv], bank[0:128, 0:dv]),
                                   waits=[(P.ts.h, tv)], bump=P.ss)
                        gate_ps[tt % 2] = (P.ss.h, sv)
                qkv_done = (P.ss.h, P.ss.n)

                sz = vchain(lambda v: v.memset(S_sb[:], 0.0), extra=[qkv_done])
                sz2 = vchain(lambda v: v.memset(S_bf[:], 0.0))
                gate_b = None
                gate_Atps = None
                gate_o = [None, None]
                gate_kdps = None
                gate_Sps = None
                for cp in range(16):
                    for ci in range(2):
                        c = 2 * cp + ci
                        base = (c % 2) * 64
                        tv = S.add("tensor", lambda t, cp=cp, ci=ci, base=base, hh=hh:
                                   t.matmul(bps[base:base + 64, :],
                                            triu2[base:base + 64, :],
                                            z_bf[base:base + 64,
                                                 cp * dk: cp * dk + dk],
                                            start=True, stop=True),
                                   waits=[qkv_done, gate_b] if gate_b else [qkv_done],
                                   bump=P.ts)
                    sv = S.add("scalar", lambda s: s.copy(b_sb[:], bps[:]),
                               waits=[(P.ts.h, tv)], bump=P.ss)
                    tv2 = S.add("tensor", lambda t: t.transpose(
                        bfps[:], b_sb[:], P.consts["c_identf"][:]),
                        waits=[(P.ss.h, sv), gate_Atps], bump=P.ts)
                    sv2 = S.add("scalar", lambda s: s.copy(b_sb[:], bfps[:]),
                                waits=[(P.ts.h, tv2)], bump=P.ss)
                    gate_b = (P.ss.h, sv2)
                    s_eb = schain(lambda s: s.activation(eb_sb[:], b_sb[:], AF.Exp,
                                                         bias=lnsc[:]),
                                  extra=[(P.ss.h, sv2)])
                    s_enb = schain(lambda s: s.activation(enb_sb[:], b_sb[:], AF.Exp,
                                                          scale=-1.0))
                    s_e2a = schain(lambda s: s.activation(e2_sb[:, 0:64], b_sb[:, 0:64],
                                                          AF.Exp, scale=-1.0,
                                                          bias=b_sb[:, 63:64]))
                    s_e2b = schain(lambda s: s.activation(e2_sb[:, 64:128], b_sb[:, 64:128],
                                                          AF.Exp, scale=-1.0,
                                                          bias=b_sb[:, 127:128]))
                    s_ebl = schain(lambda s: s.activation(ebl[:, 0:2], b_sb[:, 63:128:64],
                                                          AF.Exp))
                    eb_rdy = (P.ss.h, s_ebl)
                    v_qe = vchain(lambda v, cp=cp: v.tensor_tensor(
                        qe_sb[:], q_fm[:, cp * 128:(cp + 1) * 128], eb_sb[:], ALU.mult),
                        extra=[eb_rdy, gate_Atps])
                    v_ke = vchain(lambda v, cp=cp: v.tensor_tensor(
                        ke_sb[:], k_fm[:, cp * 128:(cp + 1) * 128], enb_sb[:], ALU.mult))
                    v_kde = vchain(lambda v, cp=cp: v.tensor_tensor(
                        kde_sb[:], k_fm[:, cp * 128:(cp + 1) * 128], e2_sb[:], ALU.mult))
                    for ci in range(2):
                        c = 2 * cp + ci
                        base = (c % 2) * 64
                        ob = c % 2
                        tva = S.add("tensor", lambda t, ci=ci:
                                    t.matmul(bps[0:64, 0:64],
                                             ke_sb[:, ci * 64:(ci + 1) * 64],
                                             qe_sb[:, ci * 64:(ci + 1) * 64],
                                             start=True, stop=True),
                                    waits=[(P.vs.h, v_kde), gate_b], bump=P.ts)
                        vmask = S.add("vector", lambda v, base=base: v.tensor_tensor(
                            At_sb[base:base + 64, :], bps[0:64, 0:64], trium[:], ALU.mult),
                            waits=[(P.ts.h, tva), (P.vs.h, P.vs.n)], bump=P.vs)
                        gate_b = (P.vs.h, vmask)
                        tvo1 = S.add("tensor", lambda t, base=base, cp=cp, ob=ob:
                                     t.matmul(ops_[ob][:],
                                              At_sb[base:base + 64, :],
                                              v_tm[base:base + 64, cp * dv: cp * dv + dv],
                                              start=True, stop=False),
                                     waits=[(P.vs.h, vmask), gate_o[ob]], bump=P.ts)
                        tvo2 = S.add("tensor", lambda t, ci=ci, ob=ob:
                                     t.matmul(ops_[ob][:],
                                              qe_sb[:, ci * 64:(ci + 1) * 64],
                                              S_bf[:],
                                              start=False, stop=True),
                                     waits=[(P.vs.h, P.vs.n)], bump=P.ts)
                        vo = S.add("vector", lambda v, base=base, cp=cp, ci=ci, ob=ob:
                                   v.tensor_copy(
                                       o_tm[base:base + 64,
                                            (2 * cp + ci) // 2 * dv:
                                            (2 * cp + ci) // 2 * dv + dv],
                                       ops_[ob][:]),
                                   waits=[(P.ts.h, tvo2), (P.vs.h, P.vs.n)], bump=P.vs)
                        gate_o[ob] = (P.vs.h, vo)
                        tkd = S.add("tensor", lambda t, ci=ci:
                                    t.transpose(kdps[0:64, 0:128],
                                                kde_sb[:, ci * 64:(ci + 1) * 64],
                                                ident[:]),
                                    waits=[(P.vs.h, vo), gate_kdps], bump=P.ts)
                        skd = S.add("scalar", lambda s, base=base: s.copy(
                            kd_sb[base:base + 64, :], kdps[0:64, 0:128]),
                            waits=[(P.ts.h, tkd), (P.ss.h, P.ss.n)], bump=P.ss)
                        gate_kdps = (P.ss.h, skd)
                        tS = S.add("tensor", lambda t, base=base, cp=cp, ci=ci:
                                   t.matmul(Sps[:],
                                            kd_sb[base:base + 64, :],
                                            v_tm[base:base + 64,
                                                 (2 * cp + ci) // 2 * dv:
                                                 (2 * cp + ci) // 2 * dv + dv],
                                            start=True, stop=True),
                                   waits=[(P.ss.h, skd), gate_Sps], bump=P.ts)
                        vS1 = vchain(lambda v, ci=ci: v.tensor_scalar(
                            S_sb[:], S_sb[:], ebl[:, ci:ci + 1], None, ALU.mult),
                            extra=[(P.ts.h, tS), (P.ss.h, s_ebl)])
                        vS2 = vchain(lambda v: v.tensor_tensor(S_sb[:], S_sb[:], Sps[:],
                                                               ALU.add))
                        vS3 = vchain(lambda v: v.tensor_copy(S_bf[:], S_sb[:]))
                        gate_Sps = (P.vs.h, vS3)
                    gate_Atps = (P.ts.h, P.ts.n)
                og_done = None
                for tt in range(16):
                    s_sq = schain(lambda s, tt=tt: s.activation(
                        sqo[:], o_tm[:, tt * dv:(tt + 1) * dv], AF.Square),
                        extra=[gate_o[0], gate_o[1]])
                    v_r = vchain(lambda v: v.reduce_sum(rstd_o[:], sqo[:], axis=AX.X),
                                 extra=[(P.ss.h, s_sq)])
                    s_r1 = schain(lambda s: s.mul(rstd_o[:], rstd_o[:], 1.0 / dv),
                                  extra=[(P.vs.h, v_r)])
                    s_r2 = schain(lambda s: s.add(rstd_o[:], rstd_o[:], 1e-6))
                    s_r3 = schain(lambda s: s.activation(rstd_o[:], rstd_o[:], AF.Ln))
                    s_r4 = schain(lambda s: s.activation(rstd_o[:], rstd_o[:], AF.Exp,
                                                         scale=-0.5))
                    v_o1 = vchain(lambda v, tt=tt: v.tensor_scalar(
                        o_tm[:, tt * dv:(tt + 1) * dv], o_tm[:, tt * dv:(tt + 1) * dv],
                        rstd_o[:], None, ALU.mult), extra=[(P.ss.h, s_r4)])
                    v_o2 = vchain(lambda v, tt=tt: v.tensor_tensor(
                        o_tm[:, tt * dv:(tt + 1) * dv], o_tm[:, tt * dv:(tt + 1) * dv],
                        gnb[:], ALU.mult))
                    s_g = schain(lambda s, tt=tt: s.activation(
                        sqo[:], g_tm[:, tt * dv:(tt + 1) * dv], AF.Silu),
                        extra=[(P.vs.h, v_o1)])
                    v_o3 = vchain(lambda v, tt=tt: v.tensor_tensor(
                        o_tm[:, tt * dv:(tt + 1) * dv], o_tm[:, tt * dv:(tt + 1) * dv],
                        sqo[:], ALU.mult), extra=[(P.ss.h, s_g)])
                    for half in range(2):
                        ttv = S.add("tensor", lambda t, tt=tt, half=half: t.transpose(
                            kdps[:, 0:128],
                            o_tm[:, tt * dv + half * 128: tt * dv + half * 128 + 128],
                            ident[:]),
                            waits=[(P.vs.h, v_o3), gate_kdps], bump=P.ts)
                        svv = S.add("scalar", lambda s, tt=tt, half=half: s.copy(
                            of_bf[:, half * T + tt * 128: half * T + tt * 128 + 128],
                            kdps[:, 0:128]),
                            waits=[(P.ts.h, ttv), (P.ss.h, P.ss.n)], bump=P.ss)
                        gate_kdps = (P.ss.h, svv)
                    og_done = (P.ss.h, P.ss.n)
                if hh == 0:
                    pass
                for m in range(DC):
                    par2 = m % 2
                    dmw = S.add("gpsimd", lambda g, hh=hh, m=m, par2=par2: g.dma_start(
                        wbuf[par2][:, 0:2 * 128].rearrange("p (c m) -> p c m", c=2),
                        Wo[hh * dv:(hh + 1) * dv, m * 128:(m + 1) * 128]
                        .rearrange("(c p) m -> p c m", p=128)),
                        waits=[(P.ts.h, P.ts.n)], bump=P.dw, k=16)
                    for nt in range(NT):
                        bank = [ps1, ps2][nt % 2]
                        def womm(t, par2=par2, nt=nt, bank=bank):
                            for c in range(2):
                                ins = t.matmul(
                                    bank[:],
                                    wbuf[par2][:, c * 128:(c + 1) * 128],
                                    of_bf[:, c * T + nt * 512: c * T + nt * 512 + 512],
                                    start=(c == 0), stop=(c == 1))
                            return ins
                        tv = S.add("tensor", womm,
                                   waits=[(P.dw.h, dmw), og_done, gate_ps[nt % 2]],
                                   bump=P.ts)
                        vv = S.add("vector", lambda v, m=m, nt=nt, bank=bank:
                                   v.tensor_tensor(
                                       xn_bf[:, m * T + nt * 512: m * T + nt * 512 + 512],
                                       xn_bf[:, m * T + nt * 512: m * T + nt * 512 + 512],
                                       bank[:], ALU.add),
                                   waits=[(P.ts.h, tv), (P.vs.h, P.vs.n)], bump=P.vs)
                        gate_ps[nt % 2] = (P.vs.h, vv)
                x2g = (P.vs.h, P.vs.n)
            S.emit(block)
            pst.close()

            S2 = Sched(P)
            for kc in range(DC):
                vv = S2.add("vector", lambda v, kc=kc: v.tensor_tensor(
                    x_fm[:, kc * T:(kc + 1) * T], x_fm[:, kc * T:(kc + 1) * T],
                    xn_bf[:, kc * T:(kc + 1) * T], ALU.add),
                    waits=[x2g, (P.vs.h, P.vs.n)], bump=P.vs)
            S2.emit(block)
            x_up = (P.vs.h, P.vs.n)

            gla_st.close()
            g1_sb = P.sb("l1a_g1", [128, IC * 512], BF16, st)
            pr_sb = P.sb("l1a_pr", [128, IC * 512], BF16, st)
            wdb = [P.sb(f"l1a_wd{i}", [128, IC * 128], BF16, st) for i in range(2)]
            ng = emit_rms_fm(P, block, xn_bf, h_bf, w_col=mnw_sb, gate=x_up, eps=1e-6)
            n_rdy = (P.vs.h, ng)
            S3 = Sched(P)
            pst2 = contextlib.ExitStack()
            mps = [P.ps(f"l1a_mps{i}", [128, 512], F32, pst2) for i in range(2)]
            dps = [P.ps(f"l1a_dps{i}", [128, 512], F32, pst2) for i in range(2)]
            gate_mps = [None, None]
            gate_dps = [None, None]
            gate_g1 = None
            gate_wd = [None, None]
            for tt in range(NT):
                prod_done = None
                for m in range(IC):
                    par = m % 2
                    def wgl(g, m=m, par=par, tt=tt):
                        ins = g.dma_start(
                            wbuf[par][:, 0:DC * 128].rearrange("p (kc m) -> p kc m", kc=DC),
                            Wgate[:, m * 128:(m + 1) * 128]
                            .rearrange("(kc p) m -> p kc m", p=128))
                        ins.then_inc(P.dw.h, 16)
                        return g.dma_start(
                            wdb[par][:, 0:DC * 128].rearrange("p (kc m) -> p kc m", kc=DC),
                            Wup[:, m * 128:(m + 1) * 128]
                            .rearrange("(kc p) m -> p kc m", p=128))
                    P.dw.reserve(16)
                    dmv = S3.add("gpsimd", wgl,
                                 waits=[n_rdy, (P.ts.h, gate_wd[par][1])
                                        if gate_wd[par] else None],
                                 bump=P.dw, k=16)
                    b0 = m % 2
                    def gmm(t, par=par, tt=tt, b0=b0):
                        for kc in range(DC):
                            ins = t.matmul(
                                mps[b0][:],
                                wbuf[par][:, kc * 128:(kc + 1) * 128],
                                h_bf[:, kc * T + tt * 512: kc * T + tt * 512 + 512],
                                start=(kc == 0), stop=(kc == DC - 1))
                        return ins
                    t1 = S3.add("tensor", gmm,
                                waits=[n_rdy, (P.dw.h, dmv), gate_mps[b0]], bump=P.ts)
                    s1 = S3.add("scalar", lambda s, m=m, b0=b0: s.activation(
                        g1_sb[:, m * 512:(m + 1) * 512], mps[b0][:], AF.Silu),
                        waits=[(P.ts.h, t1), (P.ss.h, P.ss.n), gate_g1], bump=P.ss)
                    def umm(t, par=par, tt=tt, b0=b0):
                        for kc in range(DC):
                            ins = t.matmul(
                                mps[b0][:],
                                wdb[par][:, kc * 128:(kc + 1) * 128],
                                h_bf[:, kc * T + tt * 512: kc * T + tt * 512 + 512],
                                start=(kc == 0), stop=(kc == DC - 1))
                        return ins
                    t2 = S3.add("tensor", umm, waits=[(P.ss.h, s1)], bump=P.ts)
                    v1 = S3.add("vector", lambda v, m=m, b0=b0: v.tensor_tensor(
                        pr_sb[:, m * 512:(m + 1) * 512],
                        g1_sb[:, m * 512:(m + 1) * 512], mps[b0][:], ALU.mult),
                        waits=[(P.ts.h, t2), (P.vs.h, P.vs.n)], bump=P.vs)
                    gate_mps[b0] = (P.vs.h, v1)
                    prod_done = v1
                    gate_wd[par] = (P.ts.h, P.ts.n)
                for m in range(DC):
                    par = m % 2
                    dmv = S3.add("gpsimd", lambda g, m=m, par=par: g.dma_start(
                        wdb[par][:, 0:IC * 128].rearrange("p (c mm) -> p c mm", c=IC),
                        Wdn[:, m * 128:(m + 1) * 128]
                        .rearrange("(c p) mm -> p c mm", p=128)),
                        waits=[n_rdy, (P.ts.h, P.ts.n)], bump=P.dw, k=16)
                    b0 = m % 2
                    def dmm(t, par=par, tt=tt, b0=b0):
                        for c in range(IC):
                            ins = t.matmul(
                                dps[b0][:],
                                wdb[par][:, c * 128:(c + 1) * 128],
                                pr_sb[:, c * 512:(c + 1) * 512],
                                start=(c == 0), stop=(c == IC - 1))
                        return ins
                    t3 = S3.add("tensor", dmm,
                                waits=[(P.vs.h, prod_done), (P.dw.h, P.dw.n),
                                       gate_dps[b0]], bump=P.ts)
                    v2 = S3.add("vector", lambda v, m=m, tt=tt, b0=b0: v.tensor_tensor(
                        x_fm[:, m * T + tt * 512: m * T + tt * 512 + 512],
                        x_fm[:, m * T + tt * 512: m * T + tt * 512 + 512],
                        dps[b0][:], ALU.add),
                        waits=[(P.ts.h, t3), (P.vs.h, P.vs.n)], bump=P.vs)
                    gate_dps[b0] = (P.vs.h, v2)
                gate_g1 = (P.ts.h, P.ts.n)
            S3.emit(block)
            pst2.close()
    return




L1F_DEPTH = 6


def emit_L1_ffn(P, x_fm, xn_bf, x_gate):
    DEPTH = L1F_DEPTH
    INNER = 1024
    HID = 2730
    HHALF = [(0, 1408), (1408, 2730)]
    Wqkv = P.din("L1f_Wqkv", [6 * D, 3 * INNER])
    WoA = P.din("L1f_Wo", [6 * INNER, D])
    W1A = P.din("L1f_W1", [6 * D, 2 * HID])
    W2A = P.din("L1f_W2", [6 * HID, D])
    qsA = P.din("L1f_qs", [128, 6])
    ksA = P.din("L1f_ks", [128, 6])
    lngA = P.din("L1f_lng", [128, 6 * DC])
    lnbA = P.din("L1f_lnb", [128, 6 * DC])
    b1aA = P.din("L1f_b1a", [128, 6 * 22])
    b1gA = P.din("L1f_b1g", [128, 6 * 22])
    b2A = P.din("L1f_b2", [128, 6 * DC])
    nc = P.nc
    xscr = nc.dram_tensor("l1f_xscr", [128, DC * T], F32).ap()

    rot2A = P.din("c_rot2", [128, 128])
    bd2A = P.din("c_bd2", [128, 128])
    wmA = P.din("c_l1f_wmask", [128, 384])
    csqA = P.din("c_l1f_csq", [128, 2048])
    ssqA = P.din("c_l1f_ssq", [128, 2048])
    cskA = P.din("c_l1f_csk", [128, 2048])
    sskA = P.din("c_l1f_ssk", [128, 2048])
    with contextlib.ExitStack() as st:
        with P.nc.Block() as block:
            rot2_t = P.sb("l1f_rot2", [128, 128], BF16, st)
            bd2_t = P.sb("l1f_bd2", [128, 128], BF16, st)
            wm_t = P.sb("l1f_wm", [128, 384], BF16, st)
            csq_t = P.sb("l1f_csq", [128, 2048], BF16, st)
            ssq_t = P.sb("l1f_ssq", [128, 2048], BF16, st)
            csk_t = P.sb("l1f_csk", [128, 2048], BF16, st)
            ssk_t = P.sb("l1f_ssk", [128, 2048], BF16, st)
            qs_sb = P.sb("l1f_qs", [128, 6], F32, st)
            ks_sb = P.sb("l1f_ks", [128, 6], F32, st)
            lng_sb = P.sb("l1f_lng", [128, 6 * DC], F32, st)
            lnb_sb = P.sb("l1f_lnb", [128, 6 * DC], F32, st)
            b1a_sb = P.sb("l1f_b1a", [128, 6 * 22], F32, st)
            b1g_sb = P.sb("l1f_b1g", [128, 6 * 22], F32, st)
            b2_sb = P.sb("l1f_b2", [128, 6 * DC], F32, st)
            wb = [P.sb(f"l1f_wb{i}", [128, DC * 128], BF16, st) for i in range(4)]
            wdb = [P.sb(f"l1f_wdb{i}", [128, 22 * 128], BF16, st) for i in range(2)]
            xadd = P.sb("l1f_xadd", [128, T], F32, st)

            @block.gpsimd
            def _(g):
                for dst, src in ((qs_sb, qsA), (ks_sb, ksA), (lng_sb, lngA),
                                 (lnb_sb, lnbA), (b1a_sb, b1aA), (b1g_sb, b1gA),
                                 (b2_sb, b2A), (csq_t, csqA), (ssq_t, ssqA),
                                 (csk_t, cskA), (ssk_t, sskA), (rot2_t, rot2A),
                                 (bd2_t, bd2A), (wm_t, wmA)):
                    P.dw.bump(g.dma_start(dst[:], src), 16)
            w_v = P.dw.n

            xng = emit_ln_fm(P, block, x_fm, xn_bf, x_gate=x_gate)

            @block.sync
            def _(sy):
                sy.wait_ge(P.vs.h, xng)
                P.dx.bump(sy.dma_start(xscr, x_fm[:]), 16)
            spill_v = P.dx.n

            S0 = Sched(P)
            for kc in range(DC):
                S0.add("vector", lambda v, kc=kc: v.tensor_copy(
                    x_fm[:, kc * T:(kc + 1) * T], xn_bf[:, kc * T:(kc + 1) * T]),
                    waits=[(P.dx.h, spill_v), (P.vs.h, P.vs.n)], bump=P.vs)
            S0.emit(block)
            y_rdy = (P.vs.h, P.vs.n)

            csq, ssq, csk, ssk = csq_t, ssq_t, csk_t, ssk_t
            rot2 = rot2_t
            bd2 = bd2_t
            wmask = wm_t
            ident = P.consts["c_ident"]
            onesf = P.consts["c_ones_row_f"]

            mem_gate = [y_rdy]
            for li in range(DEPTH):
                attn_st = contextlib.ExitStack()
                ql = P.sb(f"l1f_ql{li}", [128, 1536], BF16, attn_st)
                sq = P.sb(f"l1f_sq{li}", [128, 1536], BF16, attn_st)
                rsb = P.sb(f"l1f_rsb{li}", [128, 1536], BF16, attn_st)
                q_fm = P.sb(f"l1f_q{li}", [128, T], BF16, attn_st)
                k_fm = P.sb(f"l1f_k{li}", [128, T], BF16, attn_st)
                v_tm = P.sb(f"l1f_v{li}", [128, 16 * 130], BF16, attn_st)
                ptile = P.sb(f"l1f_p{li}", [128, 4 * 384], BF16, attn_st)
                zrow = P.sb(f"l1f_z{li}", [1, 512], F32, attn_st)
                rzv = P.sb(f"l1f_rz{li}", [1, 512], F32, attn_st)
                rzb = P.sb(f"l1f_rzb{li}", [64, 512], F32, attn_st)
                attn = P.sb(f"l1f_attn{li}", [128, DC * T], BF16, attn_st)
                pst = contextlib.ExitStack()
                ps1 = P.ps(f"l1f_ps1{li}", [128, 512], F32, pst)
                ps2 = P.ps(f"l1f_ps2{li}", [128, 512], F32, pst)
                ps3 = P.ps(f"l1f_ps3{li}", [128, 512], F32, pst)
                outp = [P.ps(f"l1f_out{i}_{li}", [65, 128], F32, pst) for i in range(4)]
                rzps = P.ps(f"l1f_rzps{li}", [64, 512], F32, pst)

                S = Sched(P)
                gate_ps = [None, None, None]
                gate_out = [None, None, None, None]
                gate_vps = [None, None]
                gate_rzps = None
                shift_done = None
                for kc in range(DC):
                    def shf(v, kc=kc):
                        if kc < 4:
                            return v.tensor_copy(xn_bf[:, kc * T:(kc + 1) * T],
                                                 x_fm[:, kc * T:(kc + 1) * T])
                        v.tensor_copy(xn_bf[:, kc * T + 1:(kc + 1) * T],
                                      x_fm[:, kc * T:(kc + 1) * T - 1])
                        return v.memset(xn_bf[:, kc * T: kc * T + 1], 0.0)
                    sv = S.add("vector", shf, waits=mem_gate + [(P.vs.h, P.vs.n)],
                               bump=P.vs)
                xs_rdy = (P.vs.h, P.vs.n)
                vset = S.add("vector", lambda v: v.memset(v_tm[:], 1.0),
                             waits=mem_gate + [(P.vs.h, P.vs.n)], bump=P.vs)

                oev_all = None
                for hg in range(8):
                    par = hg % 2
                    def wqk(g, li=li, hg=hg, par=par):
                        ins = g.dma_start(
                            wb[par][:].rearrange("p (kc m) -> p kc m", kc=DC),
                            Wqkv[li * D:(li + 1) * D, hg * 128: hg * 128 + 128]
                            .rearrange("(kc p) m -> p kc m", p=128))
                        ins.then_inc(P.dw.h, 16)
                        return g.dma_start(
                            wb[2 + par][:].rearrange("p (kc m) -> p kc m", kc=DC),
                            Wqkv[li * D:(li + 1) * D, INNER + hg * 128: INNER + hg * 128 + 128]
                            .rearrange("(kc p) m -> p kc m", p=128))
                    P.dw.reserve(16)
                    dv_ = S.add("gpsimd", wqk,
                                waits=mem_gate + [(P.ts.h, P.ts.n)], bump=P.dw, k=16)
                    for side, wslot, tabc, tabs, sc_sb, dstf in (
                            (0, par, csq, ssq, qs_sb, q_fm),
                            (1, 2 + par, csk, ssk, ks_sb, k_fm)):
                        psl = [ps1, ps2, ps3]
                        for nt in range(NT):
                            bank = psl[nt % 3]
                            bs = (nt % 3) * 512
                            sl = slice(bs, bs + 512)
                            def qmm(t, wslot=wslot, nt=nt, bank=bank):
                                for kc in range(DC):
                                    ins = t.matmul(
                                        bank[:],
                                        wb[wslot][:, kc * 128:(kc + 1) * 128],
                                        xn_bf[:, kc * T + nt * 512: kc * T + nt * 512 + 512],
                                        start=(kc == 0), stop=(kc == DC - 1))
                                return ins
                            tv = S.add("tensor", qmm,
                                       waits=[xs_rdy, (P.dw.h, dv_), gate_ps[nt % 3]],
                                       bump=P.ts)
                            sv = S.add("scalar", lambda s, sl=sl, bank=bank:
                                       s.copy(ql[:, sl], bank[:]),
                                       waits=[(P.ts.h, tv), (P.ss.h, P.ss.n)], bump=P.ss)
                            s2 = S.add("scalar", lambda s, sl=sl:
                                       s.activation(sq[:, sl], ql[:, sl], AF.Square),
                                       waits=[(P.ss.h, P.ss.n)], bump=P.ss)
                            tb = S.add("tensor", lambda t, sl=sl, bank=bank:
                                       t.matmul(bank[:], bd2[:], sq[:, sl],
                                                start=True, stop=True),
                                       waits=[(P.ss.h, s2)], bump=P.ts)
                            vc = S.add("vector", lambda v, sl=sl, bank=bank:
                                       v.tensor_single_scalar(rsb[:, sl], bank[:],
                                                              1e-24, ALU.max),
                                       waits=[(P.ts.h, tb), (P.vs.h, P.vs.n)], bump=P.vs)
                            gate_ps[nt % 3] = (P.vs.h, vc)
                            sl_ = S.add("scalar", lambda s, sl=sl: s.activation(
                                rsb[:, sl], rsb[:, sl], AF.Ln),
                                waits=[(P.vs.h, vc), (P.ss.h, P.ss.n)], bump=P.ss)
                            se_ = S.add("scalar", lambda s, sl=sl: s.activation(
                                rsb[:, sl], rsb[:, sl], AF.Exp, scale=-0.5),
                                waits=[(P.ss.h, P.ss.n)], bump=P.ss)
                            v1 = S.add("vector", lambda v, sl=sl, sc_sb=sc_sb, li=li:
                                       v.scalar_tensor_tensor(
                                           sq[:, sl], ql[:, sl], sc_sb[:, li:li + 1],
                                           rsb[:, sl], ALU.mult, ALU.mult),
                                       waits=[(P.ss.h, se_), (P.vs.h, P.vs.n)], bump=P.vs)
                            v2 = S.add("vector", lambda v, sl=sl, nt=nt, tabs=tabs:
                                       v.tensor_tensor(ql[:, sl], sq[:, sl],
                                                       tabs[:, nt * 512:(nt + 1) * 512],
                                                       ALU.mult),
                                       waits=[(P.vs.h, v1)], bump=P.vs)
                            v3 = S.add("vector", lambda v, sl=sl, nt=nt, tabc=tabc:
                                       v.tensor_tensor(sq[:, sl], sq[:, sl],
                                                       tabc[:, nt * 512:(nt + 1) * 512],
                                                       ALU.mult),
                                       waits=[(P.vs.h, v2)], bump=P.vs)
                            def rmm(t, sl=sl, bank=bank):
                                t.matmul(bank[:], rot2[:], ql[:, sl],
                                         start=True, stop=False)
                                return t.matmul(bank[:], ident[:], sq[:, sl],
                                                start=False, stop=True)
                            tr = S.add("tensor", rmm,
                                       waits=[(P.vs.h, v3), gate_ps[nt % 3]], bump=P.ts)
                            so = S.add("scalar", lambda s, nt=nt, bank=bank, dstf=dstf:
                                       s.copy(dstf[:, nt * 512:(nt + 1) * 512], bank[:]),
                                       waits=[(P.ts.h, tr), (P.ss.h, P.ss.n)], bump=P.ss)
                            gate_ps[nt % 3] = (P.ss.h, so)
                    def wvl(g, li=li, hg=hg, par=par):
                        return g.dma_start(
                            wb[par][:].rearrange("p (kc m) -> p kc m", kc=DC),
                            Wqkv[li * D:(li + 1) * D,
                                 2 * INNER + hg * 128: 2 * INNER + hg * 128 + 128]
                            .rearrange("(kc p) m -> p kc m", p=128))
                    dv2 = S.add("gpsimd", wvl, waits=[(P.ts.h, P.ts.n)], bump=P.dw, k=16)
                    for tt in range(16):
                        bank = tt % 2
                        def vmm(t, tt=tt, par=par, bank=bank):
                            for kc in range(DC):
                                ins = t.matmul(
                                    [ps1, ps2][bank][:, 0:128],
                                    xn_bf[:, kc * T + tt * 128: kc * T + tt * 128 + 128],
                                    wb[par][:, kc * 128:(kc + 1) * 128],
                                    start=(kc == 0), stop=(kc == DC - 1))
                            return ins
                        tv = S.add("tensor", vmm,
                                   waits=[xs_rdy, (P.dw.h, dv2), gate_ps[bank]],
                                   bump=P.ts)
                        def vev(v, tt=tt, bank=bank):
                            v.tensor_copy(v_tm[:, tt * 130: tt * 130 + 64],
                                          [ps1, ps2][bank][0:64, 0:128][:, 0:64]
                                          if False else [ps1, ps2][bank][:, 0:64])
                            return v.tensor_copy(v_tm[:, tt * 130 + 65: tt * 130 + 129],
                                                 [ps1, ps2][bank][:, 64:128])
                        vv = S.add("vector", vev,
                                   waits=[(P.ts.h, tv), (P.vs.h, vset), (P.vs.h, P.vs.n)],
                                   bump=P.vs)
                        gate_ps[bank] = (P.vs.h, vv)
                    for head in range(2):
                        hb = head * 64
                        pdone = [None] * 4
                        pexp = [None] * 4
                        blk_done = {}
                        def emit_pv(b, head=head):
                            ob = b % 4
                            srcs = [k2 for k2 in range(max(0, b - 2), b + 1)]
                            lastpv = None
                            for si, k2 in enumerate(srcs):
                                r2 = k2 % 4
                                qoff = (b - k2) * 128
                                pv_waits = [pexp[r2]]
                                if si == 0 and gate_out[ob] is not None:
                                    pv_waits.append(gate_out[ob])
                                lastpv = S.add(
                                    "tensor",
                                    lambda t, k2=k2, head=head, ob=ob, r2=r2, qoff=qoff,
                                    si=si, n=len(srcs):
                                    t.matmul(outp[ob][:],
                                             v_tm[:, k2 * 130 + head * 65:
                                                  k2 * 130 + head * 65 + 65],
                                             ptile[:, r2 * 384 + qoff: r2 * 384 + qoff + 128],
                                             start=(si == 0), stop=(si == n - 1)),
                                    waits=pv_waits, bump=P.ts)
                            blk_done[b] = lastpv

                        def emit_norm(g4, head=head):
                            zv = None
                            for j in range(4):
                                zv = S.add("scalar", lambda s, j=j, g4=g4: s.copy(
                                    zrow[0:1, j * 128:(j + 1) * 128],
                                    outp[j][64:65, :]),
                                    waits=[(P.ts.h, blk_done[g4 * 4 + j]),
                                           (P.ss.h, P.ss.n)], bump=P.ss)
                            rv = S.add("vector", lambda v: v.reciprocal(rzv[:], zrow[:]),
                                       waits=[(P.ss.h, zv), (P.vs.h, P.vs.n)], bump=P.vs)
                            bcv = S.add("tensor", lambda t: t.matmul(
                                rzps[:], onesf[0:1, 0:64], rzv[:], start=True, stop=True),
                                waits=[(P.vs.h, rv), gate_rzps], bump=P.ts)
                            rbv = S.add("scalar", lambda s: s.copy(rzb[:], rzps[:]),
                                        waits=[(P.ts.h, bcv), (P.ss.h, P.ss.n)],
                                        bump=P.ss)
                            hh = hg * 2 + head
                            oev = None
                            for j in range(4):
                                oev = S.add("vector", lambda v, hh=hh, g4=g4, j=j:
                                            v.tensor_tensor(
                                                attn[(hh % 2) * 64:(hh % 2) * 64 + 64,
                                                     (hh // 2) * T + g4 * 512 + j * 128:
                                                     (hh // 2) * T + g4 * 512 + j * 128 + 128],
                                                outp[j][0:64, :],
                                                rzb[:, j * 128:(j + 1) * 128], ALU.mult),
                                            waits=[(P.ss.h, rbv), (P.vs.h, P.vs.n)],
                                            bump=P.vs)
                            for j in range(4):
                                gate_out[j] = (P.vs.h, oev)
                            return (P.ss.h, rbv), oev

                        for kt in range(16):
                            sb_ = kt % 2
                            ri = kt % 4
                            qw = min(384, T - kt * 128)
                            tv = S.add("tensor", lambda t, hb=hb, kt=kt, qw=qw, sb_=sb_:
                                       t.matmul([ps1, ps2][sb_][:, 0:qw],
                                                k_fm[hb:hb + 64, kt * 128:(kt + 1) * 128],
                                                q_fm[hb:hb + 64, kt * 128: kt * 128 + qw],
                                                start=True, stop=True),
                                       waits=[gate_ps[sb_]], bump=P.ts)
                            ev = S.add("scalar", lambda s, sb_=sb_, ri=ri, qw=qw:
                                       s.activation(ptile[:, ri * 384: ri * 384 + qw],
                                                    [ps1, ps2][sb_][:, 0:qw], AF.Exp,
                                                    scale=8.0,
                                                    bias=P.consts["c_nb20"][:]),
                                       waits=[(P.ts.h, tv), (P.ss.h, P.ss.n)] +
                                             ([pdone[ri]] if pdone[ri] else []),
                                       bump=P.ss)
                            mv = S.add("vector", lambda v, ri=ri, qw=qw:
                                       v.tensor_tensor(
                                           ptile[:, ri * 384: ri * 384 + qw],
                                           ptile[:, ri * 384: ri * 384 + qw],
                                           wmask[:, 0:qw], ALU.mult),
                                       waits=[(P.ss.h, ev), (P.vs.h, P.vs.n)],
                                       bump=P.vs)
                            gate_ps[sb_] = (P.vs.h, mv)
                            pexp[ri] = (P.vs.h, mv)
                            if kt >= 1:
                                emit_pv(kt - 1)
                                pdone[(kt - 3) % 4] = (P.ts.h, P.ts.n)
                                if (kt - 1) % 4 == 3:
                                    _, oev_all = emit_norm((kt - 1) // 4)
                        emit_pv(15)
                        gr, oev_all = emit_norm(3)
                        gate_rzps = gr
                S.emit(block)
                evo = Evac(P, "resid", dst=x_fm, bias=None, dtokens=T)
                emit_proj(P, block, WoA[li * INNER:(li + 1) * INNER, :], INNER, D,
                          attn, T, evo, rhs_ready=(P.vs.h, oev_all),
                          wtag=f"l1fwo{li}", stack=attn_st, wb=[wdb[0], wdb[1]],
                          banks=[ps1, ps2], w_gate=mem_gate[0])
                pst.close()
                attn_st.close()
                mem_gate = [(P.ts.h, P.ts.n), (P.vs.h, P.vs.n)]

                ffn_st = contextlib.ExitStack()
                a_sb = P.sb(f"l1f_a{li}", [128, 4 * 512], BF16, ffn_st)
                pr = P.sb(f"l1f_pr{li}", [128, 11 * T], BF16, ffn_st)

                S2 = Sched(P)
                for kc in range(DC):
                    def shf2(v, kc=kc):
                        if kc < 4:
                            return v.tensor_copy(xn_bf[:, kc * T:(kc + 1) * T],
                                                 x_fm[:, kc * T:(kc + 1) * T])
                        v.tensor_copy(xn_bf[:, kc * T + 1:(kc + 1) * T],
                                      x_fm[:, kc * T:(kc + 1) * T - 1])
                        return v.memset(xn_bf[:, kc * T: kc * T + 1], 0.0)
                    S2.add("vector", shf2, waits=mem_gate + [(P.vs.h, P.vs.n)], bump=P.vs)
                S2.emit(block)
                lg = emit_ln_fm(P, block, None, xn_bf, eps=1e-5,
                                gamma_col=lng_sb[:, li * DC:(li + 1) * DC],
                                beta_col=lnb_sb[:, li * DC:(li + 1) * DC],
                                x_gate=(P.vs.h, P.vs.n))
                n_rdy = (P.vs.h, lg)
                fst = contextlib.ExitStack()
                fps = [P.ps(f"l1f_fps{i}_{li}", [128, 512], F32, fst) for i in range(4)]
                dps = [P.ps(f"l1f_dps{i}_{li}", [128, 512], F32, fst) for i in range(2)]
                S3 = Sched(P)
                gate_fps = [None, None, None, None]
                gate_dps = [None, None]
                gate_w12 = [None, None]
                gate_pr = None
                gate_wd2 = [None, None]
                for hf, (h0, h1) in enumerate(HHALF):
                    nch = (h1 - h0 + 127) // 128
                    prod_done = None
                    for m in range(nch):
                        mw = min(128, (h1 - h0) - m * 128)
                        par = m % 2
                        mc_glob = (h0 + m * 128) // 128
                        def w12l(g, li=li, h0=h0, m=m, par=par, mw=mw):
                            ins = g.dma_start(
                                wb[par][:, 0:DC * mw]
                                .rearrange("p (kc m) -> p kc m", kc=DC),
                                W1A[li * D:(li + 1) * D, h0 + m * 128: h0 + m * 128 + mw]
                                .rearrange("(kc p) m -> p kc m", p=128))
                            ins.then_inc(P.dw.h, 16)
                            return g.dma_start(
                                wb[2 + par][:, 0:DC * mw]
                                .rearrange("p (kc m) -> p kc m", kc=DC),
                                W1A[li * D:(li + 1) * D,
                                    HID + h0 + m * 128: HID + h0 + m * 128 + mw]
                                .rearrange("(kc p) m -> p kc m", p=128))
                        P.dw.reserve(16)
                        dmv = S3.add("gpsimd", w12l,
                                     waits=[n_rdy, gate_w12[par]], bump=P.dw, k=16)
                        for nt in range(NT):
                            bank = nt % 2
                            gb = 2 + bank
                            def amm(t, par=par, nt=nt, bank=bank, mw=mw):
                                for kc in range(DC):
                                    ins = t.matmul(
                                        fps[bank][0:mw, :],
                                        wb[par][:, kc * mw:(kc + 1) * mw],
                                        xn_bf[:, kc * T + nt * 512: kc * T + nt * 512 + 512],
                                        start=(kc == 0), stop=(kc == DC - 1))
                                return ins
                            t1 = S3.add("tensor", amm,
                                        waits=[n_rdy, (P.dw.h, dmv), gate_fps[bank]],
                                        bump=P.ts)
                            def gmm(t, par=par, nt=nt, gb=gb, mw=mw):
                                for kc in range(DC):
                                    ins = t.matmul(
                                        fps[gb][0:mw, :],
                                        wb[2 + par][:, kc * mw:(kc + 1) * mw],
                                        xn_bf[:, kc * T + nt * 512: kc * T + nt * 512 + 512],
                                        start=(kc == 0), stop=(kc == DC - 1))
                                return ins
                            t2 = S3.add("tensor", gmm, waits=[gate_fps[gb]], bump=P.ts)
                            av = S3.add("vector", lambda v, nt=nt, bank=bank, mw=mw,
                                        li=li, mc_glob=mc_glob:
                                        v.tensor_scalar(
                                            a_sb[0:mw, bank * 512:(bank + 1) * 512],
                                            fps[bank][0:mw, :],
                                            b1a_sb[0:mw, li * 22 + mc_glob:
                                                   li * 22 + mc_glob + 1],
                                            None, ALU.add),
                                        waits=[(P.ts.h, t1), (P.vs.h, P.vs.n)], bump=P.vs)
                            gate_fps[bank] = (P.vs.h, av)
                            gv = S3.add("scalar", lambda s, nt=nt, gb=gb, bank=bank, mw=mw,
                                        li=li, mc_glob=mc_glob:
                                        s.activation(
                                            a_sb[0:mw, (2 + bank) * 512:(3 + bank) * 512],
                                            fps[gb][0:mw, :], AF.Gelu,
                                            bias=b1g_sb[0:mw, li * 22 + mc_glob:
                                                        li * 22 + mc_glob + 1]),
                                        waits=[(P.ts.h, t2), (P.ss.h, P.ss.n)], bump=P.ss)
                            prv = S3.add("vector", lambda v, m=m, nt=nt, bank=bank, mw=mw:
                                         v.tensor_tensor(
                                             pr[0:mw, m * T + nt * 512: m * T + nt * 512 + 512],
                                             a_sb[0:mw, bank * 512:(bank + 1) * 512],
                                             a_sb[0:mw, (2 + bank) * 512:(3 + bank) * 512],
                                             ALU.mult),
                                         waits=[(P.ss.h, gv), (P.vs.h, P.vs.n), gate_pr],
                                         bump=P.vs)
                            gate_fps[gb] = (P.vs.h, prv)
                            prod_done = prv
                        gate_w12[par] = (P.ts.h, P.ts.n)
                    for mo in range(DC):
                        par = mo % 2
                        def w2l(g, li=li, h0=h0, h1=h1, mo=mo, par=par):
                            full = (h1 - h0) // 128
                            rem = (h1 - h0) - full * 128
                            ins = g.dma_start(
                                wdb[par][:, 0:full * 128]
                                .rearrange("p (c m) -> p c m", c=full),
                                W2A[li * HID + h0: li * HID + h0 + full * 128,
                                    mo * 128:(mo + 1) * 128]
                                .rearrange("(c p) m -> p c m", p=128))
                            if rem == 0:
                                return ins
                            ins.then_inc(P.dw.h, 16)
                            return g.dma_start(
                                wdb[par][0:rem, full * 128:(full + 1) * 128],
                                W2A[li * HID + h0 + full * 128: li * HID + h1,
                                    mo * 128:(mo + 1) * 128])
                        if (h1 - h0) % 128 != 0:
                            P.dw.reserve(16)
                        dm2 = S3.add("gpsimd", w2l,
                                     waits=[(P.vs.h, prod_done), gate_wd2[par]],
                                     bump=P.dw, k=16)
                        for nt in range(NT):
                            bank = nt % 2
                            def dmm(t, par=par, nt=nt, bank=bank, h0=h0, h1=h1, nch=nch):
                                for c in range(nch):
                                    cw = min(128, (h1 - h0) - c * 128)
                                    ins = t.matmul(
                                        dps[bank][:],
                                        wdb[par][0:cw, c * 128:(c + 1) * 128],
                                        pr[0:cw, c * T + nt * 512: c * T + nt * 512 + 512],
                                        start=(c == 0), stop=(c == nch - 1))
                                return ins
                            t3 = S3.add("tensor", dmm,
                                        waits=[(P.vs.h, prod_done), (P.dw.h, dm2),
                                               gate_dps[bank]],
                                        bump=P.ts)
                            if hf == 0:
                                vv = S3.add("vector", lambda v, mo=mo, nt=nt, bank=bank,
                                            li=li:
                                            v.scalar_tensor_tensor(
                                                x_fm[:, mo * T + nt * 512:
                                                     mo * T + nt * 512 + 512],
                                                dps[bank][:],
                                                b2_sb[:, li * DC + mo: li * DC + mo + 1],
                                                x_fm[:, mo * T + nt * 512:
                                                     mo * T + nt * 512 + 512],
                                                ALU.add, ALU.add),
                                            waits=[(P.ts.h, t3), (P.vs.h, P.vs.n)],
                                            bump=P.vs)
                            else:
                                vv = S3.add("vector", lambda v, mo=mo, nt=nt, bank=bank:
                                            v.tensor_tensor(
                                                x_fm[:, mo * T + nt * 512:
                                                     mo * T + nt * 512 + 512],
                                                x_fm[:, mo * T + nt * 512:
                                                     mo * T + nt * 512 + 512],
                                                dps[bank][:], ALU.add),
                                            waits=[(P.ts.h, t3), (P.vs.h, P.vs.n)],
                                            bump=P.vs)
                            gate_dps[bank] = (P.vs.h, vv)
                        gate_wd2[par] = (P.ts.h, P.ts.n)
                    gate_pr = (P.ts.h, P.ts.n)
                S3.emit(block)
                fst.close()
                ffn_st.close()
                mem_gate = [(P.ts.h, P.ts.n), (P.vs.h, P.vs.n)]

            S4 = Sched(P)
            for kc in range(DC):
                lv = S4.add("sync", lambda sy, kc=kc: sy.dma_start(
                    xadd[:], xscr[:, kc * T:(kc + 1) * T]),
                    waits=mem_gate + [(P.vs.h, P.vs.n)] if kc == 0 else
                    [(P.vs.h, P.vs.n)], bump=P.dx, k=16)
                av = S4.add("vector", lambda v, kc=kc: v.tensor_tensor(
                    x_fm[:, kc * T:(kc + 1) * T], x_fm[:, kc * T:(kc + 1) * T],
                    xadd[:], ALU.add),
                    waits=[(P.dx.h, lv), (P.vs.h, P.vs.n)], bump=P.vs)
            S4.emit(block)
    return


def emit_L0_ffn(P, x_fm, xn_bf, x_gate):
    WT = P.din("L0f_WT", [D, D])
    Vr = P.din("L0f_Vr", [D, 32])
    CT = P.din("L0f_CT", [32, D])
    bb = P.din("L0f_b", [128, DC])

    with contextlib.ExitStack() as st:
        with P.nc.Block() as block:
            vr_sb = P.sb("l0f_vr", [128, DC * 32], BF16, st)
            ct_sb = P.sb("l0f_ct", [32, D], BF16, st)
            b_sb = P.sb("l0f_b", [128, DC], F32, st)
            t_bf = P.sb("l0f_t", [32, T], BF16, st)
            wbufs = [P.sb(f"l0fw{i}", [128, DC * 128], BF16, st) for i in range(2)]

            xn_gate = emit_ln_fm(P, block, x_fm, xn_bf, x_gate=x_gate)
            ps_t = P.ps("l0f_pst", [32, 512], F32, st)

            @block.gpsimd
            def _(g):
                P.dw.bump(g.dma_start(
                    vr_sb[:].rearrange("p (kc r) -> p kc r", kc=DC),
                    Vr.rearrange("(kc p) r -> p kc r", p=128)), 16)
                P.dw.bump(g.dma_start(ct_sb[:], CT), 16)
                P.dw.bump(g.dma_start(b_sb[:], bb), 16)
            small_loaded = P.dw.n

            tt0 = P.ts.n
            st0 = P.ss.n

            @block.tensor
            def _(t):
                t.wait_ge(P.dw.h, small_loaded)
                t.wait_ge(P.vs.h, xn_gate)
                for nt in range(NT):
                    if nt >= 1:
                        t.wait_ge(P.ss.h, st0 + nt)
                    for k in range(DC):
                        ins = t.matmul(ps_t[:],
                                       vr_sb[:, k * 32:(k + 1) * 32],
                                       xn_bf[:, k * T + nt * 512: k * T + nt * 512 + 512],
                                       start=(k == 0), stop=(k == DC - 1))
                    P.ts.bump(ins, 1)

            @block.scalar
            def _(s):
                for nt in range(NT):
                    s.wait_ge(P.ts.h, tt0 + nt + 1)
                    ins = s.copy(t_bf[:, nt * 512:(nt + 1) * 512], ps_t[:])
                    P.ss.bump(ins, 1)
            t_ready = P.ss.n

            def extra(t, m, nt, ps_ap):
                return t.matmul(ps_ap, ct_sb[:, m * 128:(m + 1) * 128],
                                t_bf[:, nt * 512:(nt + 1) * 512], start=False, stop=True)

            P.tap("tbf", t_bf[:])
            ev = Evac(P, "resid", dst=x_fm, bias=b_sb, dtokens=T)
            emit_proj(P, block, WT, D, D, xn_bf, T, ev, extra_mm=extra,
                      rhs_ready=(P.vs.h, xn_gate), wait_extra=(P.ss.h, t_ready),
                      wtag="l0fw", stack=st, wb=wbufs)
    return



def emit_dbg_ln(P, x_fm, xn_bf, x_gate):
    with P.nc.Block() as block:
        emit_ln_fm(P, block, x_fm, xn_bf, x_gate=x_gate)
    P.tap("xn", xn_bf[:])
    return


PHASE_EMITTERS = {"L0_ffn": emit_L0_ffn, "dbg_ln": emit_dbg_ln, "L3_attn": emit_L3_attn,
                  "L0_attn": emit_L0_attn, "L2_attn": emit_L2_attn,
                  "L3_ffn": emit_L3_ffn, "L2_ffn": emit_L2_ffn,
                  "L1_attn": emit_L1_attn, "L1_ffn": emit_L1_ffn}


def build_program(phases):
    nc = bass.Bass("TRN2", target_bir_lowering=False, debug=False)
    with contextlib.ExitStack() as st:
        P = Prog(nc, st)
        XIN = nc.dram_tensor("XIN", [D, T], F32, kind="ExternalInput").ap()
        XOUT = nc.dram_tensor("XOUT", [D, T], F32, kind="ExternalOutput").ap()
        P.input_names.append("XIN")
        x_fm = P.sb("x_fm", [128, DC * T], F32)
        xn_bf = P.sb("xn_bf", [128, DC * T], BF16)

        needed = set(CONST_BASE)
        for ph in phases:
            needed |= CONST_NEED.get(ph, set())
        with nc.Block() as block:
            const_aps = []
            for name, shape, dt in CONST_SPECS:
                if name not in needed:
                    continue
                P.consts[name] = P.sb(name + "_sb", shape, dt)
                const_aps.append((name, P.din(name, shape)))

            @block.gpsimd
            def _(g):
                for name, ap in const_aps:
                    P.dw.bump(g.dma_start(P.consts[name][:], ap), 16)
            if "c_eps6" in P.consts:
                nc.const_aps.register(F32, 1e-06, P.consts["c_eps6"][:])

            @block.sync
            def _(sy):
                P.dx.bump(sy.dma_start(
                    x_fm[:].rearrange("p (kc t) -> p kc t", kc=DC),
                    XIN.rearrange("(kc p) t -> p kc t", p=128)), 16)

            @block.gpsimd
            def _(g):
                g.wait_ge(P.dw.h, P.dw.n)
        x_gate = (P.dx.h, P.dx.n)

        for ph in phases:
            PHASE_EMITTERS[ph](P, x_fm, xn_bf, x_gate)
            x_gate = None

        with nc.Block() as block:
            tap_aps = []
            for i, (name, tile_ap) in enumerate(P.taps):
                shp = list(tile_ap.shape)
                tap_aps.append(nc.dram_tensor(f"DBG_{name}", shp, F32,
                                              kind="ExternalOutput").ap())

            @block.gpsimd
            def _(g):
                P.dx.bump(g.dma_start(
                    XOUT.rearrange("(kc p) t -> p kc t", p=128),
                    x_fm[:].rearrange("p (kc t) -> p kc t", kc=DC)), 16)
                for (name, tile_ap), dap in zip(P.taps, tap_aps):
                    P.dx.bump(g.dma_start(dap, tile_ap), 16)
                g.wait_ge(P.dx.h, P.dx.n)
    return nc



def _colsplit(v):
    v = np.asarray(v, dtype=np.float32)
    n = v.shape[-1]
    return np.ascontiguousarray(v.reshape(n // 128, 128).T)


def prep_weights(inp):
    f32 = np.float32
    w = {}
    w["L0f_WT"] = np.ascontiguousarray(np.asarray(inp["L0_ffn_W"], f32).T)
    w["L0f_Vr"] = np.ascontiguousarray(np.asarray(inp["L0_ffn_V_r"], f32))
    w["L0f_CT"] = np.ascontiguousarray(np.asarray(inp["L0_ffn_C"], f32).T)
    w["L0f_b"] = _colsplit(inp["L0_ffn_b"])
    w["L3a_Wqkv"] = np.ascontiguousarray(np.asarray(inp["L3_attn_W_qkv"], f32))
    w["L3a_Wo"] = np.ascontiguousarray(np.asarray(inp["L3_attn_W_proj"], f32))
    bqkv = np.asarray(inp["L3_attn_b_qkv"], f32)
    w["L3a_bqk"] = np.concatenate([_colsplit(bqkv[0:D]), _colsplit(bqkv[D:2 * D])], axis=1)
    w["L3a_bv"] = np.ascontiguousarray(bqkv[2 * D:3 * D].reshape(1, D))
    w["L3a_bo"] = _colsplit(inp["L3_attn_b_proj"])
    w["L3a_g"] = _colsplit(inp["L3_attn_ln_gamma"])
    w["L3a_b"] = _colsplit(inp["L3_attn_ln_beta"])
    Wqkv = np.asarray(inp["L0_attn_Wqkv"], f32)
    bqkv0 = np.asarray(inp["L0_attn_bqkv"], f32)
    qidx = np.array([c16 * 192 + d for c16 in range(16) for d in range(64)])
    kidx = qidx + 64
    vidx = qidx + 128
    w["L0a_Wq"] = np.ascontiguousarray(Wqkv[qidx, :].T)
    w["L0a_Wk"] = np.ascontiguousarray(Wqkv[kidx, :].T)
    w["L0a_Wv"] = np.ascontiguousarray(Wqkv[vidx, :].T)
    w["L0a_WoT"] = np.ascontiguousarray(np.asarray(inp["L0_attn_Wo"], f32).T)
    w["L0a_qb"] = _colsplit(bqkv0[qidx])
    w["L0a_kb"] = _colsplit(bqkv0[kidx])
    w["L0a_bv"] = np.ascontiguousarray(bqkv0[vidx].reshape(1, D))
    w["L0a_bo"] = _colsplit(inp["L0_attn_bo"])
    w["L2a_wq"] = np.ascontiguousarray(np.asarray(inp["L2_attn_wq"], f32))
    w["L2a_wkd"] = np.ascontiguousarray(np.asarray(inp["L2_attn_wk_down"], f32))
    w["L2a_wvd"] = np.ascontiguousarray(np.asarray(inp["L2_attn_wv_down"], f32))
    w["L2a_wku"] = np.ascontiguousarray(np.asarray(inp["L2_attn_wk_up"], f32))
    w["L2a_wvu"] = np.ascontiguousarray(np.asarray(inp["L2_attn_wv_up"], f32))
    w["L2a_wo"] = np.ascontiguousarray(np.asarray(inp["L2_attn_wo"], f32))
    w["L3f_WfcT"] = np.ascontiguousarray(np.asarray(inp["L3_ffn_W_fc"], f32).T)
    w["L3f_WprT"] = np.ascontiguousarray(np.asarray(inp["L3_ffn_W_proj"], f32).T)
    w["L3f_lnw"] = _colsplit(inp["L3_ffn_ln_w"])
    w["L3f_lnb"] = _colsplit(inp["L3_ffn_ln_b"])
    w["L2f_WgT"] = np.ascontiguousarray(np.asarray(inp["L2_ffn_Wg"], f32).T)
    w["L2f_rb"] = np.ascontiguousarray(np.asarray(inp["L2_ffn_routing_bias"], f32)
                                       .reshape(1, 8))
    W1 = np.asarray(inp["L2_ffn_W1"], f32)
    W2 = np.asarray(inp["L2_ffn_W2"], f32)
    W3 = np.asarray(inp["L2_ffn_W3"], f32)
    w1t = np.concatenate([W1.transpose(0, 2, 1).reshape(8 * 1024, 2730),
                          np.asarray(inp["L2_ffn_Ws1"], f32).T], axis=0)
    w2t = np.concatenate([W2.transpose(0, 2, 1).reshape(8 * 1024, 2730),
                          np.asarray(inp["L2_ffn_Ws2"], f32).T], axis=0)
    w3t = np.concatenate([W3.transpose(0, 2, 1).reshape(8 * 2730, 1024),
                          np.asarray(inp["L2_ffn_Ws3"], f32).T], axis=0)
    w["L2f_W1T"] = np.ascontiguousarray(w1t)
    w["L2f_W2T"] = np.ascontiguousarray(w2t)
    w["L2f_W3T"] = np.ascontiguousarray(w3t)
    w["L1a_Wq"] = np.ascontiguousarray(np.asarray(inp["L1_attn_Wq"], f32))
    w["L1a_Wk"] = np.ascontiguousarray(np.asarray(inp["L1_attn_Wk"], f32))
    w["L1a_Wv"] = np.ascontiguousarray(np.asarray(inp["L1_attn_Wv"], f32))
    w["L1a_Wg"] = np.ascontiguousarray(np.asarray(inp["L1_attn_Wg"], f32))
    w["L1a_Wgk1"] = np.ascontiguousarray(np.asarray(inp["L1_attn_Wgk1"], f32))
    w["L1a_Wgk2"] = np.ascontiguousarray(np.asarray(inp["L1_attn_Wgk2"], f32))
    w["L1a_bgk2"] = np.ascontiguousarray(np.asarray(inp["L1_attn_bgk2"], f32).reshape(1, 512))
    w["L1a_anw"] = _colsplit(inp["L1_attn_attn_norm_w"])
    w["L1a_gnw"] = np.ascontiguousarray(np.asarray(inp["L1_attn_gnorm_w"], f32).reshape(1, 256))
    w["L1a_Wo"] = np.ascontiguousarray(np.asarray(inp["L1_attn_Wo"], f32))
    w["L1a_mnw"] = _colsplit(inp["L1_attn_mlp_norm_w"])
    w["L1a_Wgate"] = np.ascontiguousarray(np.asarray(inp["L1_attn_W_gate"], f32))
    w["L1a_Wup"] = np.ascontiguousarray(np.asarray(inp["L1_attn_W_up"], f32))
    w["L1a_Wdn"] = np.ascontiguousarray(np.asarray(inp["L1_attn_W_down"], f32))
    w["L1f_Wqkv"] = np.ascontiguousarray(
        np.asarray(inp["L1_ffn_Wqkv"], f32).reshape(6 * D, 3 * 1024))
    w["L1f_Wo"] = np.ascontiguousarray(
        np.asarray(inp["L1_ffn_Wo"], f32).reshape(6 * 1024, D))
    w["L1f_W1"] = np.ascontiguousarray(
        np.asarray(inp["L1_ffn_W1"], f32).reshape(6 * D, 5460))
    w["L1f_W2"] = np.ascontiguousarray(
        np.asarray(inp["L1_ffn_W2"], f32).reshape(6 * 2730, D))
    w["L1f_qs"] = np.ascontiguousarray(
        np.tile(np.asarray(inp["L1_ffn_q_scale"], f32).T, (2, 1)))
    w["L1f_ks"] = np.ascontiguousarray(
        np.tile(np.asarray(inp["L1_ffn_k_scale"], f32).T, (2, 1)))
    w["L1f_lng"] = np.concatenate(
        [_colsplit(np.asarray(inp["L1_ffn_ln_g"], f32)[i]) for i in range(6)], axis=1)
    w["L1f_lnb"] = np.concatenate(
        [_colsplit(np.asarray(inp["L1_ffn_ln_b"], f32)[i]) for i in range(6)], axis=1)
    b1 = np.asarray(inp["L1_ffn_b1"], f32)
    def padsplit(v):
        vp = np.zeros(22 * 128, dtype=f32)
        vp[:2730] = v
        return np.ascontiguousarray(vp.reshape(22, 128).T)
    w["L1f_b1a"] = np.concatenate([padsplit(b1[i, :2730]) for i in range(6)], axis=1)
    w["L1f_b1g"] = np.concatenate([padsplit(b1[i, 2730:]) for i in range(6)], axis=1)
    w["L1f_b2"] = np.concatenate(
        [_colsplit(np.asarray(inp["L1_ffn_b2"], f32)[i]) for i in range(6)], axis=1)
    return w


ALL_PHASES = ["L0_attn", "L0_ffn", "L1_attn", "L1_ffn",
              "L2_attn", "L2_ffn", "L3_attn", "L3_ffn"]


def run_phases(x, weights, consts, phases):
    nc = build_program(phases)
    in_maps = []
    for b in range(BATCH):
        m = dict(XIN=np.ascontiguousarray(x[b].T))
        m.update(consts)
        m.update(weights)
        in_maps.append(m)
    res = run_bass_kernel_spmd(nc, in_maps, core_ids=list(range(BATCH)))
    return np.stack([np.asarray(r["XOUT"]).T for r in res.results], axis=0)


def kernel(**inputs):
    x = np.asarray(inputs["L0_attn_x"], dtype=np.float32)
    consts = host_constants()
    weights = prep_weights(inputs)
    x = run_phases(x, weights, consts, ALL_PHASES)
    return x.astype(np.float32)
